# version 04
# baseline (speedup 1.0000x reference)
import os
import numpy as np
from contextlib import ExitStack

import concourse.bass as bass
import concourse.mybir as mybir
from concourse.bass_utils import run_bass_kernel_spmd
from concourse.bass import IndirectOffsetOnAxis

F32 = mybir.dt.float32
BF16 = mybir.dt.bfloat16
I32 = mybir.dt.int32
AF = mybir.ActivationFunctionType
ALU = mybir.AluOpType
AX = mybir.AxisListType

B, S, E = 8, 4096, 1024
NT = S // 128
NC = E // 128
CAP = 640
REC = 1028
SQ8 = float(1.0 / np.sqrt(8.0))
S32 = float(1.0 / 32.0)
S8 = float(1.0 / 8.0)
GELU_C = 0.7978845608028654
GELU_A = 0.044715


class Sem:
    def __init__(self, h):
        self.h = h
        self.n = 0

    def inc(self, k=1):
        self.n += k
        return self.n


class Prog:

    ENGS = ("te", "ve", "se", "sy", "gp")

    def __init__(self, nc, stack):
        self.nc = nc
        self.stack = stack
        self.q = {e: [] for e in self.ENGS}
        self.sems = {}
        for name in ("st", "sv", "ss", "sg", "qsy", "qgp", "qse"):
            h = stack.enter_context(nc.semaphore(name))
            self.sems[name] = Sem(h)
        self._dsems = {}
        self._dsem_pool = []
        self._dsem_n = 0

    def dsem(self, name):
        if name not in self._dsems:
            if self._dsem_pool:
                self._dsems[name] = self._dsem_pool.pop()
            else:
                h = self.stack.enter_context(self.nc.semaphore(f"d{self._dsem_n}"))
                self._dsem_n += 1
                self._dsems[name] = Sem(h)
        return self._dsems[name]

    def release_dsems(self, names):
        for n in list(names):
            s = self._dsems.pop(n, None)
            if s is not None:
                self._dsem_pool.append(s)

    def sb(self, name, shape, dtype):
        return self.stack.enter_context(self.nc.sbuf_tensor(name, shape, dtype))

    def ps(self, name, shape, dtype=F32):
        return self.stack.enter_context(self.nc.psum_tensor(name, shape, dtype))

    def _eq(self, eng):
        return self.q[eng]

    def wait(self, eng, sem, v):
        if v <= 0:
            return
        h = sem.h if isinstance(sem, Sem) else self.sems[sem].h
        self.q[eng].append(lambda e, s=h, v=v: e.wait_ge(s, v))

    def waitall(self, eng, pairs):
        for sem, v in pairs:
            self.wait(eng, sem, v)

    def dma(self, eng, out, in_, sem=None, **kw):
        s_ = sem if sem is not None else self.sems[{"sy": "qsy", "gp": "qgp", "se": "qse"}[eng]]
        v = s_.inc(16)
        self.q[eng].append(
            lambda e, o=out, i=in_, s=s_.h, kw=dict(kw): e.dma_start(out=o, in_=i, **kw).then_inc(s, 16))
        return v

    def idma(self, out, out_off, in_, in_off, sem=None, **kw):
        s_ = sem if sem is not None else self.sems["qgp"]
        v = s_.inc(16)
        def _emit(e, o=out, oo=out_off, i=in_, io=in_off, s=s_.h, kw=dict(kw)):
            try:
                return e.indirect_dma_start(out=o, out_offset=oo, in_=i, in_offset=io, **kw).then_inc(s, 16)
            except Exception:
                import sys
                print(f"IDMA FAIL: out={o} out_off={oo.ap if oo else None} in_={i} "
                      f"in_off={io.ap if io else None} kw={kw}", file=sys.stderr)
                raise
        self.q["gp"].append(_emit)
        return v

    def op(self, eng, fn, inc=True):
        sem = {"te": "st", "ve": "sv", "se": "ss", "gp": "sg"}[eng]
        if inc:
            v = self.sems[sem].inc(1)
            self.q[eng].append(lambda e, f=fn, s=self.sems[sem].h: f(e).then_inc(s, 1))
            return v
        else:
            self.q[eng].append(lambda e, f=fn: f(e))
            return self.sems[sem].n

    def raw(self, eng, fn):
        self.q[eng].append(fn)

    def fence(self, eng):
        sem = {"te": "st", "ve": "sv", "se": "ss", "gp": "sg"}[eng]
        v = self.sems[sem].n
        self.wait(eng, sem, v)

    def barrier(self, tag=""):
        vals = [(s, self.sems[s].n) for s in ("st", "sv", "ss", "sg", "qsy", "qgp", "qse")]
        dvals = [(s, s.n) for s in self._dsems.values()]
        for eng in self.ENGS:
            for sem, v in vals:
                self.wait(eng, sem, v)
            for sem, v in dvals:
                self.wait(eng, sem, v)

    def build(self):
        nc = self.nc
        with nc.Block() as block:
            @block.tensor
            def _(e):
                for f in self.q["te"]:
                    f(e)

            @block.vector
            def _(e):
                for f in self.q["ve"]:
                    f(e)

            @block.scalar
            def _(e):
                for f in self.q["se"]:
                    f(e)

            @block.sync
            def _(e):
                for f in self.q["sy"]:
                    f(e)

            @block.gpsimd
            def _(e):
                for f in self.q["gp"]:
                    f(e)



def host_prep(inputs, b):
    f32 = lambda x: np.ascontiguousarray(x, dtype=np.float32)
    x0 = np.asarray(inputs["L0_attn_emb"], dtype=np.float32)
    m = {}
    m["x0_tok"] = f32(x0[b])
    m["x0T_own"] = f32(x0[b].T)
    xf = np.roll(x0.reshape(B * S, E), -b * S, axis=0)
    m["x0T_full"] = f32(xf.T)
    m["x0_full"] = f32(xf)

    W1, W2, W3, Wo = (np.asarray(inputs[k], np.float32) for k in
                      ("L0_attn_W1", "L0_attn_W2", "L0_attn_W3", "L0_attn_Wo"))
    perm = np.empty(128, np.int64)
    perm[:64] = np.arange(64) * 2
    perm[64:] = np.arange(64) * 2 + 1
    full_perm = (np.arange(E) // 128)[:, None] * 128
    full_perm = (full_perm + perm[None, :]).reshape(-1)
    full_perm = np.concatenate([c * 128 + perm for c in range(NC)])
    m["w1t"] = f32(W1.T[:, full_perm])
    m["w2t"] = f32(W2.T[:, full_perm])
    m["w3t"] = f32(W3.T)
    m["wot"] = f32(Wo.T)
    gt = np.asarray(inputs["L0_ffn_gate_w"], np.float32).T
    m["gatet"] = f32(gt.reshape(8, 128, 8).transpose(1, 0, 2).reshape(128, 64))

    inv = 10000.0 ** (-np.arange(0, 128, 2, dtype=np.float64) / 128.0)
    v_of = np.empty(4096, np.int64)
    for c in range(8):
        for s_in in range(512):
            v_of[c * 512 + s_in] = 8 * s_in + c
    ang = inv[:, None] * v_of[None, :].astype(np.float64)
    m["cosr"] = f32(np.cos(ang))
    m["sinr"] = f32(np.sin(ang))

    inv2 = 10000.0 ** (-np.arange(0, 64, 2, dtype=np.float64) / 64.0)
    ang2 = inv2[:, None] * np.arange(4096, dtype=np.float64)[None, :]
    m["cos2"] = f32(np.concatenate([np.cos(ang2), np.cos(ang2)], 0))
    m["sin2"] = f32(np.concatenate([np.sin(ang2), np.sin(ang2)], 0))

    m["lnw"] = f32(np.asarray(inputs["L0_ffn_ln_w"]).reshape(1, 1024))
    m["lnb"] = f32(np.asarray(inputs["L0_ffn_ln_b"]).reshape(1, 1024))
    m["fc1sT"] = f32(np.transpose(np.asarray(inputs["L0_ffn_fc1s"]), (0, 2, 1)))
    m["b1s"] = f32(np.asarray(inputs["L0_ffn_b1s"]).reshape(8, 32, 128).transpose(0, 2, 1))
    m["fc2sT"] = f32(np.transpose(np.asarray(inputs["L0_ffn_fc2s"]), (0, 2, 1)))
    m["b2s"] = f32(np.asarray(inputs["L0_ffn_b2s"]).reshape(8, 8, 128).transpose(0, 2, 1))

    for k in ("L1_attn_F", "L1_attn_Q", "L1_attn_K", "L1_attn_V"):
        m[k.lower()] = f32(np.asarray(inputs[k]))

    perm1 = np.concatenate([np.arange(0, 32), np.arange(64, 96),
                            np.arange(32, 64), np.arange(96, 128)])
    wq_t = np.asarray(inputs["L1_ffn_wq"], np.float32).T
    wk_t = np.asarray(inputs["L1_ffn_wk"], np.float32).T
    pq8 = np.concatenate([c * 128 + perm1 for c in range(8)])
    pq2 = np.concatenate([c * 128 + perm1 for c in range(2)])
    m["wqt"] = f32(wq_t[:, pq8])
    m["wkt"] = f32(wk_t[:, pq2])
    m["wvt"] = f32(np.asarray(inputs["L1_ffn_wv"], np.float32).T)
    m["wot1"] = f32(np.asarray(inputs["L1_ffn_wo"], np.float32).T)
    m["qgain"] = f32(np.asarray(inputs["L1_ffn_q_gain"]).reshape(1, 16))
    m["kgain"] = f32(np.asarray(inputs["L1_ffn_k_gain"]).reshape(1, 4))

    def mix_fold(mix, n_in):
        out = np.empty_like(mix)
        in_f = mix.shape[2]
        j = np.arange(in_f)
        for k in range(8):
            shift = k % n_in
            pi_inv = (j % 64) * n_in + ((j // 64 + shift) % n_in)
            out[k] = mix[k][:, pi_inv]
        return out
    fc_mix = np.asarray(inputs["L1_ffn_fc_mix"], np.float32)
    pj_mix = np.asarray(inputs["L1_ffn_pj_mix"], np.float32)
    m["fcmixT"] = f32(np.transpose(mix_fold(fc_mix, 16), (0, 2, 1)))
    m["pjmixT"] = f32(np.transpose(mix_fold(pj_mix, 64), (0, 2, 1)))
    m["fcs1"] = f32(np.asarray(inputs["L1_ffn_fc_s1"]).reshape(8, 1024, 64))
    m["fcs2"] = f32(np.asarray(inputs["L1_ffn_fc_s2"]).reshape(8, 4096, 64))
    m["pjs1"] = f32(np.asarray(inputs["L1_ffn_pj_s1"]).reshape(8, 4096, 64))
    m["pjs2"] = f32(np.asarray(inputs["L1_ffn_pj_s2"]).reshape(8, 1024, 64))
    m["fclog"] = f32(np.asarray(inputs["L1_ffn_fc_logits"]).reshape(1, 8))
    m["pjlog"] = f32(np.asarray(inputs["L1_ffn_pj_logits"]).reshape(1, 8))

    m["ident"] = f32(np.eye(128))
    m["identb"] = np.eye(128).astype(np.float32)
    m["lt128"] = f32(np.tril(np.ones((128, 128)), -1))
    m["ones128"] = f32(np.ones((128, 1)))
    m["onesrow"] = f32(np.ones((1, 128)))
    m["iota128"] = np.arange(128, dtype=np.int32).reshape(128, 1)
    m["iotaf"] = f32(np.arange(128).reshape(128, 1))
    m["iota8row"] = f32(np.arange(8).reshape(1, 8))
    r_ = np.arange(128)
    hs = np.zeros((128, 2), np.float32)
    hs[((r_ % 64) < 32), 0] = 1.0
    hs[((r_ % 64) >= 32), 1] = 1.0
    m["hsel"] = hs
    m["ind128x8"] = f32((np.arange(128)[:, None] // 16 == np.arange(8)[None, :]))
    m["repl8x128"] = f32((np.arange(8)[:, None] == np.arange(128)[None, :] // 16))
    tm = np.zeros((4, 128, 512), np.float32)
    for s_ in range(4):
        tt = np.arange(128)[:, None] + 128 * s_
        qq = np.arange(512)[None, :]
        tm[s_] = np.where(tt <= qq, 0.0, -1e30)
    m["trimask4"] = f32(tm.transpose(1, 0, 2).reshape(128, 4 * 512))
    m["binmask4"] = f32((tm.transpose(1, 0, 2).reshape(128, 4 * 512) == 0.0).astype(np.float32))
    return m



def build_core(nc, b, upto=99, dbg=(), start=0):
    st = ExitStack()
    P = Prog(nc, st)
    g = {}

    def din(name, shape, dtype=F32):
        g[name] = nc.dram_tensor(name, list(shape), dtype, kind="ExternalInput").ap()
        return g[name]

    def dint(name, shape, dtype=F32):
        g[name] = nc.dram_tensor(name, list(shape), dtype, kind="Internal").ap()
        return g[name]

    def dout(name, shape, dtype=F32):
        g[name] = nc.dram_tensor(name, list(shape), dtype, kind="ExternalOutput").ap()
        return g[name]

    din("x0_tok", (S, E)); din("x0T_own", (E, S)); din("x0T_full", (E, B * S))
    din("x0_full", (B * S, E))
    din("w1t", (E, E)); din("w2t", (E, E)); din("w3t", (E, E)); din("wot", (E, E))
    din("gatet", (128, 64))
    din("cosr", (64, S)); din("sinr", (64, S)); din("cos2", (64, S)); din("sin2", (64, S))
    din("hsel", (128, 2))
    din("lnw", (1, E)); din("lnb", (1, E))
    din("fc1sT", (8, E, 4 * E)); din("b1s", (8, 128, 32))
    din("fc2sT", (8, 4 * E, E)); din("b2s", (8, 128, 8))
    din("l1_attn_f", (E, E)); din("l1_attn_q", (E, E)); din("l1_attn_k", (E, E)); din("l1_attn_v", (E, E))
    din("wqt", (E, E)); din("wkt", (E, 256)); din("wvt", (E, 256)); din("wot1", (E, E))
    din("qgain", (1, 16)); din("kgain", (1, 4))
    din("fcmixT", (8, E, 4 * E)); din("pjmixT", (8, 4 * E, E))
    din("fcs1", (8, E, 64)); din("fcs2", (8, 4 * E, 64))
    din("pjs1", (8, 4 * E, 64)); din("pjs2", (8, E, 64))
    din("fclog", (1, 8)); din("pjlog", (1, 8))
    din("ident", (128, 128)); din("identb", (128, 128)); din("lt128", (128, 128))
    din("ones128", (128, 1)); din("onesrow", (1, 128))
    din("iota128", (128, 1), I32); din("iotaf", (128, 1)); din("iota8row", (1, 8))
    din("ind128x8", (128, 8)); din("repl8x128", (8, 128))
    din("trimask4", (128, 2048)); din("binmask4", (128, 2048))

    dint("probs_d", (8, B * S))
    dint("stats_d", (2, B * S))
    dint("own_sd", (2, S))
    dint("scr8", (1, 8))
    dint("meta_d", (8 * CAP + 128, 1))
    dint("vl0_d", (S, E), BF16)
    dint("qv_d", (128, 8 * S), BF16)
    dint("kv_d", (128, 8 * S), BF16)
    dint("o2_d", (S, E), BF16)
    if start >= 8:
        din("x1_d", (S, E))
    elif upto < 8:
        dout("x1_d", (S, E))
    else:
        dint("x1_d", (S, E))
    dint("sd1_d", (3, S))
    dint("xf_d", (E, S), BF16)
    dint("qx_d", (E, S), BF16)
    dint("kx_d", (E, S), BF16)
    dint("vx_d", (S, E), BF16)
    dint("y_d", (S, E))
    dint("a1_d", (S, E))
    dint("h2_d", (4 * E, S), BF16)
    dint("q1_d", (E, S), BF16)
    dint("k1_d", (256, S), BF16)
    dint("v1_d", (S, 256), BF16)
    dint("s1sel", (2, E, 64)); dint("s2sel", (2, 4 * E, 64)); dint("mixsel", (2, E, 4 * E))
    dint("pj1sel", (2, 4 * E, 64)); dint("pj2sel", (2, E, 64)); dint("pjmixsel", (2, 4 * E, E))
    dint("fc_d", (32, 128, S)); dint("pj0_d", (8, 128, S))
    out = dout("out", (S, E))

    dbg_outs = []

    def dump(name, ap_or_tile, shape, dtype=F32, src_sbuf=None, waits=()):
        if name not in dbg:
            return
        o = dout("dbg_" + name, shape, dtype)
        P.waitall("sy", waits)
        P.dma("sy", o, ap_or_tile)
        dbg_outs.append("dbg_" + name)

    ident = P.sb("c_ident", [128, 128], F32)
    identb = P.sb("c_identb", [128, 128], BF16)
    lt128 = P.sb("c_lt", [128, 128], F32)
    ones128 = P.sb("c_ones", [128, 1], F32)
    ones128b = P.sb("c_onesb", [128, 1], BF16)
    onesrow = P.sb("c_onesrow", [1, 128], F32)
    iota128 = P.sb("c_iota", [128, 1], I32)
    iotaf = P.sb("c_iotaf", [128, 1], F32)
    iota8row = P.sb("c_iota8", [1, 8], F32)
    ind128x8 = P.sb("c_ind", [128, 8], F32)
    repl8x128 = P.sb("c_repl", [8, 128], F32)
    gatet_sb = P.sb("c_gatet", [128, 8 * 8], F32)
    lnw_sb = P.sb("c_lnw", [1, E], F32)
    lnb_sb = P.sb("c_lnb", [1, E], F32)
    hsel_sb = P.sb("c_hsel", [128, 2], F32)
    qgain_sb = P.sb("c_qgain", [1, 16], F32)
    kgain_sb = P.sb("c_kgain", [1, 4], F32)
    fclog_sb = P.sb("c_fclog", [1, 8], F32)
    pjlog_sb = P.sb("c_pjlog", [1, 8], F32)

    CSEM = P.dsem("c")
    ld0 = []
    for t, src in ((ident, g["ident"]), (lt128, g["lt128"]), (ones128, g["ones128"]),
                   (onesrow, g["onesrow"]), (iota128, g["iota128"]), (iotaf, g["iotaf"]),
                   (iota8row, g["iota8row"]), (ind128x8, g["ind128x8"]), (repl8x128, g["repl8x128"]),
                   (lnw_sb, g["lnw"]), (lnb_sb, g["lnb"]), (qgain_sb, g["qgain"]),
                   (hsel_sb, g["hsel"]),
                   (kgain_sb, g["kgain"]), (fclog_sb, g["fclog"]), (pjlog_sb, g["pjlog"])):
        ld0.append(P.dma("sy", t[:], src, sem=CSEM))
    ld0.append(P.dma("gp", identb[:], g["identb"], sem=CSEM))
    ld0.append(P.dma("gp", ones128b[:], g["ones128"], sem=CSEM))
    ld0.append(P.dma("sy", gatet_sb[:], g["gatet"], sem=CSEM))
    for _eng in ("ve", "te", "se", "sy", "gp"):
        P.wait(_eng, CSEM, max(ld0))

    negm_own = P.sb("negm_own", [128, NT], F32)
    rstd_own = P.sb("rstd_own", [128, NT], F32)
    stats1_negm = P.sb("s1_negm", [128, NT], F32)
    stats1_rstd = P.sb("s1_rstd", [128, NT], F32)
    stats1_rms = P.sb("s1_rms", [128, NT], F32)
    statsy_rms = P.sb("sy_rms", [128, NT], F32)

    def emit_stats(src_dram, dst_negm, dst_rstd, dst_rms=None, eps=1e-5, rms_eps=1e-6,
                   pre_wait=()):
        sums = P.sb("st_sums", [128, NT], F32)
        sq = P.sb("st_sq", [128, NT], F32)
        xb = [P.sb(f"st_xb{i}", [128, E], F32) for i in range(3)]
        xsem = [P.dsem(f"stx{i}") for i in range(3)]
        scr = P.sb("st_scr", [128, E], F32)
        vdone = [0] * 3
        sdone = [0] * 3
        P.waitall("sy", pre_wait)
        for f in range(NT):
            i = f % 3
            P.waitall("sy", [("sv", vdone[i]), ("ss", sdone[i])])
            w = P.dma("sy", xb[i][:], src_dram[f * 128:(f + 1) * 128, :], sem=xsem[i])
            P.wait("se", xsem[i], w)
            sdone[i] = P.op("se", lambda e, x=xb[i], c=f: e.activation(
                scr[:], x[:], AF.Square, accum_out=sq[:, c:c + 1]))
            P.wait("ve", xsem[i], w)
            vdone[i] = P.op("ve", lambda e, x=xb[i], c=f: e.reduce_sum(
                sums[:, c:c + 1], x[:], AX.X))
        P.fence("ve")
        P.wait("ve", "ss", P.sems["ss"].n)
        mean = P.sb("st_mean", [128, NT], F32)
        var = P.sb("st_var", [128, NT], F32)
        msq = P.sb("st_msq", [128, NT], F32)
        P.op("ve", lambda e: e.tensor_scalar(mean[:], sums[:], 1.0 / E, None, ALU.mult))
        P.fence("ve")
        P.op("ve", lambda e: e.tensor_tensor(msq[:], mean[:], mean[:], ALU.mult))
        P.fence("ve")
        P.op("ve", lambda e: e.scalar_tensor_tensor(var[:], sq[:], 1.0 / E, msq[:],
                                                    ALU.mult, ALU.subtract))
        P.fence("ve")
        P.op("ve", lambda e: e.tensor_scalar(dst_negm[:], mean[:], -1.0, None, ALU.mult))
        if dst_rms is not None:
            vp = P.sb("st_vp", [128, NT], F32)
            vpa = P.sb("st_vpa", [128, NT], F32)
            vpb = P.sb("st_vpb", [128, NT], F32)
            P.op("ve", lambda e: e.tensor_scalar(vpa[:], var[:], eps, None, ALU.add))
            P.fence("ve")
            P.op("ve", lambda e: e.reciprocal(vpa[:] if False else vpb[:], vpa[:]))
            P.fence("ve")
            P.op("ve", lambda e: e.tensor_tensor(vpb[:], var[:], vpb[:], ALU.mult)) if False else None
            P.op("ve", lambda e: e.tensor_tensor(vpa[:], var[:], vpb[:], ALU.mult))
            P.fence("ve")
            P.op("ve", lambda e: e.tensor_scalar(vp[:], vpa[:], rms_eps, None, ALU.add))
            P.fence("ve")
            P.wait("se", "sv", P.sems["sv"].n)
            vp2 = P.sb("st_vp2", [128, NT], F32)
            sv = P.op("se", lambda e: e.activation(vp2[:], vp[:], AF.Sqrt))
            P.wait("ve", "ss", sv)
            P.op("ve", lambda e: e.reciprocal(dst_rms[:], vp2[:]))
        vare = P.sb("st_vare", [128, NT], F32)
        P.op("ve", lambda e: e.tensor_scalar(vare[:], var[:], eps, None, ALU.add))
        P.fence("ve")
        P.wait("se", "sv", P.sems["sv"].n)
        var2 = P.sb("st_var2", [128, NT], F32)
        sv = P.op("se", lambda e: e.activation(var2[:], vare[:], AF.Sqrt))
        P.wait("ve", "ss", sv)
        P.op("ve", lambda e: e.reciprocal(dst_rstd[:], var2[:]))
        P.fence("ve")

    def _l0():
        with ExitStack() as ph1:
            _sb = P.sb; P.sb = lambda n, s, d, _st=ph1: _st.enter_context(P.nc.sbuf_tensor(n, s, d))
            emit_stats(g["x0_tok"], negm_own, rstd_own)
            P.sb = _sb
        P.barrier("ph1")
        dump("negm_own", negm_own[:], (128, NT))
        dump("rstd_own", rstd_own[:], (128, NT))

        if upto < 2:
            P.build(); return True

        with ExitStack() as ph2:
            sbl = lambda n, s, d: ph2.enter_context(P.nc.sbuf_tensor(n, s, d))
            psl = lambda n, s: ph2.enter_context(P.nc.psum_tensor(n, s, F32))
            smG = sbl("smG", [128, 256], F32)
            sqG = sbl("sqG", [128, 256], F32)
            xst = [sbl(f"xst{i}", [128, E], F32) for i in range(3)]
            gsem = [P.dsem(f"g2x{i}") for i in range(3)]
            scr = sbl("g_scr", [128, E], F32)
            vdone = [0] * 3
            sdone = [0] * 3
            for f in range(256):
                i = f % 3
                P.waitall("sy", [("sv", vdone[i]), ("ss", sdone[i])])
                src = g["x0_full"][f::256, :]
                w = P.dma("sy", xst[i][:], src, sem=gsem[i])
                P.wait("se", gsem[i], w)
                sdone[i] = P.op("se", lambda e, x=xst[i], c=f: e.activation(
                    scr[:], x[:], AF.Square, accum_out=sqG[:, c:c + 1]))
                P.wait("ve", gsem[i], w)
                vdone[i] = P.op("ve", lambda e, x=xst[i], c=f: e.reduce_sum(
                    smG[:, c:c + 1], x[:], AX.X))
            P.fence("ve"); P.wait("ve", "ss", P.sems["ss"].n)
            meanG = sbl("meanG", [128, 256], F32)
            varG = sbl("varG", [128, 256], F32)
            aG = sbl("aG", [128, 256], F32)
            bG = sbl("bG", [128, 256], F32)
            msqG = sbl("msqG", [128, 256], F32)
            varGe = sbl("varGe", [128, 256], F32)
            P.op("ve", lambda e: e.tensor_scalar(meanG[:], smG[:], 1.0 / E, None, ALU.mult))
            P.fence("ve")
            P.op("ve", lambda e: e.tensor_tensor(msqG[:], meanG[:], meanG[:], ALU.mult))
            P.fence("ve")
            P.op("ve", lambda e: e.scalar_tensor_tensor(varG[:], sqG[:], 1.0 / E, msqG[:],
                                                        ALU.mult, ALU.subtract))
            P.fence("ve")
            P.op("ve", lambda e: e.tensor_scalar(varGe[:], varG[:], 1e-5, None, ALU.add))
            P.fence("ve")
            P.wait("se", "sv", P.sems["sv"].n)
            varG2 = sbl("varG2", [128, 256], F32)
            sv = P.op("se", lambda e: e.activation(varG2[:], varGe[:], AF.Sqrt))
            P.wait("ve", "ss", sv)
            P.op("ve", lambda e: e.reciprocal(aG[:], varG2[:]))
            P.fence("ve")
            P.op("ve", lambda e: e.tensor_tensor(bG[:], meanG[:], aG[:], ALU.mult))
            vb = P.fence("ve") or P.sems["sv"].n
            SDSEM = P.dsem("sd")
            P.wait("sy", "sv", P.sems["sv"].n)
            w1 = P.dma("sy", g["stats_d"][0:1, :].rearrange("o (p f) -> (o p) f", p=128), aG[:], sem=SDSEM)
            w2 = P.dma("sy", g["stats_d"][1:2, :].rearrange("o (p f) -> (o p) f", p=128), bG[:], sem=SDSEM)

            psG = psl("psG", [1, 8])
            for k in range(8):
                P.op("te", lambda e, k=k: e.matmul(psG[:], ones128[:], gatet_sb[:, 8 * k:8 * k + 8],
                                                   start=(k == 0), stop=(k == 7)), inc=(k == 7))
            P.fence("te")
            G_sb = sbl("G_sb", [1, 8], F32)
            tv = P.sems["st"].n
            P.wait("ve", "st", tv)
            vv = P.op("ve", lambda e: e.tensor_copy(G_sb[:], psG[:]))
            GBSEM = P.dsem("gb")
            P.wait("sy", "sv", vv)
            wv = P.dma("sy", g["scr8"], G_sb[:], sem=GBSEM)
            Gcol = sbl("Gcol", [8, 1], F32)
            P.wait("sy", GBSEM, wv)
            wv2 = P.dma("sy", Gcol[:], g["scr8"].rearrange("o (n x) -> (o n) x", x=1), sem=GBSEM)

            arow = sbl("arow", [1, 2048], F32)
            brow = sbl("brow", [1, 2048], F32)
            xg = [sbl(f"xg{i}", [128, 8 * 512], F32) for i in range(2)]
            psA = psl("psA", [8, 512]); psB = psl("psB", [8, 512])
            psraw = psl("psraw", [8, 512]); psZ = psl("psZ", [1, 512]); psL = psl("psL", [8, 512])
            A_sb = sbl("A_sb", [8, 512], F32); B_sb = sbl("B_sb", [8, 512], F32)
            l_sb = sbl("l_sb", [8, 512], F32); t2_sb = sbl("t2_sb", [8, 512], F32)
            l2_sb = sbl("l2_sb", [8, 512], F32)
            E_sb = sbl("E_sb", [8, 512], F32); lse_sb = sbl("lse_sb", [1, 512], F32)
            s_sb = sbl("s_sb", [8, 512], F32); pr_sb = sbl("pr_sb", [8, 512], F32)
            P.wait("sy", GBSEM, wv2)
            P.wait("sy", SDSEM, max(w1, w2))
            P.wait("te", GBSEM, wv2)
            ABSEM = P.dsem("ab")
            XGS = [P.dsem("xg0"), P.dsem("xg1")]
            PWSEM = P.dsem("pw")
            prev_pr_w = 0
            xg_done = [0, 0]
            for Q in range(16):
                P.wait("sy", "st", P.sems["st"].n)
                wa = P.dma("sy", arow[:], g["stats_d"][0:1, Q * 2048:(Q + 1) * 2048], sem=ABSEM)
                wb = P.dma("sy", brow[:], g["stats_d"][1:2, Q * 2048:(Q + 1) * 2048], sem=ABSEM)
                for tt in range(4):
                    i = tt % 2
                    off = Q * 2048 + tt * 512
                    P.wait("sy", "st", xg_done[i])
                    wx = 0
                    for k in range(8):
                        wx = P.dma("sy", xg[i][:, 512 * k:512 * (k + 1)],
                                   g["x0T_full"][128 * k:128 * (k + 1), off:off + 512], sem=XGS[i])
                    P.waitall("te", [(ABSEM, max(wa, wb))])
                    P.op("te", lambda e, tt=tt: e.matmul(psA[:], onesrow[0:1, 0:8],
                                                         arow[0:1, tt * 512:(tt + 1) * 512],
                                                         start=True, stop=True), inc=False)
                    P.op("te", lambda e, tt=tt: e.matmul(psB[:], onesrow[0:1, 0:8],
                                                         brow[0:1, tt * 512:(tt + 1) * 512],
                                                         start=True, stop=True), inc=False)
                    P.wait("te", XGS[i], wx)
                    P.wait("te", "sv", prev_pr_w)
                    for k in range(8):
                        P.op("te", lambda e, i=i, k=k: e.matmul(
                            psraw[:], gatet_sb[:, 8 * k:8 * k + 8], xg[i][:, 512 * k:512 * (k + 1)],
                            start=(k == 0), stop=(k == 7)), inc=(k == 7))
                    xg_done[i] = P.sems["st"].n
                    tm = P.sems["st"].n
                    P.wait("ve", "st", tm)
                    P.op("ve", lambda e: e.tensor_copy(A_sb[:], psA[:]), inc=False)
                    P.op("ve", lambda e: e.tensor_copy(B_sb[:], psB[:]))
                    P.fence("ve")
                    P.op("ve", lambda e: e.tensor_tensor(l_sb[:], psraw[:], A_sb[:], ALU.mult), inc=False)
                    P.op("ve", lambda e: e.tensor_scalar(t2_sb[:], B_sb[:], Gcol[:, 0:1], None, ALU.mult))
                    P.fence("ve")
                    P.op("ve", lambda e: e.tensor_tensor(l2_sb[:], l_sb[:], t2_sb[:], ALU.subtract))
                    vv = P.fence("ve") or P.sems["sv"].n
                    P.wait("se", "sv", P.sems["sv"].n)
                    sv1 = P.op("se", lambda e: e.activation(E_sb[:], l2_sb[:], AF.Exp))
                    P.wait("te", "ss", sv1)
                    tz = P.op("te", lambda e: e.matmul(psZ[:], ones128[0:8, 0:1], E_sb[:],
                                                       start=True, stop=True))
                    P.wait("se", "st", tz)
                    sv2 = P.op("se", lambda e: e.activation(lse_sb[:], psZ[:], AF.Ln))
                    P.wait("te", "ss", sv2)
                    tl = P.op("te", lambda e: e.matmul(psL[:], onesrow[0:1, 0:8], lse_sb[:],
                                                       start=True, stop=True))
                    P.wait("ve", "st", tl)
                    P.op("ve", lambda e: e.tensor_tensor(s_sb[:], l2_sb[:], psL[:], ALU.subtract))
                    vv = P.fence("ve") or P.sems["sv"].n
                    P.wait("se", "sv", P.sems["sv"].n)
                    sv3 = P.op("se", lambda e: e.activation(pr_sb[:], s_sb[:], AF.Exp))
                    P.wait("sy", "ss", sv3)
                    prw = P.dma("sy", g["probs_d"][:, off:off + 512], pr_sb[:], sem=PWSEM)
                    P.wait("ve", PWSEM, prw)
                    prev_pr_w = P.sems["sv"].n
        P.barrier("ph2")
        dump("probs", g["probs_d"], (8, B * S))
        dump("statsd", g["stats_d"], (2, B * S))

        if upto < 3:
            P.build(); return True


        dint = None
        with ExitStack() as phx:
            sbl = lambda n, s, d=F32: phx.enter_context(P.nc.sbuf_tensor(n, s, d))
            psl = lambda n, s: phx.enter_context(P.nc.psum_tensor(n, s, F32))
            pT = psl("pT2b", [128, 128])
            stT = sbl("stT2b", [128, 128])
            B2SEM = P.dsem("b2b")
            for row, tile in ((0, negm_own), (1, rstd_own)):
                P.wait("te", "sv", P.sems["sv"].n)
                tv = P.op("te", lambda e, t=tile: e.transpose(pT[0:NT, 0:128], t[:], ident[:]))
                P.waitall("ve", [("st", tv), (B2SEM, B2SEM.n)])
                vv = P.op("ve", lambda e: e.tensor_copy(stT[0:NT, :], pT[0:NT, 0:128]))
                P.wait("sy", "sv", vv)
                P.dma("sy", g["own_sd"][row:row + 1, :].rearrange("o (f p) -> (o f) p", p=128),
                      stT[0:NT, :], sem=B2SEM)
        P.barrier("ph2b")
        dump("ownsd", g["own_sd"], (2, S))

        moe_sb = ExitStack()
        RSSM = P.dsem("rssm")
        MZS = P.dsem("mzs")
        slots_v = 0
        mz_final = 0
        slots_i = None
        ttab = None
        if upto >= 7:
            sblm = lambda n, s, d=F32: moe_sb.enter_context(P.nc.sbuf_tensor(n, s, d))
            slots_i = sblm("p6_slots", [128, 8 * 32], I32)
            ttab = sblm("p6_ttab", [128, 32])
            ztm = sblm("p6_ztm", [128, 41])
            with ExitStack() as ph6a_st:
                sbl = lambda n, s, d=F32: ph6a_st.enter_context(P.nc.sbuf_tensor(n, s, d))
                psla = lambda n, s: ph6a_st.enter_context(P.nc.psum_tensor(n, s, F32))
                zmv = P.op("ve", lambda e: e.memset(ztm[:], 0.0))
                P.wait("se", "sv", zmv)
                mz_final = P.dma("se", g["meta_d"].rearrange("(p f) c -> p (f c)", p=128),
                                 ztm[:], sem=MZS)
                pb = sbl("p6_pb", [128, 2048])
                P6P = P.dsem("p6p")
                wpb = P.dma("sy", pb[:], g["probs_d"].rearrange("e (sub q) -> (e sub) q", sub=16), sem=P6P)
                lo = [sbl("p6_lo0", [8, 1]), sbl("p6_lo1", [8, 1])]
                hi = [sbl("p6_hi0", [8, 1]), sbl("p6_hi1", [8, 1])]
                mid = sbl("p6_mid", [8, 1])
                mid2 = sbl("p6_mid2", [8, 1])
                thr = sbl("p6_thr", [128, 1])
                cmp_ = sbl("p6_cmp", [128, 2048])
                part = sbl("p6_part", [128, 1])
                sel = sbl("p6_sel", [8, 1])
                tq = sbl("p6_tq", [8, 1])
                pbc = psla("p6_pbc", [128, 1])
                pcnt = psla("p6_pcnt", [8, 1])
                P.op("ve", lambda e: e.memset(lo[0][:], 0.0), inc=False)
                P.op("ve", lambda e: e.memset(hi[0][:], 1.0))
                P.fence("ve")
                P.wait("ve", P6P, wpb)
                cur = 0
                for it in range(34):
                    nxt = 1 - cur
                    P.op("ve", lambda e, c=cur: e.tensor_tensor(mid[:], lo[c][:], hi[c][:], ALU.add))
                    P.fence("ve")
                    vv = P.op("ve", lambda e: e.tensor_scalar(mid2[:], mid[:], 0.5, None, ALU.mult))
                    P.wait("te", "sv", vv)
                    tv = P.op("te", lambda e: e.matmul(pbc[:], repl8x128[:], mid2[:], start=True, stop=True))
                    P.wait("ve", "st", tv)
                    P.op("ve", lambda e: e.tensor_copy(thr[:], pbc[:]))
                    P.fence("ve")
                    P.op("ve", lambda e: e.tensor_scalar(cmp_[:], pb[:], thr[:, 0:1], None, ALU.is_gt))
                    P.fence("ve")
                    vv = P.op("ve", lambda e: e.reduce_sum(part[:], cmp_[:], AX.X))
                    P.wait("te", "sv", vv)
                    tv = P.op("te", lambda e: e.matmul(pcnt[:], ind128x8[:], part[:], start=True, stop=True))
                    P.wait("ve", "st", tv)
                    P.op("ve", lambda e: e.tensor_scalar(sel[:], pcnt[:], 4096.0, None, ALU.is_ge))
                    P.fence("ve")
                    P.op("ve", lambda e, c=cur: e.tensor_tensor(tq[:], mid2[:], lo[c][:], ALU.subtract))
                    P.fence("ve")
                    P.op("ve", lambda e: e.tensor_tensor(tq[:], tq[:], sel[:], ALU.mult))
                    P.fence("ve")
                    P.op("ve", lambda e, c=cur, n=nxt: e.tensor_tensor(lo[n][:], lo[c][:], tq[:], ALU.add))
                    P.fence("ve")
                    P.op("ve", lambda e, c=cur: e.tensor_tensor(tq[:], hi[c][:], mid2[:], ALU.subtract))
                    P.fence("ve")
                    P.op("ve", lambda e: e.tensor_tensor(tq[:], tq[:], sel[:], ALU.mult))
                    P.fence("ve")
                    P.op("ve", lambda e, n=nxt: e.tensor_tensor(hi[n][:], mid2[:], tq[:], ALU.add))
                    P.fence("ve")
                    cur = nxt
                vv = P.sems["sv"].n
                P.wait("sy", "sv", vv)
                P6B = P.dsem("p6b")
                wb1 = P.dma("sy", g["scr8"].rearrange("o (n x) -> (o n) x", n=8), lo[cur][:], sem=P6B)
                loT = sbl("p6_loT", [1, 8])
                P.wait("sy", P6B, wb1)
                wb2 = P.dma("sy", loT[:], g["scr8"], sem=P6B)
                P.wait("te", P6B, wb2)
                pbc8 = psla("p6_pbc8", [128, 8])
                tv = P.op("te", lambda e: e.matmul(pbc8[:], onesrow[:], loT[:], start=True, stop=True))
                lo_bc = sbl("p6_lobc", [128, 8])
                P.wait("ve", "st", tv)
                P.op("ve", lambda e: e.tensor_copy(lo_bc[:], pbc8[:]))
                P.fence("ve")
                dump("lo_bc", lo_bc[:], (128, 8), waits=[("sv", P.sems["sv"].n)])

                wq_all = sbl("p6_wq", [128, 8 * 32])
                mask = sbl("p6_mask", [128, 32])
                scn = sbl("p6_scn", [128, 32])
                rexc = sbl("p6_rexc", [128, 32])
                rowtot = sbl("p6_rt", [128, 1])
                pcol = psla("p6_pcol", [128, 1])
                slf = sbl("p6_slf", [128, 32])
                part6 = sbl("p6_part6", [128, 1])
                P6W = P.dsem("p6w")
                for e_ in range(8):
                    P.wait("sy", "sv", P.sems["sv"].n)
                    wqe = P.dma("sy", wq_all[:, 32 * e_:32 * (e_ + 1)],
                                g["probs_d"][e_:e_ + 1, 0:4096].rearrange("o (p f) -> (o p) f", p=128),
                                sem=P6W)
                    P.wait("ve", P6W, wqe)
                    P.op("ve", lambda e, e_=e_: e.tensor_scalar(
                        mask[:], wq_all[:, 32 * e_:32 * (e_ + 1)], lo_bc[:, e_:e_ + 1], None, ALU.is_gt))
                    P.fence("ve")
                    P.op("ve", lambda e: e.tensor_tensor_scan(scn[:], mask[:], mask[:], 0.0, ALU.add, ALU.bypass), inc=False)
                    P.op("ve", lambda e: e.reduce_sum(rowtot[:], mask[:], AX.X))
                    P.fence("ve")
                    P.op("ve", lambda e: e.tensor_tensor(rexc[:], scn[:], mask[:], ALU.subtract), inc=False)
                    vv = P.sems["sv"].n + 1
                    P.op("ve", lambda e: e.tensor_copy(part6[:], rowtot[:]))
                    P.wait("te", "sv", vv)
                    tv = P.op("te", lambda e: e.matmul(pcol[:], lt128[:], part6[:], start=True, stop=True))
                    P.wait("ve", "st", tv)
                    P.op("ve", lambda e: e.tensor_scalar(slf[:], rexc[:], pcol[:, 0:1], None, ALU.add))
                    P.fence("ve")
                    P.op("ve", lambda e, e_=e_: e.tensor_scalar(
                        slf[:], slf[:], 1.0, float(CAP * e_ - 8 * CAP), ALU.mult, ALU.add))
                    P.fence("ve")
                    P.op("ve", lambda e: e.tensor_tensor(slf[:], slf[:], mask[:], ALU.mult))
                    P.fence("ve")
                    P.op("ve", lambda e: e.tensor_scalar(slf[:], slf[:], float(8 * CAP),
                                                         float(8 * CAP + 127), ALU.add, ALU.min))
                    P.fence("ve")
                    P.op("ve", lambda e, e_=e_: e.tensor_copy(slots_i[:, 32 * e_:32 * (e_ + 1)], slf[:]))
                    P.fence("ve")
                dump("slots", slots_i[:], (128, 256), I32, waits=[("sv", P.sems["sv"].n)])
                for f in range(32):
                    P.op("ve", lambda e, f=f: e.tensor_scalar(
                        ttab[:, f:f + 1], iotaf[:], 32.0, float(f + 1), ALU.mult, ALU.add))
                slots_v = P.sems["sv"].n

        xn_ctx = P.nc.sbuf_tensor("xn_fm", [128, NC * S], BF16)
        xn_fm = xn_ctx.__enter__()
        with ExitStack() as ph3:
            sbl = lambda n, s, d=F32: ph3.enter_context(P.nc.sbuf_tensor(n, s, d))
            psl = lambda n, s: ph3.enter_context(P.nc.psum_tensor(n, s, F32))
            nrow = sbl("p3_nrow", [1, 512])
            rrow = sbl("p3_rrow", [1, 512])
            pA = psl("p3_pA", [128, 512])
            pB = psl("p3_pB", [128, 512])
            xc = [sbl(f"p3_xc{i}", [128, 512]) for i in range(2)]
            xcs = [P.dsem("p3x0"), P.dsem("p3x1")]
            RSEM = P.dsem("p3r")
            tmp = sbl("p3_tmp", [128, 512])
            xcd = [0, 0]
            for tt in range(8):
                P.wait("sy", "st", P.sems["st"].n)
                wn = P.dma("sy", nrow[:], g["own_sd"][0:1, tt * 512:(tt + 1) * 512], sem=RSEM)
                wr = P.dma("sy", rrow[:], g["own_sd"][1:2, tt * 512:(tt + 1) * 512], sem=RSEM)
                P.wait("te", RSEM, max(wn, wr))
                P.wait("te", "sv", P.sems["sv"].n)
                P.op("te", lambda e: e.matmul(pA[:], onesrow[:], nrow[:], start=True, stop=True), inc=False)
                tb = P.op("te", lambda e: e.matmul(pB[:], onesrow[:], rrow[:], start=True, stop=True))
                for c in range(NC):
                    i = c % 2
                    P.wait("sy", "sv", xcd[i])
                    wx = P.dma("sy", xc[i][:], g["x0T_own"][128 * c:128 * (c + 1), tt * 512:(tt + 1) * 512],
                               sem=xcs[i])
                    P.waitall("ve", [(xcs[i], wx), ("st", tb)])
                    P.op("ve", lambda e, i=i: e.tensor_tensor(tmp[:], xc[i][:], pA[:], ALU.add))
                    P.fence("ve")
                    xcd[i] = P.op("ve", lambda e, i=i, c=c, tt=tt: e.tensor_tensor(
                        xn_fm[:, c * S + tt * 512: c * S + tt * 512 + 512], tmp[:], pB[:], ALU.mult))
        P.barrier("ph3")
        dump("xnfm", xn_fm[:], (128, NC * S), BF16)

        if upto < 4:
            P.build(); return True

        with ExitStack() as ph4:
            sbl = lambda n, s, d=F32: ph4.enter_context(P.nc.sbuf_tensor(n, s, d))
            psl = lambda n, s: ph4.enter_context(P.nc.psum_tensor(n, s, F32))
            cosr = sbl("p4_cosr", [64, S])
            sinr = sbl("p4_sinr", [64, S])
            P4C = P.dsem("p4c")
            wc = P.dma("sy", cosr[:], g["cosr"], sem=P4C)
            ws_ = P.dma("sy", sinr[:], g["sinr"], sem=P4C)
            P.wait("ve", P4C, max(wc, ws_))
            pq = psl("p4_pq", [128, 2048])
            t1 = sbl("p4_t1", [64, 512]); t2 = sbl("p4_t2", [64, 512])
            t3 = sbl("p4_t3", [64, 512]); t4 = sbl("p4_t4", [64, 512])
            wbuf = [sbl(f"p4_w{i}", [128, 1024], BF16) for i in range(2)]
            wbs = [P.dsem("p4w0"), P.dsem("p4w1")]
            qst = [sbl(f"p4_qst{i}", [128, 512], BF16) for i in range(2)]
            qss = [P.dsem("p4q0"), P.dsem("p4q1")]
            qstd = [0, 0]
            wdone = [0, 0]

            def proj_qk(wt_name, dest_dram):
                for c in range(NC):
                    i = c % 2
                    P.wait("gp", "st", wdone[i])
                    ww = P.dma("gp", wbuf[i][:].rearrange("p (k q) -> p k q", k=8),
                               g[wt_name][:, 128 * c:128 * (c + 1)].rearrange("(k p) q -> p k q", p=128),
                               sem=wbs[i])
                    P.wait("te", wbs[i], ww)
                    for half in range(2):
                        P.wait("te", "sv", P.sems["sv"].n)
                        for k in range(8):
                            for nt in range(4):
                                P.op("te", lambda e, i=i, k=k, half=half, nt=nt: e.matmul(
                                    pq[:, 512 * nt:512 * (nt + 1)],
                                    wbuf[i][:, 128 * k:128 * (k + 1)],
                                    xn_fm[:, k * S + half * 2048 + 512 * nt:
                                          k * S + half * 2048 + 512 * nt + 512],
                                    start=(k == 0), stop=(k == 7)),
                                    inc=(k == 7 and nt == 3))
                        wdone[i] = P.sems["st"].n
                        tmm = P.sems["st"].n
                        P.wait("ve", "st", tmm)
                        for hh in range(4):
                            h = 4 * half + hh
                            csl = cosr[:, 512 * c:512 * (c + 1)]
                            ssl = sinr[:, 512 * c:512 * (c + 1)]
                            pe_ = pq[0:64, 512 * hh:512 * (hh + 1)]
                            po_ = pq[64:128, 512 * hh:512 * (hh + 1)]
                            P.op("ve", lambda e, pe_=pe_, csl=csl: e.tensor_tensor(t1[:], pe_, csl, ALU.mult), inc=False)
                            P.op("ve", lambda e, po_=po_, ssl=ssl: e.tensor_tensor(t2[:], po_, ssl, ALU.mult), inc=False)
                            P.op("ve", lambda e, po_=po_, csl=csl: e.tensor_tensor(t3[:], po_, csl, ALU.mult), inc=False)
                            P.op("ve", lambda e, pe_=pe_, ssl=ssl: e.tensor_tensor(t4[:], pe_, ssl, ALU.mult))
                            P.fence("ve")
                            si = (c * 8 + h) % 2
                            P.wait("ve", qss[si], qstd[si])
                            P.op("ve", lambda e, si=si: e.tensor_tensor(qst[si][0:64, :], t1[:], t2[:], ALU.subtract), inc=False)
                            vv = P.op("ve", lambda e, si=si: e.tensor_tensor(qst[si][64:128, :], t3[:], t4[:], ALU.add))
                            P.wait("sy", "sv", vv)
                            qstd[si] = P.dma("sy", dest_dram[:, h * S + c * 512: h * S + c * 512 + 512],
                                             qst[si][:], sem=qss[si])
            proj_qk("w1t", g["qv_d"])
            proj_qk("w2t", g["kv_d"])
            w3b = [sbl(f"p4_w3{i}", [128, 1024], BF16) for i in range(2)]
            w3s = [P.dsem("p4v0"), P.dsem("p4v1")]
            VW = P.dsem("p4vw")
            pv = psl("p4_pv", [128, 1024])
            vbf = sbl("p4_vbf", [128, 1024], BF16)
            w3d = [0, 0]
            for stile in range(NT):
                P.wait("te", "sv", P.sems["sv"].n)
                for k in range(8):
                    i = k % 2
                    P.wait("gp", "st", w3d[i])
                    ww = P.dma("gp", w3b[i][:], g["w3t"][128 * k:128 * (k + 1), :], sem=w3s[i])
                    P.wait("te", w3s[i], ww)
                    for nn in range(2):
                        P.op("te", lambda e, i=i, k=k, nn=nn, stile=stile: e.matmul(
                            pv[:, 512 * nn:512 * (nn + 1)],
                            xn_fm[:, k * S + stile * 128: k * S + stile * 128 + 128],
                            w3b[i][:, 512 * nn:512 * (nn + 1)],
                            start=(k == 0), stop=(k == 7)), inc=(nn == 1))
                    w3d[i] = P.sems["st"].n
                tv = P.sems["st"].n
                P.wait("ve", "st", tv)
                vv = P.op("ve", lambda e: e.tensor_copy(vbf[:], pv[:]))
                P.wait("sy", "sv", vv)
                wv = P.dma("sy", g["vl0_d"][stile * 128:(stile + 1) * 128, :], vbf[:], sem=VW)
                P.wait("ve", VW, wv)
        xn_ctx.__exit__(None, None, None)
        P.barrier("ph4")
        dump("q_v", g["qv_d"], (128, 8 * S), BF16)
        dump("k_v", g["kv_d"], (128, 8 * S), BF16)
        dump("vl0", g["vl0_d"], (S, E), BF16)

        rss_final = 0
        if upto >= 7:
            P.wait("gp", MZS, mz_final)
            P.wait("gp", "sv", slots_v)
            for e_ in range(8):
                for f in range(32):
                    rss_final = P.idma(
                        g["meta_d"], IndirectOffsetOnAxis(slots_i[:, 32 * e_ + f:32 * e_ + f + 1], 0),
                        ttab[:, f:f + 1], None, sem=RSSM)

        if upto < 5:
            P.build(); return True

        with ExitStack() as ph5:
            sbl = lambda n, s, d=F32: ph5.enter_context(P.nc.sbuf_tensor(n, s, d))
            psl = lambda n, s, d=F32: ph5.enter_context(P.nc.psum_tensor(n, s, d))
            v_tok = sbl("p5_vtok", [128, NT * 128], BF16)
            q_h = sbl("p5_qh", [128, S], BF16)
            k_h = sbl("p5_kh", [128, S], BF16)
            pt = sbl("p5_pt", [128, NT * 512], BF16)
            pp = [psl(f"p5_pp{i}", [128, 1024]) for i in range(2)]
            pO = psl("p5_pO", [128, 512])
            pZ = psl("p5_pZ", [1, 512])
            pMB = psl("p5_pMB", [128, 512])
            pTb = psl("p5_pTb", [128, 128], BF16)
            tmax = sbl("p5_tmax", [128, 1])
            mrun = sbl("p5_mrun", [128, 1])
            negm = sbl("p5_negm", [1, 1])
            negm2 = sbl("p5_negm2", [1, 1])
            bias_nm = sbl("p5_bias", [128, 1])
            rz = sbl("p5_rz", [1, 512])
            rzbc = sbl("p5_rzbc", [128, 512])
            obf = sbl("p5_obf", [128, 512], BF16)
            o2st = sbl("p5_o2st", [128, 512], BF16)
            vload = 0

            P1N = int(os.environ.get("P1N", "16"))

            def st_tiles(h, gq, which, on_tile):
                for jp in range(P1N):
                    i = jp % 2
                    buf = pp[i]
                    if which == 1:
                        P.wait("te", "sv", P.sems["sv"].n)
                    else:
                        P.wait("te", "ss", P.sems["ss"].n)
                    for u in range(2):
                        j = 2 * jp + u
                        c2, sb2 = j // 4, j % 4
                        ksl = k_h[:, c2 * 512 + sb2 * 128: c2 * 512 + sb2 * 128 + 128]
                        qsl = q_h[:, gq * 512: gq * 512 + 512]
                        P.op("te", lambda e, b_=buf, u=u, a=ksl, qq=qsl: e.matmul(
                            b_[:, 512 * u:512 * (u + 1)], a, qq, start=True, stop=True), inc=True)
                    on_tile(jp, buf)

            H5 = int(os.environ.get("H5", "8"))
            G5 = int(os.environ.get("G5", "8"))
            ST5 = int(os.environ.get("ST5", "9"))
            P5L = P.dsem("p5l")
            for h in range(H5):
                P.wait("gp", "st", vload)
                P.wait("sy", "st", vload)
                wv = P.dma("gp", v_tok[:].rearrange("p (c sb d) -> p c sb d", c=8, sb=4),
                           g["vl0_d"][512 * h:512 * (h + 1), :]
                           .rearrange("(sb sl) (c d) -> sl c sb d", sl=128, c=8), sem=P5L)
                wq_ = P.dma("sy", q_h[:], g["qv_d"][:, h * S:(h + 1) * S], sem=P5L)
                wk_ = P.dma("sy", k_h[:], g["kv_d"][:, h * S:(h + 1) * S], sem=P5L)
                P.wait("te", P5L, max(wv, wq_, wk_))
                for gq in range(G5):
                    P.op("ve", lambda e: e.memset(mrun[:], -1e30), inc=True)
                    P.fence("ve")

                    def p1(jp, buf):
                        tm = P.sems["st"].n
                        P.wait("ve", "st", tm)
                        P.op("ve", lambda e, b_=buf: e.reduce_max(tmax[:], b_[:], AX.X))
                        P.fence("ve")
                        P.op("ve", lambda e: e.tensor_max(mrun[:], mrun[:], tmax[:]))
                        P.fence("ve")
                    st_tiles(h, gq, 1, p1)
                    if ST5 < 2:
                        continue
                    vv = P.sems["sv"].n
                    P.wait("te", "sv", vv)
                    tv = P.op("te", lambda e: e.transpose(pMB[0:1, 0:128], mrun[:], ident[:]))
                    P.wait("ve", "st", tv)
                    P.op("ve", lambda e: e.reduce_max(negm[:], pMB[0:1, 0:128], AX.X, negate=True))
                    P.fence("ve")
                    vv = P.op("ve", lambda e: e.tensor_scalar(negm2[:], negm[:], SQ8, None, ALU.mult))
                    P.wait("te", "sv", vv)
                    tv = P.op("te", lambda e: e.matmul(pMB[:, 0:1], onesrow[:], negm2[:], start=True, stop=True))
                    P.wait("ve", "st", tv)
                    P.op("ve", lambda e: e.tensor_copy(bias_nm[:], pMB[:, 0:1]))
                    P.fence("ve")
                    P.wait("se", "sv", P.sems["sv"].n)

                    if ST5 < 3:
                        continue
                    EXP5 = int(os.environ.get("EXP5", "1"))

                    def p2(jp, buf):
                        tm = P.sems["st"].n
                        P.wait("se", "st", tm)
                        for u in range(2):
                            j = 2 * jp + u
                            if EXP5 == 1:
                                P.op("se", lambda e, b_=buf, u=u, j=j: e.activation(
                                    pt[:, 512 * j:512 * (j + 1)], b_[:, 512 * u:512 * (u + 1)],
                                    AF.Exp, bias=bias_nm[:, 0:1], scale=SQ8))
                            elif EXP5 == 2:
                                P.op("se", lambda e, b_=buf, u=u, j=j: e.activation(
                                    pt[:, 512 * j:512 * (j + 1)], b_[:, 512 * u:512 * (u + 1)],
                                    AF.Exp, bias=-20.0, scale=SQ8))
                            elif EXP5 == 3:
                                P.op("se", lambda e, b_=buf, u=u, j=j: e.activation(
                                    pt[:, 512 * j:512 * (j + 1)], b_[:, 512 * u:512 * (u + 1)],
                                    AF.Identity, bias=bias_nm[:, 0:1], scale=SQ8))
                    st_tiles(h, gq, 2, p2)
                    sdone = P.sems["ss"].n
                    if ST5 < 4:
                        continue
                    P.wait("te", "ss", sdone)
                    P.wait("te", "sv", P.sems["sv"].n)
                    for j in range(NT):
                        P.op("te", lambda e, j=j: e.matmul(
                            pO[:], v_tok[:, j * 128:(j + 1) * 128],
                            pt[:, 512 * j:512 * (j + 1)], start=(j == 0), stop=(j == NT - 1)),
                            inc=False)
                    for j in range(NT):
                        P.op("te", lambda e, j=j: e.matmul(
                            pZ[:], ones128b[:], pt[:, 512 * j:512 * (j + 1)],
                            start=(j == 0), stop=(j == NT - 1)), inc=(j == NT - 1))
                    tav = P.sems["st"].n
                    if ST5 < 5:
                        continue
                    P.wait("ve", "st", tav)
                    P.op("ve", lambda e: e.reciprocal(rz[:], pZ[:]))
                    vv = P.fence("ve") or P.sems["sv"].n
                    P.wait("te", "sv", P.sems["sv"].n)
                    tv = P.op("te", lambda e: e.matmul(pMB[:], onesrow[:], rz[:], start=True, stop=True))
                    P.wait("ve", "st", tv)
                    P.op("ve", lambda e: e.tensor_copy(rzbc[:], pMB[:]))
                    P.fence("ve")
                    P.op("ve", lambda e: e.tensor_tensor(obf[:], pO[:], rzbc[:], ALU.mult))
                    vv = P.fence("ve") or P.sems["sv"].n
                    if ST5 < 6:
                        continue
                    for t in range(4):
                        P.wait("te", "sv", P.sems["sv"].n)
                        tv = P.op("te", lambda e, t=t: e.transpose(
                            pTb[:], obf[:, 128 * t:128 * (t + 1)], identb[:]))
                        P.wait("ve", "st", tv)
                        P.op("ve", lambda e, t=t: e.tensor_copy(o2st[:, 128 * t:128 * (t + 1)], pTb[:]))
                    vv = P.fence("ve") or P.sems["sv"].n
                    if ST5 < 7:
                        continue
                    P.wait("sy", "sv", P.sems["sv"].n)
                    wo_ = P.dma("sy",
                                g["o2_d"].rearrange("(t sl gg) (hh d) -> sl gg hh t d",
                                                    t=4, sl=128, gg=8, hh=8)[:, gq, h, :, :],
                                o2st[:].rearrange("p (t d) -> p t d", t=4), sem=P.dsem("p5o"))
                    P.wait("ve", P.dsem("p5o"), wo_)
                vload = P.sems["st"].n
        P.barrier("ph5")
        dump("o2", g["o2_d"], (S, E), BF16)

        if upto < 6:
            P.build(); return True

        with ExitStack() as ph7:
            sbl = lambda n, s, d=F32: ph7.enter_context(P.nc.sbuf_tensor(n, s, d))
            psl = lambda n, s, d=F32: ph7.enter_context(P.nc.psum_tensor(n, s, d))
            wotb = sbl("p7_wot", [128, 8 * 1024], BF16)
            P7W = P.dsem("p7w"); P7O = P.dsem("p7o")
            P7X = [P.dsem("p7x0"), P.dsem("p7x1")]
            P7B = [P.dsem("p7b0"), P.dsem("p7b1")]
            wwo = P.dma("gp", wotb[:].rearrange("p (k q) -> p k q", k=8),
                        g["wot"].rearrange("(k p) q -> p k q", p=128), sem=P7W)
            o2t = [sbl(f"p7_o2{i}", [128, 1024], BF16) for i in range(8)]
            x0t = [sbl(f"p7_x0{i}", [128, 1024]) for i in range(2)]
            pA0 = psl("p7_pA0", [128, 1024])
            xnt = sbl("p7_xn", [128, 1024])
            x1sb = [sbl(f"p7_x1{i}", [128, 1024]) for i in range(2)]
            P.wait("te", P7W, wwo)
            x0d = [0, 0]
            x1d = [0, 0]
            te_prev_cblk = 0
            for cblk in range(4):
                P.wait("sy", "st", te_prev_cblk)
                o2d = [P.dma("sy", o2t[jc][:],
                             g["o2_d"][1024 * cblk + 128 * jc: 1024 * cblk + 128 * (jc + 1), :], sem=P7O)
                       for jc in range(8)]
                wo2 = max(o2d)
                for ut in range(8):
                    tile = 8 * cblk + ut
                    i = tile % 2
                    P.wait("sy", "sv", x0d[i])
                    wx = P.dma("sy", x0t[i][:], g["x0_tok"][tile * 128:(tile + 1) * 128, :], sem=P7B[i])
                    P.waitall("te", [(P7O, wo2), ("sv", P.sems["sv"].n)])
                    for jc in range(8):
                        for nn in range(2):
                            P.op("te", lambda e, jc=jc, nn=nn, ut=ut: e.matmul(
                                pA0[:, 512 * nn:512 * (nn + 1)],
                                wotb[:, 1024 * jc + 128 * ut: 1024 * jc + 128 * ut + 128],
                                o2t[jc][:, 512 * nn:512 * (nn + 1)],
                                start=(jc == 0), stop=(jc == 7)), inc=(jc == 7 and nn == 1))
                    tmm = P.sems["st"].n
                    P.waitall("ve", [("st", tmm), (P7B[i], wx)])
                    P.op("ve", lambda e, i=i, tile=tile: e.tensor_scalar(
                        xnt[:], x0t[i][:], negm_own[:, tile:tile + 1], rstd_own[:, tile:tile + 1],
                        ALU.add, ALU.mult))
                    P.fence("ve")
                    P.op("ve", lambda e, i=i: e.tensor_tensor(xnt[:], xnt[:], x0t[i][:], ALU.add))
                    P.fence("ve")
                    x0d[i] = P.sems["sv"].n
                    P.wait("ve", P7X[i], x1d[i])
                    vv = P.op("ve", lambda e, i=i: e.tensor_tensor(x1sb[i][:], xnt[:], pA0[:], ALU.add))
                    P.wait("sy", "sv", vv)
                    x1d[i] = P.dma("sy", g["x1_d"][tile * 128:(tile + 1) * 128, :], x1sb[i][:], sem=P7X[i])
                te_prev_cblk = P.sems["st"].n
        P.barrier("ph7")
        dump("x1base", g["x1_d"], (S, E))

        if upto < 7:
            P.build(); return True

        with ExitStack() as ph6:
            sbl = lambda n, s, d=F32: ph6.enter_context(P.nc.sbuf_tensor(n, s, d))
            psl = lambda n, s, d=F32: ph6.enter_context(P.nc.psum_tensor(n, s, d))
            for _eng in P.ENGS:
                P.wait(_eng, RSSM, rss_final)
                P.wait(_eng, MZS, mz_final)
            moe_sb.close()
            lnwbc = sbl("p6_lnwbc", [128, 1024])
            lnbbc = sbl("p6_lnbbc", [128, 1024])
            pbig = psl("p6_pbig", [128, 1024])
            for dst, row in ((lnwbc, lnw_sb), (lnbbc, lnb_sb)):
                for nn in range(2):
                    tv = P.op("te", lambda e, row=row, nn=nn: e.matmul(
                        pbig[:, 512 * nn:512 * (nn + 1)], onesrow[:], row[0:1, 512 * nn:512 * (nn + 1)],
                        start=True, stop=True))
                P.wait("ve", "st", tv)
                P.op("ve", lambda e, dst=dst: e.tensor_copy(dst[:], pbig[:]))
                P.fence("ve")
                P.wait("te", "sv", P.sems["sv"].n)
            wsb = sbl("p6_wsb", [128, 1024])
            mtile = sbl("p6_mt", [128, 1])
            tfl = sbl("p6_tfl", [128, 1])
            msk = sbl("p6_msk", [128, 1])
            wraw = sbl("p6_wraw", [128, 1])
            y2b = sbl("p6_y2b", [128, 1024], BF16)
            yfm = sbl("p6_yfm", [128, 8 * CAP], BF16)
            hfm = sbl("p6_hfm", [128, 32 * CAP], BF16)
            offm = sbl("p6_offm", [128, 8 * CAP])
            wcol = sbl("p6_wcol", [128, 5])
            ticol = sbl("p6_ti", [128, 5], I32)
            tie = sbl("p6_tie", [128, 5], I32)
            lnsum = sbl("p6_lnsum", [128, 1])
            lnsq = sbl("p6_lnsq", [128, 1])
            lnscr = sbl("p6_lnscr", [128, 1024])
            lnm = sbl("p6_lnm", [128, 1])
            lnv = sbl("p6_lnv", [128, 1])
            lnv2 = sbl("p6_lnv2", [128, 1])
            lnr = sbl("p6_lnr", [128, 1])
            w1s = [sbl(f"p6_w1s{i}", [128, 1024], BF16) for i in range(2)]
            w2s = [sbl(f"p6_w2s{i}", [128, 32 * 128], BF16) for i in range(2)]
            b1c = sbl("p6_b1c", [128, 32])
            b2c = sbl("p6_b2c", [128, 8])
            otok = sbl("p6_otok", [128, 1024])
            pmm = psl("p6_pmm", [128, CAP])
            ptr = psl("p6_ptr", [128, 128], BF16)
            ptrf = psl("p6_ptrf", [128, 128])
            w1d = [0, 0]
            w2d = [0, 0]
            scatter_prev = 0
            P6BB = P.dsem("p6bb")
            WS6 = P.dsem("p6ws")
            W1S = [P.dsem("p6w1a"), P.dsem("p6w1b")]
            W2S = [P.dsem("p6w2a"), P.dsem("p6w2b")]
            XSC = P.dsem("p6xsc")
            GW6 = P.dsem("p6gw")
            mt_free = 0
            wsb_v = 0
            wsb_s = 0
            tic_g = [0] * 5
            for e_ in range(8):
                wb1_ = P.dma("sy", b1c[:], g["b1s"][e_], sem=P6BB)
                wb2_ = P.dma("sy", b2c[:], g["b2s"][e_], sem=P6BB)
                for t5 in range(5):
                    P.wait("sy", "sv", mt_free)
                    wm_ = P.dma("sy", mtile[:],
                                g["meta_d"][CAP * e_ + 128 * t5: CAP * e_ + 128 * (t5 + 1), :], sem=WS6)
                    P.waitall("ve", [(WS6, wm_), (XSC, scatter_prev), (GW6, tic_g[t5])])
                    P.op("ve", lambda e: e.tensor_scalar(tfl[:], mtile[:], -1.0, 0.0, ALU.add, ALU.max), inc=False)
                    P.op("ve", lambda e: e.tensor_scalar(msk[:], mtile[:], 0.5, None, ALU.is_gt))
                    P.fence("ve")
                    P.op("ve", lambda e, t5=t5: e.tensor_copy(ticol[:, t5:t5 + 1], tfl[:]))
                    P.fence("ve")
                    P.op("ve", lambda e, e_=e_: e.tensor_scalar(tfl[:], tfl[:], 1.0, float(B * S * e_),
                                                                ALU.mult, ALU.add))
                    P.fence("ve")
                    P.op("ve", lambda e, t5=t5: e.tensor_copy(tie[:, t5:t5 + 1], tfl[:]))
                    P.fence("ve")
                    mt_free = P.sems["sv"].n
                    P.waitall("gp", [("sv", mt_free), ("sv", wsb_v), ("ss", wsb_s)])
                    P.idma(wsb[:, 0:1024], None, g["x0_tok"],
                           IndirectOffsetOnAxis(ticol[:, t5:t5 + 1], 0), sem=GW6)
                    gw_ = P.idma(wraw[:], None,
                                 g["probs_d"].rearrange("e (t x) -> (e t) x", x=1),
                                 IndirectOffsetOnAxis(tie[:, t5:t5 + 1], 0), sem=GW6)
                    tic_g[t5] = gw_
                    P.wait("ve", GW6, gw_)
                    P.op("ve", lambda e, t5=t5: e.tensor_tensor(wcol[:, t5:t5 + 1], wraw[:], msk[:], ALU.mult), inc=False)
                    P.op("ve", lambda e: e.reduce_sum(lnsum[:], wsb[:, 0:1024], AX.X))
                    P.fence("ve")
                    P.wait("se", "sv", P.sems["sv"].n)
                    sv_ = P.op("se", lambda e: e.activation(lnscr[:], wsb[:, 0:1024], AF.Square,
                                                            accum_out=lnsq[:]))
                    P.wait("ve", "ss", sv_)
                    P.op("ve", lambda e: e.tensor_scalar(lnm[:], lnsum[:], 1.0 / 1024, None, ALU.mult))
                    P.fence("ve")
                    P.op("ve", lambda e: e.tensor_tensor(lnv[:], lnm[:], lnm[:], ALU.mult))
                    P.fence("ve")
                    P.op("ve", lambda e: e.scalar_tensor_tensor(lnv[:], lnsq[:], 1.0 / 1024, lnv[:],
                                                                ALU.mult, ALU.subtract))
                    P.fence("ve")
                    P.op("ve", lambda e: e.tensor_scalar(lnv2[:], lnv[:], 1e-5, None, ALU.add))
                    P.fence("ve")
                    P.wait("se", "sv", P.sems["sv"].n)
                    sv_ = P.op("se", lambda e: e.activation(lnv2[:], lnv2[:], AF.Sqrt))
                    P.wait("ve", "ss", sv_)
                    P.op("ve", lambda e: e.reciprocal(lnr[:], lnv2[:]))
                    P.fence("ve")
                    P.op("ve", lambda e: e.tensor_scalar(lnm[:], lnm[:], -1.0, None, ALU.mult))
                    P.fence("ve")
                    P.op("ve", lambda e: e.tensor_scalar(lnscr[:], wsb[:, 0:1024], lnm[:, 0:1],
                                                         lnr[:, 0:1], ALU.add, ALU.mult))
                    P.fence("ve")
                    P.op("ve", lambda e: e.tensor_tensor(lnscr[:], lnscr[:], lnwbc[:], ALU.mult))
                    P.fence("ve")
                    P.op("ve", lambda e: e.tensor_tensor(y2b[:], lnscr[:], lnbbc[:], ALU.add))
                    P.fence("ve")
                    vv = P.sems["sv"].n
                    for k in range(8):
                        P.wait("te", "sv", vv)
                        tv = P.op("te", lambda e, k=k: e.transpose(ptr[:], y2b[:, 128 * k:128 * (k + 1)], identb[:]))
                        P.wait("ve", "st", tv)
                        vv = P.op("ve", lambda e, k=k, t5=t5: e.tensor_copy(
                            yfm[:, CAP * k + 128 * t5: CAP * k + 128 * (t5 + 1)], ptr[:]))
                        P.wait("te", "sv", vv)
                    wsb_v = P.sems["sv"].n
                    wsb_s = P.sems["ss"].n
                for m in range(32):
                    i = m % 2
                    P.wait("gp", "st", w1d[i])
                    ww = P.dma("gp", w1s[i][:].rearrange("p (k q) -> p k q", k=8),
                               g["fc1sT"][e_, :, 128 * m:128 * (m + 1)].rearrange("(k p) q -> p k q", p=128),
                               sem=W1S[i])
                    P.waitall("te", [(W1S[i], ww), ("ss", P.sems["ss"].n)])
                    for k in range(8):
                        for nn in range(2):
                            n0, n1 = (0, 512) if nn == 0 else (512, CAP)
                            P.op("te", lambda e, i=i, k=k, n0=n0, n1=n1: e.matmul(
                                pmm[:, n0:n1], w1s[i][:, 128 * k:128 * (k + 1)],
                                yfm[:, CAP * k + n0: CAP * k + n1],
                                start=(k == 0), stop=(k == 7)), inc=(nn == 1))
                    w1d[i] = P.sems["st"].n
                    P.waitall("se", [("st", w1d[i]), (P6BB, wb1_)])
                    P.op("se", lambda e, m=m: e.activation(
                        hfm[:, CAP * m: CAP * (m + 1)], pmm[:], AF.Gelu_apprx_tanh,
                        bias=b1c[:, m:m + 1], scale=1.0))
                    P.fence("se")
                sprev = P.sems["ss"].n
                for oc in range(8):
                    i = oc % 2
                    P.wait("gp", "st", w2d[i])
                    ww = P.dma("gp", w2s[i][:].rearrange("p (k q) -> p k q", k=32),
                               g["fc2sT"][e_, :, 128 * oc:128 * (oc + 1)].rearrange("(k p) q -> p k q", p=128),
                               sem=W2S[i])
                    P.waitall("te", [(W2S[i], ww), ("ss", sprev)])
                    for k in range(32):
                        for nn in range(2):
                            n0, n1 = (0, 512) if nn == 0 else (512, CAP)
                            P.op("te", lambda e, i=i, k=k, n0=n0, n1=n1: e.matmul(
                                pmm[:, n0:n1], w2s[i][:, 128 * k:128 * (k + 1)],
                                hfm[:, CAP * k + n0: CAP * k + n1],
                                start=(k == 0), stop=(k == 31)), inc=(nn == 1))
                    w2d[i] = P.sems["st"].n
                    P.waitall("se", [("st", w2d[i]), (P6BB, wb2_)])
                    sprev = P.op("se", lambda e, oc=oc: e.activation(
                        offm[:, CAP * oc: CAP * (oc + 1)], pmm[:], AF.Identity,
                        bias=b2c[:, oc:oc + 1], scale=1.0))
                    P.fence("se")
                for t5 in range(5):
                    vv = P.sems["sv"].n
                    for oc in range(8):
                        P.waitall("te", [("sv", vv), ("ss", P.sems["ss"].n)])
                        tv = P.op("te", lambda e, oc=oc, t5=t5: e.transpose(
                            ptrf[:], offm[:, CAP * oc + 128 * t5: CAP * oc + 128 * (t5 + 1)], ident[:]))
                        P.wait("ve", "st", tv)
                        vv = P.op("ve", lambda e, oc=oc, t5=t5: e.tensor_scalar(
                            otok[:, 128 * oc:128 * (oc + 1)], ptrf[:], wcol[:, t5:t5 + 1], None, ALU.mult))
                    P.fence("ve")
                    vv = P.sems["sv"].n
                    P.waitall("gp", [("sv", vv), (XSC, scatter_prev)])
                    scatter_prev = P.idma(g["x1_d"], IndirectOffsetOnAxis(ticol[:, t5:t5 + 1], 0),
                                          otok[:], None, compute_op=ALU.add, sem=XSC)
                    P.wait("ve", XSC, scatter_prev)
        P.barrier("ph6")
        dump("x1full", g["x1_d"], (S, E))
        return False

    if start <= 7:
        _pre = set(P._dsems)
        if _l0():
            return dbg_outs, st
        P.release_dsems(set(P._dsems) - _pre)

    if upto < 8:
        P.build(); return dbg_outs, st

    _pre8 = set(P._dsems)
    with ExitStack() as ph8:
        _sb = P.sb
        P.sb = lambda n, s, d, _st=ph8: _st.enter_context(P.nc.sbuf_tensor("x1" + n, s, d))
        emit_stats(g["x1_d"], stats1_negm, stats1_rstd, dst_rms=stats1_rms)
        P.sb = _sb
    with ExitStack() as ph8b:
        psl = lambda n, s: ph8b.enter_context(P.nc.psum_tensor(n, s, F32))
        sbl = lambda n, s, d=F32: ph8b.enter_context(P.nc.sbuf_tensor(n, s, d))
        pT = psl("p8_pT", [128, 128])
        stT = sbl("p8_stT", [128, 128])
        B8 = P.dsem("b8")
        for row, tile in ((0, stats1_negm), (1, stats1_rstd), (2, stats1_rms)):
            P.wait("te", "sv", P.sems["sv"].n)
            tv = P.op("te", lambda e, t=tile: e.transpose(pT[0:NT, 0:128], t[:], ident[:]))
            P.waitall("ve", [("st", tv), (B8, B8.n)])
            vv = P.op("ve", lambda e: e.tensor_copy(stT[0:NT, :], pT[0:NT, 0:128]))
            P.wait("sy", "sv", vv)
            P.dma("sy", g["sd1_d"][row:row + 1, :].rearrange("o (f p) -> (o f) p", p=128),
                  stT[0:NT, :], sem=B8)
    P.barrier("ph8")
    P.release_dsems(set(P._dsems) - _pre8)
    dump("sd1", g["sd1_d"], (3, S))
    dump("x1v8", g["x1_d"], (S, E))

    if upto < 9:
        P.build(); return dbg_outs, st

    k1sb_ctx = P.nc.sbuf_tensor("k1sb", [128, 2 * S], BF16)
    k1sb = k1sb_ctx.__enter__()
    v1sb_ctx = P.nc.sbuf_tensor("v1sb", [128, NT * 256], BF16)
    v1sb = v1sb_ctx.__enter__()
    xn1_ctx = P.nc.sbuf_tensor("xn1_fm", [128, NC * S], BF16)
    xn1_fm = xn1_ctx.__enter__()
    def _ph9_fn():
        with ExitStack() as ph9:
            sbl = lambda n, s, d=F32: ph9.enter_context(P.nc.sbuf_tensor(n, s, d))
            psl = lambda n, s, d=F32: ph9.enter_context(P.nc.psum_tensor(n, s, d))
            x1t = [sbl(f"p9_x1{i}", [128, 1024]) for i in range(2)]
            x1s = [P.dsem("p9x0"), P.dsem("p9x1")]
            xnb = sbl("p9_xnb", [128, 1024], BF16)
            pt9 = psl("p9_pt", [128, 128], BF16)
            x1d = [0, 0]
            for tile in range(NT):
                i = tile % 2
                P.wait("sy", "sv", x1d[i])
                wx = P.dma("sy", x1t[i][:], g["x1_d"][tile * 128:(tile + 1) * 128, :], sem=x1s[i])
                P.waitall("ve", [(x1s[i], wx), ("st", P.sems["st"].n)])
                x1d[i] = P.op("ve", lambda e, i=i, tile=tile: e.tensor_scalar(
                    xnb[:], x1t[i][:], stats1_negm[:, tile:tile + 1], stats1_rstd[:, tile:tile + 1],
                    ALU.add, ALU.mult))
                P.fence("ve")
                vv = P.sems["sv"].n
                for k in range(NC):
                    P.wait("te", "sv", vv)
                    tv = P.op("te", lambda e, k=k: e.transpose(pt9[:], xnb[:, 128 * k:128 * (k + 1)], identb[:]))
                    P.wait("ve", "st", tv)
                    vv = P.op("ve", lambda e, k=k, tile=tile: e.tensor_copy(
                        xn1_fm[:, k * S + tile * 128: k * S + tile * 128 + 128], pt9[:]))
                P.fence("ve")
    _pre9 = set(P._dsems)
    _ph9_fn()
    P.barrier("ph9")
    P.release_dsems(set(P._dsems) - _pre9)
    dump("xn1fm", xn1_fm[:], (128, NC * S), BF16)
    dump("x1v9", g["x1_d"], (S, E))

    if upto < 10:
        P.build(); return dbg_outs, st

    P10 = int(os.environ.get("P10", "9"))

    def _ph10a_fn():
        with ExitStack() as ph10:
            sbl = lambda n, s, d=F32: ph10.enter_context(P.nc.sbuf_tensor(n, s, d))
            psl = lambda n, s, d=F32: ph10.enter_context(P.nc.psum_tensor(n, s, d))

            def proj_fm(wt_key, src_fm_sb, dst_d, tag):
                wb = [sbl(f"{tag}_w{i}", [128, 1024], BF16) for i in range(2)]
                wsem = [P.dsem(f"{tag}w{i}") for i in range(2)]
                osem = [P.dsem(f"{tag}o{i}") for i in range(2)]
                ob = [sbl(f"{tag}_ob{i}", [128, 2048], BF16) for i in range(2)]
                pq = psl(f"{tag}_pq", [128, 2048])
                wdone = [0, 0]
                odone = [0, 0]
                for c in range(NC):
                    i = c % 2
                    P.wait("gp", "st", wdone[i])
                    ww = P.dma("gp", wb[i][:].rearrange("p (k q) -> p k q", k=8),
                               g[wt_key][:, 128 * c:128 * (c + 1)].rearrange("(k p) q -> p k q", p=128),
                               sem=wsem[i])
                    P.wait("te", wsem[i], ww)
                    for half in range(2):
                        hi = half
                        P.wait("te", "sv", P.sems["sv"].n)
                        for k in range(8):
                            for nt in range(4):
                                P.op("te", lambda e, i=i, k=k, half=half, nt=nt: e.matmul(
                                    pq[:, 512 * nt:512 * (nt + 1)], wb[i][:, 128 * k:128 * (k + 1)],
                                    src_fm_sb[:, k * S + half * 2048 + 512 * nt:
                                              k * S + half * 2048 + 512 * nt + 512],
                                    start=(k == 0), stop=(k == 7)), inc=(k == 7 and nt == 3))
                        wdone[i] = P.sems["st"].n
                        P.waitall("ve", [("st", P.sems["st"].n), (osem[hi], odone[hi])])
                        odone[hi] = P.op("ve", lambda e, hi=hi: e.tensor_copy(ob[hi][:], pq[:]))
                        P.wait("sy", "sv", P.sems["sv"].n)
                        odone[hi] = P.dma("sy", dst_d[128 * c:128 * (c + 1),
                                                      half * 2048:(half + 1) * 2048],
                                          ob[hi][:], sem=osem[hi])
            proj_fm("l1_attn_f", xn1_fm, g["xf_d"], "pF")
            vw = [sbl(f"p10_vw{i}", [128, 1024], BF16) for i in range(2)] if P10 >= 2 else None
            vws = [P.dsem("p10v0"), P.dsem("p10v1")]
            VV = P.dsem("p10vv")
            pv = psl("p10_pv", [128, 1024]) if P10 >= 2 else None
            vbf = sbl("p10_vbf", [128, 1024], BF16) if P10 >= 2 else None
            vd = [0, 0]
            for stile in range(NT if P10 >= 2 else 0):
                P.wait("te", "sv", P.sems["sv"].n)
                for k in range(8):
                    i = k % 2
                    P.wait("gp", "st", vd[i])
                    ww = P.dma("gp", vw[i][:], g["l1_attn_v"][128 * k:128 * (k + 1), :], sem=vws[i])
                    P.wait("te", vws[i], ww)
                    for nn in range(2):
                        P.op("te", lambda e, i=i, k=k, nn=nn, stile=stile: e.matmul(
                            pv[:, 512 * nn:512 * (nn + 1)],
                            xn1_fm[:, k * S + stile * 128: k * S + stile * 128 + 128],
                            vw[i][:, 512 * nn:512 * (nn + 1)],
                            start=(k == 0), stop=(k == 7)), inc=(nn == 1))
                    vd[i] = P.sems["st"].n
                tv = P.sems["st"].n
                P.wait("ve", "st", tv)
                vv = P.op("ve", lambda e: e.tensor_copy(vbf[:], pv[:]))
                P.wait("sy", "sv", vv)
                wv = P.dma("sy", g["vx_d"][stile * 128:(stile + 1) * 128, :], vbf[:], sem=VV)
                P.wait("ve", VV, wv)
    _pre10a = set(P._dsems)
    _ph10a_fn()
    P.barrier("ph10a")
    P.release_dsems(set(P._dsems) - _pre10a)
    if P10 < 2:
        P.build(); return dbg_outs, st
    if P10 < 3:
        P.build(); return dbg_outs, st
    xf_ctx = P.nc.sbuf_tensor("xf_fm", [128, NC * S], BF16)
    xf_fm = xf_ctx.__enter__()
    def _ph10b_fn():
        with ExitStack() as ph10b:
            sbl = lambda n, s, d=F32: ph10b.enter_context(P.nc.sbuf_tensor(n, s, d))
            psl = lambda n, s, d=F32: ph10b.enter_context(P.nc.psum_tensor(n, s, d))
            XFL = P.dsem("xfl")
            wxf = P.dma("sy", xf_fm[:].rearrange("p (c t) -> p c t", c=8),
                        g["xf_d"].rearrange("(c p) t -> p c t", p=128), sem=XFL)
            for _e in ("te", "ve"):
                P.wait(_e, XFL, wxf)

            def proj_fm2(wt_key, src_fm_sb, dst_d, tag):
                wb = [sbl(f"{tag}_w{i}", [128, 1024], BF16) for i in range(2)]
                wsem = [P.dsem(f"{tag}w{i}") for i in range(2)]
                osem = [P.dsem(f"{tag}o{i}") for i in range(2)]
                ob = [sbl(f"{tag}_ob{i}", [128, 2048], BF16) for i in range(2)]
                pq = psl(f"{tag}_pq", [128, 2048])
                wdone = [0, 0]
                odone = [0, 0]
                for c in range(NC):
                    i = c % 2
                    P.wait("gp", "st", wdone[i])
                    ww = P.dma("gp", wb[i][:].rearrange("p (k q) -> p k q", k=8),
                               g[wt_key][:, 128 * c:128 * (c + 1)].rearrange("(k p) q -> p k q", p=128),
                               sem=wsem[i])
                    P.wait("te", wsem[i], ww)
                    for half in range(2):
                        hi = half
                        P.wait("te", "sv", P.sems["sv"].n)
                        for k in range(8):
                            for nt in range(4):
                                P.op("te", lambda e, i=i, k=k, half=half, nt=nt: e.matmul(
                                    pq[:, 512 * nt:512 * (nt + 1)], wb[i][:, 128 * k:128 * (k + 1)],
                                    src_fm_sb[:, k * S + half * 2048 + 512 * nt:
                                              k * S + half * 2048 + 512 * nt + 512],
                                    start=(k == 0), stop=(k == 7)), inc=(k == 7 and nt == 3))
                        wdone[i] = P.sems["st"].n
                        P.waitall("ve", [("st", P.sems["st"].n), (osem[hi], odone[hi])])
                        odone[hi] = P.op("ve", lambda e, hi=hi: e.tensor_copy(ob[hi][:], pq[:]))
                        P.wait("sy", "sv", P.sems["sv"].n)
                        odone[hi] = P.dma("sy", dst_d[128 * c:128 * (c + 1),
                                                      half * 2048:(half + 1) * 2048],
                                          ob[hi][:], sem=osem[hi])
            proj_fm2("l1_attn_q", xf_fm, g["qx_d"], "pQ")
            proj_fm2("l1_attn_k", xf_fm, g["kx_d"], "pK")
    _pre10b = set(P._dsems)
    _ph10b_fn()
    xf_ctx.__exit__(None, None, None)
    P.barrier("ph10b")
    P.release_dsems(set(P._dsems) - _pre10b)
    if P10 < 4:
        P.build(); return dbg_outs, st

    def _ph10c_fn():
        with ExitStack() as ph10c:
            sbl = lambda n, s, d=F32: ph10c.enter_context(P.nc.sbuf_tensor(n, s, d))
            psl = lambda n, s, d=F32: ph10c.enter_context(P.nc.psum_tensor(n, s, d))
            qg_sb = sbl("pA_qg", [128, 8 * 512], BF16)
            kt = [sbl(f"pA_kt{i}", [128, 1024], BF16) for i in range(2)]
            vt = [sbl(f"pA_vt{i}", [128, 1024], BF16) for i in range(2)]
            pt = sbl("pA_pt", [128, NT * 512], BF16)
            QGS = P.dsem("pAq")
            kts = [P.dsem("pAk0"), P.dsem("pAk1")]
            vts = [P.dsem("pAv0"), P.dsem("pAv1")]
            A1W = P.dsem("pAa")
            pp = [psl(f"pA_pp{i}", [128, 512]) for i in range(2)]
            pO4 = psl("pA_pO4", [128, 2048])
            pZ = psl("pA_pZ", [1, 512])
            pMB = psl("pA_pMB", [128, 512])
            pTf = pMB
            tmax = sbl("pA_tmax", [128, 1])
            mrun = sbl("pA_mrun", [128, 1])
            negm = sbl("pA_negm", [1, 1])
            negm2 = sbl("pA_negm2", [1, 1])
            bias_nm = sbl("pA_bias", [128, 1])
            rz = sbl("pA_rz", [1, 512])
            rzbc = sbl("pA_rzbc", [128, 512])
            onrm = sbl("pA_onrm", [128, 512])
            a1st = sbl("pA_a1st", [128, 4 * 1024])
            ktd = [0, 0]
            vtd = [0, 0]
            a1w = 0
            for gq in range(8):
                wq_ = P.dma("sy", qg_sb[:].rearrange("p (c t) -> p c t", c=8),
                            g["qx_d"][:, gq * 512:(gq + 1) * 512].rearrange("(c p) t -> p c t", p=128),
                            sem=QGS)
                P.wait("te", QGS, wq_)
                P.op("ve", lambda e: e.memset(mrun[:], -1e30))
                P.fence("ve")
                for which in (1, 2):
                    for j in range(NT):
                        i = j % 2
                        P.wait("gp", "st", ktd[i])
                        wk_ = P.dma("gp", kt[i][:].rearrange("p (k q) -> p k q", k=8),
                                    g["kx_d"][:, 128 * j:128 * (j + 1)].rearrange("(k p) q -> p k q", p=128),
                                    sem=kts[i])
                        P.wait("te", kts[i], wk_)
                        if which == 1:
                            P.wait("te", "sv", P.sems["sv"].n)
                        else:
                            P.wait("te", "ss", P.sems["ss"].n)
                        buf = pp[j % 2]
                        for k in range(8):
                            P.op("te", lambda e, i=i, k=k, b_=buf: e.matmul(
                                b_[:], kt[i][:, 128 * k:128 * (k + 1)],
                                qg_sb[:, k * 512:(k + 1) * 512],
                                start=(k == 0), stop=(k == 7)), inc=(k == 7))
                        ktd[i] = P.sems["st"].n
                        tm = P.sems["st"].n
                        if which == 1:
                            P.wait("ve", "st", tm)
                            P.op("ve", lambda e, b_=buf: e.reduce_max(tmax[:], b_[:], AX.X))
                            P.fence("ve")
                            P.op("ve", lambda e: e.tensor_max(mrun[:], mrun[:], tmax[:]))
                            P.fence("ve")
                        else:
                            P.wait("se", "st", tm)
                            P.op("se", lambda e, b_=buf, j=j: e.activation(
                                pt[:, 512 * j:512 * (j + 1)], b_[:],
                                AF.Exp, bias=bias_nm[:, 0:1], scale=S32))
                    if which == 1:
                        vv = P.sems["sv"].n
                        P.wait("te", "sv", vv)
                        tv = P.op("te", lambda e: e.transpose(pMB[0:1, 0:128], mrun[:], ident[:]))
                        P.wait("ve", "st", tv)
                        P.op("ve", lambda e: e.reduce_max(negm[:], pMB[0:1, 0:128], AX.X, negate=True))
                        P.fence("ve")
                        vv = P.op("ve", lambda e: e.tensor_scalar(negm2[:], negm[:], S32, None, ALU.mult))
                        P.wait("te", "sv", vv)
                        tv = P.op("te", lambda e: e.matmul(pMB[:, 0:1], onesrow[:], negm2[:],
                                                           start=True, stop=True))
                        P.wait("ve", "st", tv)
                        P.op("ve", lambda e: e.tensor_copy(bias_nm[:], pMB[:, 0:1]))
                        P.fence("ve")
                        P.wait("se", "sv", P.sems["sv"].n)
                sdone = P.sems["ss"].n
                P.waitall("te", [("ss", sdone), ("sv", P.sems["sv"].n)])
                for j in range(NT):
                    P.op("te", lambda e, j=j: e.matmul(pZ[:], ones128b[:], pt[:, 512 * j:512 * (j + 1)],
                                                       start=(j == 0), stop=(j == NT - 1)),
                         inc=(j == NT - 1))
                tz = P.sems["st"].n
                P.wait("ve", "st", tz)
                P.op("ve", lambda e: e.reciprocal(rz[:], pZ[:]))
                vv = P.fence("ve") or P.sems["sv"].n
                P.wait("te", "sv", P.sems["sv"].n)
                tv = P.op("te", lambda e: e.matmul(pMB[:], onesrow[:], rz[:], start=True, stop=True))
                P.wait("ve", "st", tv)
                P.op("ve", lambda e: e.tensor_copy(rzbc[:], pMB[:]))
                P.fence("ve")
                for dh in range(2):
                    P.wait("te", "sv", P.sems["sv"].n)
                    for j in range(NT):
                        i = j % 2
                        P.wait("gp", "st", vtd[i])
                        wv_ = P.dma("gp", vt[i][:, 0:512],
                                    g["vx_d"][128 * j:128 * (j + 1), 512 * dh:512 * (dh + 1)], sem=vts[i])
                        P.wait("te", vts[i], wv_)
                        for dq in range(4):
                            P.op("te", lambda e, i=i, j=j, dq=dq: e.matmul(
                                pO4[:, 512 * dq:512 * (dq + 1)], vt[i][:, 128 * dq:128 * (dq + 1)],
                                pt[:, 512 * j:512 * (j + 1)],
                                start=(j == 0), stop=(j == NT - 1)), inc=(dq == 3))
                        vtd[i] = P.sems["st"].n
                    tav = P.sems["st"].n
                    for dq in range(4):
                        dt = 4 * dh + dq
                        P.wait("ve", "st", tav)
                        P.op("ve", lambda e, dq=dq: e.tensor_tensor(
                            onrm[:], pO4[:, 512 * dq:512 * (dq + 1)], rzbc[:], ALU.mult))
                        P.fence("ve")
                        vv = P.sems["sv"].n
                        for t in range(4):
                            P.wait("te", "sv", vv)
                            tv = P.op("te", lambda e, t=t: e.transpose(
                                pTf[:, 0:128], onrm[:, 128 * t:128 * (t + 1)], ident[:]))
                            P.wait("ve", "st", tv)
                            vv = P.op("ve", lambda e, t=t, dt=dt: e.tensor_copy(
                                a1st[:, 1024 * t + 128 * dt: 1024 * t + 128 * (dt + 1)], pTf[:, 0:128]))
                        P.fence("ve")
                P.waitall("sy", [("sv", P.sems["sv"].n), (A1W, a1w)])
                a1w = P.dma("sy", g["a1_d"][gq * 512:(gq + 1) * 512, :]
                            .rearrange("(t sl) d -> sl t d", sl=128),
                            a1st[:].rearrange("p (t d) -> p t d", t=4), sem=A1W)
                P.wait("ve", A1W, a1w)
    _pre10c = set(P._dsems)
    _ph10c_fn()
    P.barrier("ph10")
    P.release_dsems(set(P._dsems) - _pre10c)
    dump("a1", g["a1_d"], (S, E))

    if upto < 11:
        P.build(); return dbg_outs, st

    def _ph11_fn():
        with ExitStack() as ph11:
            sbl = lambda n, s, d=F32: ph11.enter_context(P.nc.sbuf_tensor(n, s, d))
            psl = lambda n, s, d=F32: ph11.enter_context(P.nc.psum_tensor(n, s, d))
            cos2t = sbl("p11_cos2", [64, S])
            sin2t = sbl("p11_sin2", [64, S])
            C11 = P.dsem("p11c")
            wc = P.dma("sy", cos2t[:], g["cos2"], sem=C11)
            ws2 = P.dma("sy", sin2t[:], g["sin2"], sem=C11)
            P.wait("ve", C11, max(wc, ws2))
            wqb = [sbl(f"p11_wq{i}", [128, 1024], BF16) for i in range(2)]
            wqs = [P.dsem("p11w0"), P.dsem("p11w1")]
            QO = P.dsem("p11qo")
            pq = psl("p11_pq", [128, 512])
            prm0 = psl("p11_prm0", [1, 512])
            prm1 = psl("p11_prm1", [1, 512])
            pbc = psl("p11_pbc", [64, 512])
            sqs = sbl("p11_sqs", [128, 512])
            rr0 = sbl("p11_rr0", [1, 512]); rr1 = sbl("p11_rr1", [1, 512])
            rr0b = sbl("p11_rr0b", [1, 512]); rr1b = sbl("p11_rr1b", [1, 512])
            sbc = sbl("p11_sbc", [64, 512])
            t1 = sbl("p11_t1", [64, 512]); t2 = sbl("p11_t2", [64, 512])
            t3 = sbl("p11_t3", [64, 512]); t4 = sbl("p11_t4", [64, 512])
            u1 = sbl("p11_u1", [64, 512]); u2 = sbl("p11_u2", [64, 512])
            q1bA = sbl("p11_q1bA", [64, 512], BF16)
            q1bB = sbl("p11_q1bB", [64, 512], BF16)
            wdone = [0, 0]
            q1w = 0

            def qk_proj(wt_key, nchunks, dest, gain_sb, tag):
                nonlocal q1w
                for c in range(nchunks):
                    i = c % 2
                    P.wait("gp", "st", wdone[i])
                    ww = P.dma("gp", wqb[i][:].rearrange("p (k q) -> p k q", k=8),
                               g[wt_key][:, 128 * c:128 * (c + 1)].rearrange("(k p) q -> p k q", p=128),
                               sem=wqs[i])
                    P.wait("te", wqs[i], ww)
                    for sub in range(8):
                        t0 = sub * 512
                        P.waitall("te", [("sv", P.sems["sv"].n), ("ss", P.sems["ss"].n)])
                        for k in range(8):
                            P.op("te", lambda e, i=i, k=k, t0=t0: e.matmul(
                                pq[:], wqb[i][:, 128 * k:128 * (k + 1)],
                                xn1_fm[:, k * S + t0: k * S + t0 + 512],
                                start=(k == 0), stop=(k == 7)), inc=(k == 7))
                        wdone[i] = P.sems["st"].n
                        tmm = P.sems["st"].n
                        P.wait("se", "st", tmm)
                        sv_ = P.op("se", lambda e: e.activation(sqs[:], pq[:], AF.Square))
                        P.wait("te", "ss", sv_)
                        P.op("te", lambda e: e.matmul(prm0[:], hsel_sb[:, 0:1], sqs[:],
                                                      start=True, stop=True), inc=False)
                        tv = P.op("te", lambda e: e.matmul(prm1[:], hsel_sb[:, 1:2], sqs[:],
                                                           start=True, stop=True))
                        P.wait("ve", "st", tv)
                        for hh, rrx, rrxb, prmx in ((0, rr0, rr0b, prm0), (1, rr1, rr1b, prm1)):
                            h = 2 * c + hh
                            P.op("ve", lambda e, rrx=rrx, prmx=prmx: e.tensor_scalar(
                                rrx[:], prmx[:], 1.0 / 64, 1e-6, ALU.mult, ALU.add))
                            P.fence("ve")
                            P.wait("se", "sv", P.sems["sv"].n)
                            sv_ = P.op("se", lambda e, rrx=rrx, rrxb=rrxb: e.activation(rrxb[:], rrx[:], AF.Sqrt))
                            P.wait("ve", "ss", sv_)
                            P.op("ve", lambda e, rrx=rrx, rrxb=rrxb: e.reciprocal(rrx[:], rrxb[:]))
                            P.fence("ve")
                            P.op("ve", lambda e, rrx=rrx, rrxb=rrxb, h=h, gain_sb=gain_sb: e.tensor_single_scalar(
                                rrxb[:], rrx[:], gain_sb[0:1, h:h + 1], ALU.mult))
                            P.fence("ve")
                            P.op("ve", lambda e, rrx=rrx, rrxb=rrxb: e.tensor_copy(rrx[:], rrxb[:]))
                            P.fence("ve")
                        vv = P.sems["sv"].n
                        P.wait("te", "sv", vv)
                        P.op("te", lambda e: e.matmul(pbc[0:32, :], onesrow[0:1, 0:32], rr0[:],
                                                      start=True, stop=True), inc=False)
                        tv = P.op("te", lambda e: e.matmul(pbc[32:64, :], onesrow[0:1, 0:32], rr1[:],
                                                           start=True, stop=True))
                        P.wait("ve", "st", tv)
                        P.op("ve", lambda e: e.tensor_copy(sbc[:], pbc[:]))
                        P.fence("ve")
                        cs = cos2t[:, t0:t0 + 512]
                        sn = sin2t[:, t0:t0 + 512]
                        P.op("ve", lambda e, cs=cs: e.tensor_tensor(t1[:], pq[0:64, :], cs, ALU.mult), inc=False)
                        P.op("ve", lambda e, sn=sn: e.tensor_tensor(t2[:], pq[64:128, :], sn, ALU.mult), inc=False)
                        P.op("ve", lambda e, sn=sn: e.tensor_tensor(t3[:], pq[0:64, :], sn, ALU.mult), inc=False)
                        P.op("ve", lambda e, cs=cs: e.tensor_tensor(t4[:], pq[64:128, :], cs, ALU.mult))
                        P.fence("ve")
                        P.op("ve", lambda e: e.tensor_tensor(u1[:], t1[:], t2[:], ALU.add), inc=False)
                        P.op("ve", lambda e: e.tensor_tensor(u2[:], t4[:], t3[:], ALU.subtract))
                        P.fence("ve")
                        P.op("ve", lambda e: e.tensor_tensor(q1bA[0:32, :], u1[0:32, :], sbc[0:32, :], ALU.mult), inc=False)
                        P.op("ve", lambda e: e.tensor_tensor(q1bA[32:64, :], u2[0:32, :], sbc[0:32, :], ALU.mult), inc=False)
                        P.op("ve", lambda e: e.tensor_tensor(q1bB[0:32, :], u1[32:64, :], sbc[32:64, :], ALU.mult), inc=False)
                        P.op("ve", lambda e: e.tensor_tensor(q1bB[32:64, :], u2[32:64, :], sbc[32:64, :], ALU.mult))
                        P.fence("ve")
                        vv = P.sems["sv"].n
                        dest(c, sub, q1bA, q1bB, vv)
            def q_dest(c, sub, qA, qB, vv):
                nonlocal q1w
                P.wait("sy", "sv", vv)
                P.dma("sy", g["q1_d"][128 * c:128 * c + 64, sub * 512:(sub + 1) * 512],
                      qA[:], sem=QO)
                q1w = P.dma("sy", g["q1_d"][128 * c + 64:128 * c + 128, sub * 512:(sub + 1) * 512],
                            qB[:], sem=QO)
                P.wait("ve", QO, q1w)
            qk_proj("wqt", 8, q_dest, qgain_sb, "q")
            def k_dest(c, sub, qA, qB, vv):
                P.op("ve", lambda e, c=c, sub=sub, qA=qA: e.tensor_copy(
                    k1sb[0:64, c * S + sub * 512: c * S + sub * 512 + 512], qA[:]), inc=False)
                P.op("ve", lambda e, c=c, sub=sub, qB=qB: e.tensor_copy(
                    k1sb[64:128, c * S + sub * 512: c * S + sub * 512 + 512], qB[:]))
                P.fence("ve")
            qk_proj("wkt", 2, k_dest, kgain_sb, "k")
            vwb = [sbl(f"p11_vw{i}", [128, 256], BF16) for i in range(2)]
            vws = [P.dsem("p11v0"), P.dsem("p11v1")]
            pv1 = psl("p11_pv1", [128, 256])
            v1d = [0, 0]
            for stile in range(NT):
                P.wait("te", "sv", P.sems["sv"].n)
                for k in range(8):
                    i = k % 2
                    P.wait("gp", "st", v1d[i])
                    ww = P.dma("gp", vwb[i][:], g["wvt"][128 * k:128 * (k + 1), :], sem=vws[i])
                    P.wait("te", vws[i], ww)
                    P.op("te", lambda e, i=i, k=k, stile=stile: e.matmul(
                        pv1[:], xn1_fm[:, k * S + stile * 128: k * S + stile * 128 + 128],
                        vwb[i][:], start=(k == 0), stop=(k == 7)), inc=True)
                    v1d[i] = P.sems["st"].n
                tv = P.sems["st"].n
                P.wait("ve", "st", tv)
                P.op("ve", lambda e, stile=stile: e.tensor_scalar(
                    v1sb[:, stile * 256:(stile + 1) * 256], pv1[:],
                    stats1_rms[:, stile:stile + 1], None, ALU.mult))
                P.fence("ve")
    _pre11 = set(P._dsems)
    _ph11_fn()
    xn1_ctx.__exit__(None, None, None)
    P.barrier("ph11")
    P.release_dsems(set(P._dsems) - _pre11)
    dump("q1", g["q1_d"], (E, S), BF16)
    dump("k1", k1sb[:], (128, 2 * S), BF16)
    dump("v1", v1sb[:], (128, NT * 256), BF16)

    if upto < 12:
        P.build(); return dbg_outs, st

    Hfm_ctx = P.nc.sbuf_tensor("Hfm", [128, NC * S], BF16)
    Hfm = Hfm_ctx.__enter__()
    def _ph12_fn():
        with ExitStack() as ph12:
            sbl = lambda n, s, d=F32: ph12.enter_context(P.nc.sbuf_tensor(n, s, d))
            psl = lambda n, s, d=F32: ph12.enter_context(P.nc.psum_tensor(n, s, d))
            trimask = sbl("p12_trim", [128, 4 * 512], F32)
            binmask = sbl("p12_binm", [128, 4 * 512], BF16)
            M12 = P.dsem("p12m")
            wm1 = P.dma("sy", trimask[:], g["trimask4"], sem=M12)
            wm2 = P.dma("gp", binmask[:], g["binmask4"], sem=M12)
            for _e in ("ve", "se"):
                P.wait(_e, M12, max(wm1, wm2))
            qh = sbl("p12_qh", [128, 512], BF16)
            QH = P.dsem("p12q")
            pt12 = sbl("p12_pt", [128, NT * 512], BF16)
            escr = sbl("p12_escr", [128, 512], BF16)
            pp = [psl(f"p12_pp{i}", [128, 512]) for i in range(2)]
            pO = psl("p12_pO", [64, 512])
            pZ = psl("p12_pZ", [1, 512])
            pMB = psl("p12_pMB", [128, 512])
            tmax = sbl("p12_tmax", [128, 1])
            mrun = sbl("p12_mrun", [128, 1])
            negm = sbl("p12_negm", [1, 1])
            negm2 = sbl("p12_negm2", [1, 1])
            bias_nm = sbl("p12_bias", [128, 1])
            rz = sbl("p12_rz", [1, 512])
            rzbc = sbl("p12_rzbc", [64, 512])
            qhw = 0
            ppv = [0, 0]
            pps = [0, 0]
            hfm_v = 0
            rz_v = 0
            for h in range(16):
                kv = h // 4
                cc, kb = kv // 2, (kv % 2) * 64
                for gq in range(8):
                    nj = 4 * gq + 4
                    P.wait("sy", "st", qhw)
                    wq_ = P.dma("sy", qh[kb:kb + 64, :],
                                g["q1_d"][64 * h:64 * (h + 1), gq * 512:(gq + 1) * 512], sem=QH)
                    P.wait("te", QH, wq_)
                    P.op("ve", lambda e: e.memset(mrun[:], -1e30))
                    P.fence("ve")
                    for which in (1, 2):
                        for j in range(nj):
                            i = j % 2
                            buf = pp[i]
                            P.waitall("te", [("sv", ppv[i]), ("ss", pps[i])])
                            P.op("te", lambda e, b_=buf, j=j, kb=kb, cc=cc: e.matmul(
                                b_[:], k1sb[kb:kb + 64, cc * S + 128 * j: cc * S + 128 * (j + 1)],
                                qh[kb:kb + 64, :], start=True, stop=True), inc=True)
                            tm = P.sems["st"].n
                            if which == 1:
                                P.wait("ve", "st", tm)
                                P.op("ve", lambda e, b_=buf: e.reduce_max(tmax[:], b_[:], AX.X))
                                P.fence("ve")
                                ppv[i] = P.sems["sv"].n
                                P.op("ve", lambda e: e.tensor_max(mrun[:], mrun[:], tmax[:]))
                                P.fence("ve")
                            else:
                                dk = j - 4 * gq
                                vstt = 0
                                if dk >= 0:
                                    P.wait("ve", "st", tm)
                                    vstt = P.op("ve", lambda e, b_=buf, dk=dk: e.scalar_tensor_tensor(
                                        b_[:], trimask[:, 512 * dk:512 * (dk + 1)], 1.0, b_[:],
                                        ALU.mult, ALU.add))
                                    ppv[i] = vstt
                                    P.wait("se", "sv", vstt)
                                P.wait("se", "st", tm)
                                pps[i] = P.op("se", lambda e, b_=buf, j=j: e.activation(
                                    pt12[:, 512 * j:512 * (j + 1)], b_[:],
                                    AF.Exp, bias=bias_nm[:, 0:1], scale=S8))
                        if which == 1:
                            vv = P.sems["sv"].n
                            P.wait("te", "sv", vv)
                            tv = P.op("te", lambda e: e.transpose(pMB[0:1, 0:128], mrun[:], ident[:]))
                            P.wait("ve", "st", tv)
                            P.op("ve", lambda e: e.reduce_max(negm[:], pMB[0:1, 0:128], AX.X, negate=True))
                            P.fence("ve")
                            vv = P.op("ve", lambda e: e.tensor_scalar(negm2[:], negm[:], S8, None, ALU.mult))
                            P.wait("te", "sv", vv)
                            tv = P.op("te", lambda e: e.matmul(pMB[:, 0:1], onesrow[:], negm2[:],
                                                               start=True, stop=True))
                            P.wait("ve", "st", tv)
                            P.op("ve", lambda e: e.tensor_copy(bias_nm[:], pMB[:, 0:1]))
                            P.fence("ve")
                            P.wait("se", "sv", P.sems["sv"].n)
                    sdone = P.sems["ss"].n
                    P.waitall("te", [("ss", sdone), ("sv", hfm_v)])
                    for j in range(nj):
                        P.op("te", lambda e, j=j, kv=kv, nj=nj: e.matmul(
                            pO[:], v1sb[:, j * 256 + kv * 64: j * 256 + kv * 64 + 64],
                            pt12[:, 512 * j:512 * (j + 1)],
                            start=(j == 0), stop=(j == nj - 1)), inc=False)
                    P.wait("te", "sv", rz_v)
                    for j in range(nj):
                        P.op("te", lambda e, j=j, nj=nj: e.matmul(
                            pZ[:], ones128b[:], pt12[:, 512 * j:512 * (j + 1)],
                            start=(j == 0), stop=(j == nj - 1)), inc=(j == nj - 1))
                    tz = P.sems["st"].n
                    P.wait("ve", "st", tz)
                    rz_v = P.op("ve", lambda e: e.reciprocal(rz[:], pZ[:]))
                    P.fence("ve")
                    vv = P.sems["sv"].n
                    P.wait("te", "sv", vv)
                    tv = P.op("te", lambda e: e.matmul(pMB[0:64, :], onesrow[0:1, 0:64], rz[:],
                                                       start=True, stop=True))
                    P.wait("ve", "st", tv)
                    hc, hb = h // 2, (h % 2) * 64
                    P.op("ve", lambda e: e.tensor_copy(rzbc[:], pMB[0:64, :]))
                    P.fence("ve")
                    hfm_v = P.op("ve", lambda e, hc=hc, hb=hb, gq=gq: e.tensor_tensor(
                        Hfm[hb:hb + 64, hc * S + 512 * gq: hc * S + 512 * (gq + 1)],
                        pO[:], rzbc[:], ALU.mult))
                    P.fence("ve")
                    qhw = P.sems["st"].n
    _pre12 = set(P._dsems)
    _ph12_fn()
    P.barrier("ph12")
    P.release_dsems(set(P._dsems) - _pre12)
    dump("hfm", Hfm[:], (128, NC * S), BF16)

    if upto < 13:
        P.build(); return dbg_outs, st

    def _ph13_fn():
        with ExitStack() as ph13:
            sbl = lambda n, s, d=F32: ph13.enter_context(P.nc.sbuf_tensor(n, s, d))
            psl = lambda n, s, d=F32: ph13.enter_context(P.nc.psum_tensor(n, s, d))
            wob = [sbl(f"p13_wo{i}", [128, 1024], BF16) for i in range(2)]
            wos = [P.dsem("p13w0"), P.dsem("p13w1")]
            X13 = [P.dsem("p13x0"), P.dsem("p13x1")]
            Y13 = [P.dsem("p13y0"), P.dsem("p13y1")]
            x1t = [sbl(f"p13_x1{i}", [128, 1024]) for i in range(2)]
            ysb = [sbl(f"p13_y{i}", [128, 1024]) for i in range(2)]
            patt = psl("p13_patt", [128, 1024])
            ysum = sbl("p13_ysum", [128, NT])
            ysq = sbl("p13_ysq", [128, NT])
            yscr = sbl("p13_yscr", [128, 1024])
            wdone = [0, 0]
            xd = [0, 0]
            yd = [0, 0]
            for tile in range(NT):
                i = tile % 2
                P.wait("sy", "sv", xd[i])
                wx = P.dma("sy", x1t[i][:], g["x1_d"][tile * 128:(tile + 1) * 128, :], sem=X13[i])
                P.wait("te", "sv", P.sems["sv"].n)
                for ic in range(8):
                    wi = ic % 2
                    P.wait("gp", "st", wdone[wi])
                    ww = P.dma("gp", wob[wi][:], g["wot1"][128 * ic:128 * (ic + 1), :], sem=wos[wi])
                    P.wait("te", wos[wi], ww)
                    for nn in range(2):
                        P.op("te", lambda e, wi=wi, ic=ic, nn=nn, tile=tile: e.matmul(
                            patt[:, 512 * nn:512 * (nn + 1)],
                            Hfm[:, ic * S + tile * 128: ic * S + tile * 128 + 128],
                            wob[wi][:, 512 * nn:512 * (nn + 1)],
                            start=(ic == 0), stop=(ic == 7)), inc=(nn == 1))
                    wdone[wi] = P.sems["st"].n
                tmm = P.sems["st"].n
                P.waitall("ve", [("st", tmm), (X13[i], wx), (Y13[i], yd[i])])
                P.op("ve", lambda e, i=i, tile=tile: e.tensor_scalar(
                    yscr[:], x1t[i][:], stats1_negm[:, tile:tile + 1], stats1_rstd[:, tile:tile + 1],
                    ALU.add, ALU.mult))
                P.fence("ve")
                P.op("ve", lambda e, i=i: e.tensor_tensor(ysb[i][:], yscr[:], patt[:], ALU.add))
                P.fence("ve")
                P.op("ve", lambda e, i=i, tile=tile: e.reduce_sum(ysum[:, tile:tile + 1], ysb[i][:], AX.X))
                P.fence("ve")
                vv = P.sems["sv"].n
                P.wait("se", "sv", vv)
                sv_ = P.op("se", lambda e, i=i, tile=tile: e.activation(
                    yscr[:], ysb[i][:], AF.Square, accum_out=ysq[:, tile:tile + 1]))
                xd[i] = P.sems["sv"].n
                P.wait("sy", "sv", vv)
                yd[i] = P.dma("sy", g["y_d"][tile * 128:(tile + 1) * 128, :], ysb[i][:], sem=Y13[i])
            P.fence("ve"); P.wait("ve", "ss", P.sems["ss"].n)
            yv = sbl("p13_yv", [128, NT]); yv2 = sbl("p13_yv2", [128, NT])
            P.op("ve", lambda e: e.tensor_scalar(yv[:], ysq[:], 1.0 / 1024, 1e-6, ALU.mult, ALU.add))
            P.fence("ve")
            P.wait("se", "sv", P.sems["sv"].n)
            sv_ = P.op("se", lambda e: e.activation(yv2[:], yv[:], AF.Sqrt))
            P.wait("ve", "ss", sv_)
            P.op("ve", lambda e: e.reciprocal(statsy_rms[:], yv2[:]))
            P.fence("ve")
    _pre13 = set(P._dsems)
    _ph13_fn()
    Hfm_ctx.__exit__(None, None, None)
    v1sb_ctx.__exit__(None, None, None)
    k1sb_ctx.__exit__(None, None, None)
    P.barrier("ph13")
    P.release_dsems(set(P._dsems) - _pre13)
    dump("ydump", g["y_d"], (S, E))

    if upto < 14:
        P.build(); return dbg_outs, st

    cbc = P.sb("cbc14", [128, 4], F32)
    hfm_ctx = P.nc.sbuf_tensor("hfm2", [128, NC * S], BF16)
    h_fm = hfm_ctx.__enter__()
    def _ph14_fn():
        with ExitStack() as ph14:
            sbl = lambda n, s, d=F32: ph14.enter_context(P.nc.sbuf_tensor(n, s, d))
            psl = lambda n, s, d=F32: ph14.enter_context(P.nc.psum_tensor(n, s, d))
            selc = {}
            m1t = sbl("p14_m1", [1, 1]); m2t = sbl("p14_m2", [1, 1])
            msk = sbl("p14_msk", [1, 8]); mi = sbl("p14_mi", [1, 8])
            l2t = sbl("p14_l2", [1, 8])
            i1t = sbl("p14_i1", [1, 1]); i2t = sbl("p14_i2", [1, 1])
            dt_ = sbl("p14_dt", [1, 1]); c1t = sbl("p14_c1", [1, 1]); c2t = sbl("p14_c2", [1, 1])
            sel_sb = sbl("p14_sel", [1, 8])
            for bi, logit in ((0, fclog_sb), (1, pjlog_sb)):
                P.op("ve", lambda e, logit=logit: e.reduce_max(m1t[:], logit[:], AX.X))
                P.fence("ve")
                P.op("ve", lambda e, logit=logit: e.tensor_single_scalar(msk[:], logit[:], m1t[0:1, 0:1], ALU.is_ge))
                P.fence("ve")
                P.op("ve", lambda e: e.tensor_scalar(mi[:], iota8row[:], -100.0, None, ALU.add))
                P.fence("ve")
                P.op("ve", lambda e: e.tensor_tensor(mi[:], mi[:], msk[:], ALU.mult))
                P.fence("ve")
                P.op("ve", lambda e: e.tensor_scalar(mi[:], mi[:], 100.0, None, ALU.add))
                P.fence("ve")
                P.op("ve", lambda e: e.tensor_reduce(i1t[:], mi[:], AX.X, ALU.min))
                P.fence("ve")
                P.op("ve", lambda e: e.tensor_single_scalar(msk[:], iota8row[:], i1t[0:1, 0:1], ALU.is_equal))
                P.fence("ve")
                P.op("ve", lambda e: e.tensor_scalar(msk[:], msk[:], -1e30, None, ALU.mult))
                P.fence("ve")
                P.op("ve", lambda e, logit=logit: e.tensor_tensor(l2t[:], logit[:], msk[:], ALU.add))
                P.fence("ve")
                P.op("ve", lambda e: e.reduce_max(m2t[:], l2t[:], AX.X))
                P.fence("ve")
                P.op("ve", lambda e: e.tensor_single_scalar(msk[:], l2t[:], m2t[0:1, 0:1], ALU.is_ge))
                P.fence("ve")
                P.op("ve", lambda e: e.tensor_scalar(mi[:], iota8row[:], -100.0, None, ALU.add))
                P.fence("ve")
                P.op("ve", lambda e: e.tensor_tensor(mi[:], mi[:], msk[:], ALU.mult))
                P.fence("ve")
                P.op("ve", lambda e: e.tensor_scalar(mi[:], mi[:], 100.0, None, ALU.add))
                P.fence("ve")
                P.op("ve", lambda e: e.tensor_reduce(i2t[:], mi[:], AX.X, ALU.min))
                P.fence("ve")
                P.op("ve", lambda e: e.tensor_single_scalar(dt_[:], m2t[:], m1t[0:1, 0:1], ALU.subtract))
                P.fence("ve")
                P.wait("se", "sv", P.sems["sv"].n)
                sv_ = P.op("se", lambda e: e.activation(dt_[:], dt_[:], AF.Exp))
                P.wait("ve", "ss", sv_)
                P.op("ve", lambda e: e.tensor_scalar(c1t[:], dt_[:], 1.0, None, ALU.add))
                P.fence("ve")
                P.op("ve", lambda e: e.reciprocal(c1t[:], c1t[:]) if False else e.reciprocal(c2t[:], c1t[:]))
                P.fence("ve")
                P.op("ve", lambda e: e.tensor_copy(c1t[:], c2t[:]))
                P.fence("ve")
                P.op("ve", lambda e: e.tensor_tensor(c2t[:], dt_[:], c1t[:], ALU.mult))
                P.fence("ve")
                for col, t in ((0, i1t), (1, i2t), (2, c1t), (3, c2t)):
                    P.op("ve", lambda e, bi=bi, col=col, t=t: e.tensor_copy(
                        sel_sb[0:1, 4 * bi + col:4 * bi + col + 1], t[:]))
                P.fence("ve")
            dump("selsb", sel_sb[:], (1, 8))
            pbc1 = psl("p14_pbc", [128, 1])
            imbc = sbl("p14_imbc", [128, 1])
            idxf = sbl("p14_idxf", [128, 1])
            idxi = sbl("p14_idxi", [128, 1], I32)
            gtmp = sbl("p14_gt", [128, 4096])
            GW = P.dsem("p14g")
            GTO = P.dsem("p14go")
            gw_v = 0
            gto_v = 0

            def gather_rows(src_flat, nrows, bank_rows, dst_d, bi, mem):
                nonlocal gw_v, gto_v
                width = int(src_flat.shape[-1])
                vv = P.sems["sv"].n
                P.wait("te", "sv", vv)
                tv = P.op("te", lambda e, bi=bi, mem=mem: e.matmul(
                    pbc1[:], onesrow[:], sel_sb[0:1, 4 * bi + mem:4 * bi + mem + 1],
                    start=True, stop=True))
                P.wait("ve", "st", tv)
                P.op("ve", lambda e: e.tensor_copy(imbc[:], pbc1[:]))
                P.fence("ve")
                for t in range(nrows // 128):
                    P.op("ve", lambda e, t=t, bank_rows=bank_rows: e.tensor_scalar(
                        idxf[:], imbc[:], float(bank_rows), float(128 * t), ALU.mult, ALU.add))
                    P.fence("ve")
                    P.op("ve", lambda e: e.tensor_tensor(idxf[:], idxf[:], iotaf[:], ALU.add))
                    P.fence("ve")
                    P.op("ve", lambda e: e.tensor_copy(idxi[:], idxf[:]))
                    P.fence("ve")
                    vv = P.sems["sv"].n
                    P.waitall("gp", [("sv", vv), (GW, gw_v), (GTO, gto_v)])
                    gw_v = P.idma(gtmp[:, 0:width], None,
                                  src_flat, IndirectOffsetOnAxis(idxi[:], 0), sem=GW)
                    P.wait("sy", GW, gw_v)
                    gto_v = P.dma("sy", dst_d[128 * t:128 * (t + 1), :], gtmp[:, 0:width], sem=GTO)
                    P.wait("ve", GW, gw_v)
            gather_rows(g["fcs1"].rearrange("a b c -> (a b) c"), 1024, 1024, g["s1sel"].rearrange("m a c -> (m a) c")[0:1024, :], 0, 0)
            gather_rows(g["fcs1"].rearrange("a b c -> (a b) c"), 1024, 1024, g["s1sel"].rearrange("m a c -> (m a) c")[1024:2048, :], 0, 1)
            gather_rows(g["fcs2"].rearrange("a b c -> (a b) c"), 4096, 4096, g["s2sel"].rearrange("m a c -> (m a) c")[0:4096, :], 0, 0)
            gather_rows(g["fcs2"].rearrange("a b c -> (a b) c"), 4096, 4096, g["s2sel"].rearrange("m a c -> (m a) c")[4096:8192, :], 0, 1)
            gather_rows(g["fcmixT"].rearrange("a b c -> (a b) c"), 1024, 1024, g["mixsel"].rearrange("m a c -> (m a) c")[0:1024, :], 0, 0)
            gather_rows(g["fcmixT"].rearrange("a b c -> (a b) c"), 1024, 1024, g["mixsel"].rearrange("m a c -> (m a) c")[1024:2048, :], 0, 1)
            gather_rows(g["pjs1"].rearrange("a b c -> (a b) c"), 4096, 4096, g["pj1sel"].rearrange("m a c -> (m a) c")[0:4096, :], 1, 0)
            gather_rows(g["pjs1"].rearrange("a b c -> (a b) c"), 4096, 4096, g["pj1sel"].rearrange("m a c -> (m a) c")[4096:8192, :], 1, 1)
            gather_rows(g["pjs2"].rearrange("a b c -> (a b) c"), 1024, 1024, g["pj2sel"].rearrange("m a c -> (m a) c")[0:1024, :], 1, 0)
            gather_rows(g["pjs2"].rearrange("a b c -> (a b) c"), 1024, 1024, g["pj2sel"].rearrange("m a c -> (m a) c")[1024:2048, :], 1, 1)
            gather_rows(g["pjmixT"].rearrange("a b c -> (a b) c"), 4096, 4096, g["pjmixsel"].rearrange("m a c -> (m a) c")[0:4096, :], 1, 0)
            gather_rows(g["pjmixT"].rearrange("a b c -> (a b) c"), 4096, 4096, g["pjmixsel"].rearrange("m a c -> (m a) c")[4096:8192, :], 1, 1)
            for col in range(4):
                vv = P.sems["sv"].n
                P.wait("te", "sv", vv)
                tv = P.op("te", lambda e, col=col: e.matmul(
                    pbc1[:], onesrow[:], sel_sb[0:1, 2 + 4 * (col // 2) + col % 2: 3 + 4 * (col // 2) + col % 2],
                    start=True, stop=True))
                P.wait("ve", "st", tv)
                P.op("ve", lambda e, col=col: e.tensor_copy(cbc[:, col:col + 1], pbc1[:]))
                P.fence("ve")
            yt14 = [sbl(f"p14_y{i}", [128, 1024]) for i in range(2)]
            xt14 = [sbl(f"p14_x{i}", [128, 1024]) for i in range(2)]
            at14 = [sbl(f"p14_a{i}", [128, 1024]) for i in range(2)]
            bt14 = [sbl(f"p14_b{i}", [128, 1024]) for i in range(2)]
            ct14 = [sbl(f"p14_ct{i}", [128, 1024]) for i in range(2)]
            ys14 = [P.dsem("p14y0"), P.dsem("p14y1")]
            xs14 = [P.dsem("p14x0"), P.dsem("p14x1")]
            as14 = [P.dsem("p14a0"), P.dsem("p14a1")]
            OS14 = [P.dsem("p14o0"), P.dsem("p14o1")]
            hnb = sbl("p14_hnb", [128, 1024], BF16)
            pt14 = psl("p14_pt", [128, 128], BF16)
            ydn = [0, 0]
            xdn = [0, 0]
            adn = [0, 0]
            odn = [0, 0]
            for tile in range(NT):
                i = tile % 2
                P.wait("sy", "sv", ydn[i])
                wy = P.dma("sy", yt14[i][:], g["y_d"][tile * 128:(tile + 1) * 128, :], sem=ys14[i])
                P.wait("sy", "sv", xdn[i])
                wx = P.dma("sy", xt14[i][:], g["x1_d"][tile * 128:(tile + 1) * 128, :], sem=xs14[i])
                P.wait("sy", "sv", adn[i])
                wa = P.dma("sy", at14[i][:], g["a1_d"][tile * 128:(tile + 1) * 128, :], sem=as14[i])
                P.waitall("ve", [(ys14[i], wy), ("st", P.sems["st"].n)])
                P.op("ve", lambda e, i=i, tile=tile: e.tensor_scalar(
                    hnb[:], yt14[i][:], statsy_rms[:, tile:tile + 1], None, ALU.mult))
                P.fence("ve")
                vv = P.sems["sv"].n
                for k in range(NC):
                    P.wait("te", "sv", vv)
                    tv = P.op("te", lambda e, k=k: e.transpose(pt14[:], hnb[:, 128 * k:128 * (k + 1)], identb[:]))
                    P.wait("ve", "st", tv)
                    vv = P.op("ve", lambda e, k=k, tile=tile: e.tensor_copy(
                        h_fm[:, k * S + tile * 128: k * S + tile * 128 + 128], pt14[:]))
                P.fence("ve")
                P.waitall("ve", [(xs14[i], wx), (as14[i], wa), (OS14[i], odn[i])])
                P.op("ve", lambda e, i=i: e.tensor_tensor(bt14[i][:], yt14[i][:], xt14[i][:], ALU.add))
                P.fence("ve")
                P.op("ve", lambda e, i=i: e.tensor_tensor(ct14[i][:], bt14[i][:], at14[i][:], ALU.add))
                P.fence("ve")
                vv2 = P.sems["sv"].n
                ydn[i] = vv2
                xdn[i] = vv2
                adn[i] = vv2
                P.wait("sy", "sv", vv2)
                odn[i] = P.dma("sy", g["out"][tile * 128:(tile + 1) * 128, :], ct14[i][:], sem=OS14[i])
    _pre14 = set(P._dsems)
    _ph14_fn()
    P.barrier("ph14")
    P.release_dsems(set(P._dsems) - _pre14)

    if upto < 15:
        P.build(); return dbg_outs, st

    def _ph15_fn():
        with ExitStack() as ph15:
            sbl = lambda n, s, d=F32: ph15.enter_context(P.nc.sbuf_tensor(n, s, d))
            psl = lambda n, s, d=F32: ph15.enter_context(P.nc.psum_tensor(n, s, d))
            s1b = sbl("p15_s1", [128, 2 * 8 * 64], BF16)
            s2b = sbl("p15_s2", [128, 2 * 32 * 64], BF16)
            WL15 = P.dsem("p15w")
            w1_ = P.dma("gp", s1b[:].rearrange("p (m k q) -> p (m k) q", m=2, k=8),
                        g["s1sel"].rearrange("m (k p) c -> p (m k) c", p=128), sem=WL15)
            w2_ = P.dma("gp", s2b[:].rearrange("p (m k q) -> p (m k) q", m=2, k=32),
                        g["s2sel"].rearrange("m (k p) c -> p (m k) c", p=128), sem=WL15)
            P.wait("te", WL15, max(w1_, w2_))
            mxslab = sbl("p15_mx", [128, 8 * 4096], BF16)
            MXS = P.dsem("p15mx")
            hb15 = sbl("p15_hb", [128, 8 * 512], BF16)
            mfm = sbl("p15_mfm", [128, 512], BF16)
            fch = [sbl(f"p15_f{i}", [128, 512]) for i in range(2)]
            fch2 = [sbl(f"p15_g{i}", [128, 512]) for i in range(2)]
            ftl = [sbl(f"p15_ft{i}", [128, 512]) for i in range(2)]
            h2b = [sbl(f"p15_h2{i}", [128, 512], BF16) for i in range(2)]
            FW = [P.dsem("p15fw0"), P.dsem("p15fw1")]
            FL = [P.dsem("p15fl0"), P.dsem("p15fl1")]
            HW = [P.dsem("p15hw0"), P.dsem("p15hw1")]
            pb1a = psl("p15_p1a", [64, 512])
            pb1b = psl("p15_p1b", [64, 512])
            pm15 = [psl(f"p15_pm{i}", [128, 512]) for i in range(2)]
            pb2a = [psl(f"p15_p2a{i}", [64, 512]) for i in range(2)]
            pb2b = [psl(f"p15_p2b{i}", [64, 512]) for i in range(2)]
            mfm2 = [mfm, sbl("p15_mfm1", [128, 512], BF16)]
            fwv = [0, 0]
            flv = [0, 0]
            hwv = [0, 0]
            mfv = [0, 0]
            b2v = [0, 0]
            for m in range(2):
                P.barrier(f"p15m{m}")
                wmx = 0
                for k in range(8):
                    wmx = P.dma("gp", mxslab[:, k * 4096:(k + 1) * 4096],
                                g["mixsel"][m, 128 * k:128 * (k + 1), :], sem=MXS)
                P.wait("te", MXS, wmx)
                for toff in range(8):
                    t0 = toff * 512
                    for k in range(8):
                        P.op("te", lambda e, m=m, k=k, t0=t0: e.matmul(
                            pb1a[:], s1b[0:64, (m * 8 + k) * 64:(m * 8 + k) * 64 + 64],
                            h_fm[0:64, k * S + t0: k * S + t0 + 512],
                            start=True, stop=True), inc=False)
                        tv = P.op("te", lambda e, m=m, k=k, t0=t0: e.matmul(
                            pb1b[:], s1b[64:128, (m * 8 + k) * 64:(m * 8 + k) * 64 + 64],
                            h_fm[64:128, k * S + t0: k * S + t0 + 512],
                            start=True, stop=True))
                        P.wait("ve", "st", tv)
                        P.op("ve", lambda e, k=k: e.tensor_copy(
                            hb15[0:64, k * 512:(k + 1) * 512], pb1a[:]), inc=False)
                        vcp = P.op("ve", lambda e, k=k: e.tensor_copy(
                            hb15[64:128, k * 512:(k + 1) * 512], pb1b[:]))
                        P.wait("te", "sv", vcp)
                    for oc in range(32):
                        i = oc % 2
                        flw = 0
                        if m == 1:
                            P.wait("sy", "sv", flv[i])
                            flw = P.dma("sy", ftl[i][:], g["fc_d"][oc, :, t0:t0 + 512], sem=FL[i])
                        P.wait("te", "sv", mfv[i])
                        for k in range(8):
                            P.op("te", lambda e, i=i, k=k, oc=oc: e.matmul(
                                pm15[i][:], mxslab[:, k * 4096 + oc * 128: k * 4096 + oc * 128 + 128],
                                hb15[:, k * 512:(k + 1) * 512],
                                start=(k == 0), stop=(k == 7)), inc=(k == 7))
                        tmm = P.sems["st"].n
                        P.wait("ve", "st", tmm)
                        mfv[i] = P.op("ve", lambda e, i=i: e.tensor_copy(mfm2[i][:], pm15[i][:]))
                        P.fence("ve")
                        P.waitall("te", [("sv", mfv[i]), ("sv", b2v[i])])
                        P.op("te", lambda e, i=i, m=m, oc=oc: e.matmul(
                            pb2a[i][:], s2b[0:64, (m * 32 + oc) * 64:(m * 32 + oc) * 64 + 64],
                            mfm2[i][0:64, :], start=True, stop=True), inc=False)
                        tv2 = P.op("te", lambda e, i=i, m=m, oc=oc: e.matmul(
                            pb2b[i][:], s2b[64:128, (m * 32 + oc) * 64:(m * 32 + oc) * 64 + 64],
                            mfm2[i][64:128, :], start=True, stop=True))
                        P.wait("ve", "st", tv2)
                        if m == 0:
                            P.wait("ve", FW[i], fwv[i])
                            P.op("ve", lambda e, i=i: e.tensor_scalar(
                                fch[i][0:64, :], pb2a[i][:], cbc[0:64, 0:1], None, ALU.mult), inc=False)
                            vsc = P.op("ve", lambda e, i=i: e.tensor_scalar(
                                fch[i][64:128, :], pb2b[i][:], cbc[0:64, 0:1], None, ALU.mult))
                            b2v[i] = vsc
                            P.wait("sy", "sv", vsc)
                            fwv[i] = P.dma("sy", g["fc_d"][oc, :, t0:t0 + 512], fch[i][:], sem=FW[i])
                        else:
                            P.waitall("ve", [(FL[i], flw), (HW[i], hwv[i])])
                            P.op("ve", lambda e, i=i: e.tensor_scalar(
                                fch[i][0:64, :], pb2a[i][:], cbc[0:64, 1:2], None, ALU.mult), inc=False)
                            b2v[i] = P.op("ve", lambda e, i=i: e.tensor_scalar(
                                fch[i][64:128, :], pb2b[i][:], cbc[0:64, 1:2], None, ALU.mult))
                            P.fence("ve")
                            P.op("ve", lambda e, i=i: e.tensor_tensor(
                                fch2[i][:], fch[i][:], ftl[i][:], ALU.add))
                            P.fence("ve")
                            flv[i] = P.sems["sv"].n
                            P.op("ve", lambda e, i=i: e.tensor_scalar(
                                fch[i][:], fch2[i][:], 0.0, None, ALU.max))
                            P.fence("ve")
                            vsq = P.op("ve", lambda e, i=i: e.tensor_tensor(
                                h2b[i][:], fch[i][:], fch[i][:], ALU.mult))
                            P.wait("sy", "sv", vsq)
                            hwv[i] = P.dma("sy", g["h2_d"][oc * 128:(oc + 1) * 128, t0:t0 + 512],
                                           h2b[i][:], sem=HW[i])
    _pre15 = set(P._dsems)
    _ph15_fn()
    hfm_ctx.__exit__(None, None, None)
    P.barrier("ph15")
    P.release_dsems(set(P._dsems) - _pre15)
    dump("h2dump", g["h2_d"], (4 * E, S), BF16)

    if upto < 16:
        P.build(); return dbg_outs, st

    def _ph16_fn():
        with ExitStack() as ph16:
            sbl = lambda n, s, d=F32: ph16.enter_context(P.nc.sbuf_tensor(n, s, d))
            psl = lambda n, s, d=F32: ph16.enter_context(P.nc.psum_tensor(n, s, d))
            s1b6 = sbl("p16_s1", [128, 2 * 32 * 64], BF16)
            s2b6 = sbl("p16_s2", [128, 2 * 8 * 64], BF16)
            WL16 = P.dsem("p16w")
            w1_ = P.dma("gp", s1b6[:].rearrange("p (m k q) -> p (m k) q", m=2, k=32),
                        g["pj1sel"].rearrange("m (k p) c -> p (m k) c", p=128), sem=WL16)
            w2_ = P.dma("gp", s2b6[:].rearrange("p (m k q) -> p (m k) q", m=2, k=8),
                        g["pj2sel"].rearrange("m (k p) c -> p (m k) c", p=128), sem=WL16)
            P.wait("te", WL16, max(w1_, w2_))
            mxs6 = sbl("p16_mx", [128, 32 * 1024], BF16)
            MX6 = P.dsem("p16mx")
            htl = sbl("p16_ht", [128, 32 * 512], BF16)
            HT6 = P.dsem("p16ht")
            hb16 = sbl("p16_hb", [128, 32 * 512], BF16)
            mfm6 = sbl("p16_mfm", [128, 512], BF16)
            pjh = [sbl(f"p16_f{i}", [128, 512]) for i in range(2)]
            pjf = [sbl(f"p16_g{i}", [128, 512]) for i in range(2)]
            ptl = [sbl(f"p16_pt{i}", [128, 512]) for i in range(2)]
            PW6 = [P.dsem("p16pw0"), P.dsem("p16pw1")]
            PL6 = [P.dsem("p16pl0"), P.dsem("p16pl1")]
            obuf = [sbl(f"p16_o{i}", [128, 128]) for i in range(2)]
            osum = [sbl(f"p16_os{i}", [128, 128]) for i in range(2)]
            OL6 = [P.dsem("p16ol0"), P.dsem("p16ol1")]
            OW6 = [P.dsem("p16ow0"), P.dsem("p16ow1")]
            pb1a6 = psl("p16_p1a", [64, 512])
            pb1b6 = psl("p16_p1b", [64, 512])
            _pm16 = psl("p16_pm0", [128, 512])
            pm16 = [_pm16, _pm16]
            pb2a6 = [psl(f"p16_p2a{i}", [64, 512]) for i in range(2)]
            pb2b6 = [psl(f"p16_p2b{i}", [64, 512]) for i in range(2)]
            mfm62 = [mfm6, sbl("p16_mfm1", [128, 512], BF16)]
            ptr16 = psl("p16_ptr", [128, 128])
            pwv = [0, 0]
            plv = [0, 0]
            olv = [0, 0]
            owv = [0, 0]
            mfv6 = [0, 0]
            b2v6 = [0, 0]
            for m in range(2):
                P.barrier(f"p16m{m}")
                wmx = 0
                for k in range(32):
                    wmx = P.dma("gp", mxs6[:, k * 1024:(k + 1) * 1024],
                                g["pjmixsel"][m, 128 * k:128 * (k + 1), :], sem=MX6)
                P.wait("te", MX6, wmx)
                for toff in range(8):
                    t0 = toff * 512
                    P.wait("sy", "st", P.sems["st"].n)
                    wht = 0
                    for k in range(32):
                        wht = P.dma("sy", htl[:, k * 512:(k + 1) * 512],
                                    g["h2_d"][128 * k:128 * (k + 1), t0:t0 + 512], sem=HT6)
                    P.wait("te", HT6, wht)
                    for k in range(32):
                        P.op("te", lambda e, m=m, k=k: e.matmul(
                            pb1a6[:], s1b6[0:64, (m * 32 + k) * 64:(m * 32 + k) * 64 + 64],
                            htl[0:64, k * 512:(k + 1) * 512],
                            start=True, stop=True), inc=False)
                        tv = P.op("te", lambda e, m=m, k=k: e.matmul(
                            pb1b6[:], s1b6[64:128, (m * 32 + k) * 64:(m * 32 + k) * 64 + 64],
                            htl[64:128, k * 512:(k + 1) * 512],
                            start=True, stop=True))
                        P.wait("ve", "st", tv)
                        P.op("ve", lambda e, k=k: e.tensor_copy(
                            hb16[0:64, k * 512:(k + 1) * 512], pb1a6[:]), inc=False)
                        vcp = P.op("ve", lambda e, k=k: e.tensor_copy(
                            hb16[64:128, k * 512:(k + 1) * 512], pb1b6[:]))
                        P.wait("te", "sv", vcp)
                    for oc in range(8):
                        i = oc % 2
                        plw = 0
                        if m == 1:
                            P.wait("sy", "sv", plv[i])
                            plw = P.dma("sy", ptl[i][:], g["pj0_d"][oc, :, t0:t0 + 512], sem=PL6[i])
                        P.wait("te", "sv", max(mfv6))
                        for k in range(32):
                            P.op("te", lambda e, i=i, k=k, oc=oc: e.matmul(
                                pm16[i][:], mxs6[:, k * 1024 + oc * 128: k * 1024 + oc * 128 + 128],
                                hb16[:, k * 512:(k + 1) * 512],
                                start=(k == 0), stop=(k == 31)), inc=(k == 31))
                        tmm = P.sems["st"].n
                        P.wait("ve", "st", tmm)
                        mfv6[i] = P.op("ve", lambda e, i=i: e.tensor_copy(mfm62[i][:], pm16[i][:]))
                        P.fence("ve")
                        P.waitall("te", [("sv", mfv6[i]), ("sv", b2v6[i])])
                        P.op("te", lambda e, i=i, m=m, oc=oc: e.matmul(
                            pb2a6[i][:], s2b6[0:64, (m * 8 + oc) * 64:(m * 8 + oc) * 64 + 64],
                            mfm62[i][0:64, :], start=True, stop=True), inc=False)
                        tv2 = P.op("te", lambda e, i=i, m=m, oc=oc: e.matmul(
                            pb2b6[i][:], s2b6[64:128, (m * 8 + oc) * 64:(m * 8 + oc) * 64 + 64],
                            mfm62[i][64:128, :], start=True, stop=True))
                        P.wait("ve", "st", tv2)
                        if m == 0:
                            P.wait("ve", PW6[i], pwv[i])
                            P.op("ve", lambda e, i=i: e.tensor_scalar(
                                pjh[i][0:64, :], pb2a6[i][:], cbc[0:64, 2:3], None, ALU.mult), inc=False)
                            vsc = P.op("ve", lambda e, i=i: e.tensor_scalar(
                                pjh[i][64:128, :], pb2b6[i][:], cbc[0:64, 2:3], None, ALU.mult))
                            b2v6[i] = vsc
                            P.wait("sy", "sv", vsc)
                            pwv[i] = P.dma("sy", g["pj0_d"][oc, :, t0:t0 + 512], pjh[i][:], sem=PW6[i])
                        else:
                            P.wait("ve", PL6[i], plw)
                            P.op("ve", lambda e, i=i: e.tensor_scalar(
                                pjh[i][0:64, :], pb2a6[i][:], cbc[0:64, 3:4], None, ALU.mult), inc=False)
                            b2v6[i] = P.op("ve", lambda e, i=i: e.tensor_scalar(
                                pjh[i][64:128, :], pb2b6[i][:], cbc[0:64, 3:4], None, ALU.mult))
                            P.fence("ve")
                            P.op("ve", lambda e, i=i: e.tensor_tensor(
                                pjf[i][:], pjh[i][:], ptl[i][:], ALU.add))
                            P.fence("ve")
                            plv[i] = P.sems["sv"].n
                            for q4 in range(4):
                                j = q4 % 2
                                vvw = P.sems["sv"].n
                                P.wait("te", "sv", vvw)
                                tv3 = P.op("te", lambda e, i=i, q4=q4: e.transpose(
                                    ptr16[:], pjf[i][:, q4 * 128:(q4 + 1) * 128], ident[:]))
                                P.wait("sy", "sv", olv[j])
                                olw = P.dma("sy", obuf[j][:],
                                            g["out"][t0 + q4 * 128: t0 + (q4 + 1) * 128,
                                                     oc * 128:(oc + 1) * 128], sem=OL6[j])
                                P.waitall("ve", [("st", tv3), (OL6[j], olw), (OW6[j], owv[j])])
                                vos = P.op("ve", lambda e, j=j: e.tensor_tensor(
                                    osum[j][:], obuf[j][:], ptr16[:], ALU.add))
                                olv[j] = vos
                                P.wait("sy", "sv", vos)
                                owv[j] = P.dma("sy", g["out"][t0 + q4 * 128: t0 + (q4 + 1) * 128,
                                                              oc * 128:(oc + 1) * 128],
                                               osum[j][:], sem=OW6[j])
                                P.wait("te", "sv", vos)
    _ph16_fn()
    P.barrier("ph16")
    P.build()
    return dbg_outs, st




def kernel(**inputs):
    in_maps = [host_prep(inputs, b) for b in range(B)]
    nc = bass.Bass("TRN2", target_bir_lowering=False)
    build_core(nc, 0)
    res = run_bass_kernel_spmd(nc, in_maps, core_ids=list(range(B)))
    out = np.stack([res.results[i]["out"] for i in range(B)], axis=0)
    return out.astype(np.float32)


# version 05 (final)
# speedup vs baseline: 1.3499x; 1.3499x over previous
import os
import numpy as np
from contextlib import ExitStack

import concourse.bass as bass
import concourse.mybir as mybir
from concourse.bass_utils import run_bass_kernel_spmd
from concourse.bass import IndirectOffsetOnAxis

F32 = mybir.dt.float32
BF16 = mybir.dt.bfloat16
I32 = mybir.dt.int32
AF = mybir.ActivationFunctionType
ALU = mybir.AluOpType
AX = mybir.AxisListType

B, S, E = 8, 4096, 1024
NT = S // 128
NC = E // 128
CAP = 640
REC = 1028
SQ8 = float(1.0 / np.sqrt(8.0))
S32 = float(1.0 / 32.0)
S8 = float(1.0 / 8.0)
GELU_C = 0.7978845608028654
GELU_A = 0.044715


class Sem:
    def __init__(self, h):
        self.h = h
        self.n = 0

    def inc(self, k=1):
        self.n += k
        return self.n


class Prog:

    ENGS = ("te", "ve", "se", "sy", "gp")

    def __init__(self, nc, stack):
        self.nc = nc
        self.stack = stack
        self.q = {e: [] for e in self.ENGS}
        self.sems = {}
        for name in ("st", "sv", "ss", "sg", "qsy", "qgp", "qse"):
            h = stack.enter_context(nc.semaphore(name))
            self.sems[name] = Sem(h)
        self._dsems = {}
        self._dsem_pool = []
        self._dsem_n = 0

    def dsem(self, name):
        if name not in self._dsems:
            if self._dsem_pool:
                self._dsems[name] = self._dsem_pool.pop()
            else:
                h = self.stack.enter_context(self.nc.semaphore(f"d{self._dsem_n}"))
                self._dsem_n += 1
                self._dsems[name] = Sem(h)
        return self._dsems[name]

    def release_dsems(self, names):
        for n in list(names):
            s = self._dsems.pop(n, None)
            if s is not None:
                self._dsem_pool.append(s)

    def sb(self, name, shape, dtype):
        return self.stack.enter_context(self.nc.sbuf_tensor(name, shape, dtype))

    def ps(self, name, shape, dtype=F32):
        return self.stack.enter_context(self.nc.psum_tensor(name, shape, dtype))

    def _eq(self, eng):
        return self.q[eng]

    def wait(self, eng, sem, v):
        if v <= 0:
            return
        h = sem.h if isinstance(sem, Sem) else self.sems[sem].h
        self.q[eng].append(lambda e, s=h, v=v: e.wait_ge(s, v))

    def waitall(self, eng, pairs):
        for sem, v in pairs:
            self.wait(eng, sem, v)

    def dma(self, eng, out, in_, sem=None, **kw):
        s_ = sem if sem is not None else self.sems[{"sy": "qsy", "gp": "qgp", "se": "qse"}[eng]]
        v = s_.inc(16)
        self.q[eng].append(
            lambda e, o=out, i=in_, s=s_.h, kw=dict(kw): e.dma_start(out=o, in_=i, **kw).then_inc(s, 16))
        return v

    def idma(self, out, out_off, in_, in_off, sem=None, **kw):
        s_ = sem if sem is not None else self.sems["qgp"]
        v = s_.inc(16)
        def _emit(e, o=out, oo=out_off, i=in_, io=in_off, s=s_.h, kw=dict(kw)):
            try:
                return e.indirect_dma_start(out=o, out_offset=oo, in_=i, in_offset=io, **kw).then_inc(s, 16)
            except Exception:
                import sys
                print(f"IDMA FAIL: out={o} out_off={oo.ap if oo else None} in_={i} "
                      f"in_off={io.ap if io else None} kw={kw}", file=sys.stderr)
                raise
        self.q["gp"].append(_emit)
        return v

    def op(self, eng, fn, inc=True):
        sem = {"te": "st", "ve": "sv", "se": "ss", "gp": "sg"}[eng]
        if inc:
            v = self.sems[sem].inc(1)
            self.q[eng].append(lambda e, f=fn, s=self.sems[sem].h: f(e).then_inc(s, 1))
            return v
        else:
            self.q[eng].append(lambda e, f=fn: f(e))
            return self.sems[sem].n

    def raw(self, eng, fn):
        self.q[eng].append(fn)

    def fence(self, eng):
        sem = {"te": "st", "ve": "sv", "se": "ss", "gp": "sg"}[eng]
        v = self.sems[sem].n
        self.wait(eng, sem, v)

    def barrier(self, tag=""):
        vals = [(s, self.sems[s].n) for s in ("st", "sv", "ss", "sg", "qsy", "qgp", "qse")]
        dvals = [(s, s.n) for s in self._dsems.values()]
        for eng in self.ENGS:
            for sem, v in vals:
                self.wait(eng, sem, v)
            for sem, v in dvals:
                self.wait(eng, sem, v)

    def build(self):
        nc = self.nc
        with nc.Block() as block:
            @block.tensor
            def _(e):
                for f in self.q["te"]:
                    f(e)

            @block.vector
            def _(e):
                for f in self.q["ve"]:
                    f(e)

            @block.scalar
            def _(e):
                for f in self.q["se"]:
                    f(e)

            @block.sync
            def _(e):
                for f in self.q["sy"]:
                    f(e)

            @block.gpsimd
            def _(e):
                for f in self.q["gp"]:
                    f(e)



def host_prep(inputs, b):
    f32 = lambda x: np.ascontiguousarray(x, dtype=np.float32)
    x0 = np.asarray(inputs["L0_attn_emb"], dtype=np.float32)
    m = {}
    m["x0_tok"] = f32(x0[b])
    m["x0T_own"] = f32(x0[b].T)
    xf = np.roll(x0.reshape(B * S, E), -b * S, axis=0)
    m["x0T_full"] = f32(xf.T)
    m["x0_full"] = f32(xf)

    W1, W2, W3, Wo = (np.asarray(inputs[k], np.float32) for k in
                      ("L0_attn_W1", "L0_attn_W2", "L0_attn_W3", "L0_attn_Wo"))
    perm = np.empty(128, np.int64)
    perm[:64] = np.arange(64) * 2
    perm[64:] = np.arange(64) * 2 + 1
    full_perm = (np.arange(E) // 128)[:, None] * 128
    full_perm = (full_perm + perm[None, :]).reshape(-1)
    full_perm = np.concatenate([c * 128 + perm for c in range(NC)])
    m["w1t"] = f32(W1.T[:, full_perm])
    m["w2t"] = f32(W2.T[:, full_perm])
    m["w3t"] = f32(W3.T)
    m["wot"] = f32(Wo.T)
    gt = np.asarray(inputs["L0_ffn_gate_w"], np.float32).T
    m["gatet"] = f32(gt.reshape(8, 128, 8).transpose(1, 0, 2).reshape(128, 64))

    inv = 10000.0 ** (-np.arange(0, 128, 2, dtype=np.float64) / 128.0)
    v_of = np.empty(4096, np.int64)
    for c in range(8):
        for s_in in range(512):
            v_of[c * 512 + s_in] = 8 * s_in + c
    ang = inv[:, None] * v_of[None, :].astype(np.float64)
    m["cosr"] = f32(np.cos(ang))
    m["sinr"] = f32(np.sin(ang))

    inv2 = 10000.0 ** (-np.arange(0, 64, 2, dtype=np.float64) / 64.0)
    ang2 = inv2[:, None] * np.arange(4096, dtype=np.float64)[None, :]
    m["cos2"] = f32(np.concatenate([np.cos(ang2), np.cos(ang2)], 0))
    m["sin2"] = f32(np.concatenate([np.sin(ang2), np.sin(ang2)], 0))

    m["lnw"] = f32(np.asarray(inputs["L0_ffn_ln_w"]).reshape(1, 1024))
    m["lnb"] = f32(np.asarray(inputs["L0_ffn_ln_b"]).reshape(1, 1024))
    m["fc1sT"] = f32(np.transpose(np.asarray(inputs["L0_ffn_fc1s"]), (0, 2, 1)))
    m["b1s"] = f32(np.asarray(inputs["L0_ffn_b1s"]).reshape(8, 32, 128).transpose(0, 2, 1))
    m["fc2sT"] = f32(np.transpose(np.asarray(inputs["L0_ffn_fc2s"]), (0, 2, 1)))
    m["b2s"] = f32(np.asarray(inputs["L0_ffn_b2s"]).reshape(8, 8, 128).transpose(0, 2, 1))

    for k in ("L1_attn_F", "L1_attn_Q", "L1_attn_K", "L1_attn_V"):
        m[k.lower()] = f32(np.asarray(inputs[k]))

    perm1 = np.concatenate([np.arange(0, 32), np.arange(64, 96),
                            np.arange(32, 64), np.arange(96, 128)])
    wq_t = np.asarray(inputs["L1_ffn_wq"], np.float32).T
    wk_t = np.asarray(inputs["L1_ffn_wk"], np.float32).T
    pq8 = np.concatenate([c * 128 + perm1 for c in range(8)])
    pq2 = np.concatenate([c * 128 + perm1 for c in range(2)])
    m["wqt"] = f32(wq_t[:, pq8])
    m["wkt"] = f32(wk_t[:, pq2])
    m["wvt"] = f32(np.asarray(inputs["L1_ffn_wv"], np.float32).T)
    m["wot1"] = f32(np.asarray(inputs["L1_ffn_wo"], np.float32).T)
    m["qgain"] = f32(np.asarray(inputs["L1_ffn_q_gain"]).reshape(1, 16))
    m["kgain"] = f32(np.asarray(inputs["L1_ffn_k_gain"]).reshape(1, 4))

    def mix_fold(mix, n_in):
        out = np.empty_like(mix)
        in_f = mix.shape[2]
        j = np.arange(in_f)
        for k in range(8):
            shift = k % n_in
            pi_inv = (j % 64) * n_in + ((j // 64 + shift) % n_in)
            out[k] = mix[k][:, pi_inv]
        return out
    fc_mix = np.asarray(inputs["L1_ffn_fc_mix"], np.float32)
    pj_mix = np.asarray(inputs["L1_ffn_pj_mix"], np.float32)
    m["fcmixT"] = f32(np.transpose(mix_fold(fc_mix, 16), (0, 2, 1)))
    m["pjmixT"] = f32(np.transpose(mix_fold(pj_mix, 64), (0, 2, 1)))
    m["fcs1"] = f32(np.asarray(inputs["L1_ffn_fc_s1"]).reshape(8, 1024, 64))
    m["fcs2"] = f32(np.asarray(inputs["L1_ffn_fc_s2"]).reshape(8, 4096, 64))
    m["pjs1"] = f32(np.asarray(inputs["L1_ffn_pj_s1"]).reshape(8, 4096, 64))
    m["pjs2"] = f32(np.asarray(inputs["L1_ffn_pj_s2"]).reshape(8, 1024, 64))
    m["fclog"] = f32(np.asarray(inputs["L1_ffn_fc_logits"]).reshape(1, 8))
    m["pjlog"] = f32(np.asarray(inputs["L1_ffn_pj_logits"]).reshape(1, 8))

    m["ident"] = f32(np.eye(128))
    m["identb"] = np.eye(128).astype(np.float32)
    m["lt128"] = f32(np.tril(np.ones((128, 128)), -1))
    m["ones128"] = f32(np.ones((128, 1)))
    m["onesrow"] = f32(np.ones((1, 128)))
    m["iota128"] = np.arange(128, dtype=np.int32).reshape(128, 1)
    m["iotaf"] = f32(np.arange(128).reshape(128, 1))
    m["iota8row"] = f32(np.arange(8).reshape(1, 8))
    r_ = np.arange(128)
    hs = np.zeros((128, 2), np.float32)
    hs[((r_ % 64) < 32), 0] = 1.0
    hs[((r_ % 64) >= 32), 1] = 1.0
    m["hsel"] = hs
    m["ind128x8"] = f32((np.arange(128)[:, None] // 16 == np.arange(8)[None, :]))
    m["repl8x128"] = f32((np.arange(8)[:, None] == np.arange(128)[None, :] // 16))
    tm = np.zeros((4, 128, 512), np.float32)
    for s_ in range(4):
        tt = np.arange(128)[:, None] + 128 * s_
        qq = np.arange(512)[None, :]
        tm[s_] = np.where(tt <= qq, 0.0, -1e30)
    m["trimask4"] = f32(tm.transpose(1, 0, 2).reshape(128, 4 * 512))
    m["binmask4"] = f32((tm.transpose(1, 0, 2).reshape(128, 4 * 512) == 0.0).astype(np.float32))
    return m



def build_core(nc, b, upto=99, dbg=(), start=0):
    st = ExitStack()
    P = Prog(nc, st)
    g = {}

    def din(name, shape, dtype=F32):
        g[name] = nc.dram_tensor(name, list(shape), dtype, kind="ExternalInput").ap()
        return g[name]

    def dint(name, shape, dtype=F32):
        g[name] = nc.dram_tensor(name, list(shape), dtype, kind="Internal").ap()
        return g[name]

    def dout(name, shape, dtype=F32):
        g[name] = nc.dram_tensor(name, list(shape), dtype, kind="ExternalOutput").ap()
        return g[name]

    din("x0_tok", (S, E)); din("x0T_own", (E, S)); din("x0T_full", (E, B * S))
    din("x0_full", (B * S, E))
    din("w1t", (E, E)); din("w2t", (E, E)); din("w3t", (E, E)); din("wot", (E, E))
    din("gatet", (128, 64))
    din("cosr", (64, S)); din("sinr", (64, S)); din("cos2", (64, S)); din("sin2", (64, S))
    din("hsel", (128, 2))
    din("lnw", (1, E)); din("lnb", (1, E))
    din("fc1sT", (8, E, 4 * E)); din("b1s", (8, 128, 32))
    din("fc2sT", (8, 4 * E, E)); din("b2s", (8, 128, 8))
    din("l1_attn_f", (E, E)); din("l1_attn_q", (E, E)); din("l1_attn_k", (E, E)); din("l1_attn_v", (E, E))
    din("wqt", (E, E)); din("wkt", (E, 256)); din("wvt", (E, 256)); din("wot1", (E, E))
    din("qgain", (1, 16)); din("kgain", (1, 4))
    din("fcmixT", (8, E, 4 * E)); din("pjmixT", (8, 4 * E, E))
    din("fcs1", (8, E, 64)); din("fcs2", (8, 4 * E, 64))
    din("pjs1", (8, 4 * E, 64)); din("pjs2", (8, E, 64))
    din("fclog", (1, 8)); din("pjlog", (1, 8))
    din("ident", (128, 128)); din("identb", (128, 128)); din("lt128", (128, 128))
    din("ones128", (128, 1)); din("onesrow", (1, 128))
    din("iota128", (128, 1), I32); din("iotaf", (128, 1)); din("iota8row", (1, 8))
    din("ind128x8", (128, 8)); din("repl8x128", (8, 128))
    din("trimask4", (128, 2048)); din("binmask4", (128, 2048))

    dint("probs_d", (8, B * S))
    dint("stats_d", (2, B * S))
    dint("own_sd", (2, S))
    dint("scr8", (1, 8))
    dint("meta_d", (8 * CAP + 128, 1))
    dint("vl0_d", (S, E), BF16)
    dint("qv_d", (128, 8 * S), BF16)
    dint("kv_d", (128, 8 * S), BF16)
    dint("o2_d", (S, E), BF16)
    if start >= 8:
        din("x1_d", (S, E))
    elif upto < 8:
        dout("x1_d", (S, E))
    else:
        dint("x1_d", (S, E))
    dint("sd1_d", (3, S))
    dint("xf_d", (E, S), BF16)
    dint("qx_d", (E, S), BF16)
    dint("kx_d", (E, S), BF16)
    dint("vx_d", (S, E), BF16)
    dint("y_d", (S, E))
    dint("a1_d", (S, E))
    dint("h2_d", (4 * E, S), BF16)
    dint("q1_d", (E, S), BF16)
    dint("k1_d", (256, S), BF16)
    dint("v1_d", (S, 256), BF16)
    dint("s1sel", (2, E, 64)); dint("s2sel", (2, 4 * E, 64)); dint("mixsel", (2, E, 4 * E))
    dint("pj1sel", (2, 4 * E, 64)); dint("pj2sel", (2, E, 64)); dint("pjmixsel", (2, 4 * E, E))
    dint("fc_d", (32, 128, S)); dint("pj0_d", (8, 128, S))
    out = dout("out", (S, E))

    dbg_outs = []

    def dump(name, ap_or_tile, shape, dtype=F32, src_sbuf=None, waits=()):
        if name not in dbg:
            return
        o = dout("dbg_" + name, shape, dtype)
        P.waitall("sy", waits)
        P.dma("sy", o, ap_or_tile)
        dbg_outs.append("dbg_" + name)

    ident = P.sb("c_ident", [128, 128], F32)
    identb = P.sb("c_identb", [128, 128], BF16)
    lt128 = P.sb("c_lt", [128, 128], F32)
    ones128 = P.sb("c_ones", [128, 1], F32)
    ones128b = P.sb("c_onesb", [128, 1], BF16)
    onesrow = P.sb("c_onesrow", [1, 128], F32)
    iota128 = P.sb("c_iota", [128, 1], I32)
    iotaf = P.sb("c_iotaf", [128, 1], F32)
    iota8row = P.sb("c_iota8", [1, 8], F32)
    ind128x8 = P.sb("c_ind", [128, 8], F32)
    repl8x128 = P.sb("c_repl", [8, 128], F32)
    gatet_sb = P.sb("c_gatet", [128, 8 * 8], F32)
    lnw_sb = P.sb("c_lnw", [1, E], F32)
    lnb_sb = P.sb("c_lnb", [1, E], F32)
    hsel_sb = P.sb("c_hsel", [128, 2], F32)
    qgain_sb = P.sb("c_qgain", [1, 16], F32)
    kgain_sb = P.sb("c_kgain", [1, 4], F32)
    fclog_sb = P.sb("c_fclog", [1, 8], F32)
    pjlog_sb = P.sb("c_pjlog", [1, 8], F32)

    CSEM = P.dsem("c")
    ld0 = []
    for t, src in ((ident, g["ident"]), (lt128, g["lt128"]), (ones128, g["ones128"]),
                   (onesrow, g["onesrow"]), (iota128, g["iota128"]), (iotaf, g["iotaf"]),
                   (iota8row, g["iota8row"]), (ind128x8, g["ind128x8"]), (repl8x128, g["repl8x128"]),
                   (lnw_sb, g["lnw"]), (lnb_sb, g["lnb"]), (qgain_sb, g["qgain"]),
                   (hsel_sb, g["hsel"]),
                   (kgain_sb, g["kgain"]), (fclog_sb, g["fclog"]), (pjlog_sb, g["pjlog"])):
        ld0.append(P.dma("sy", t[:], src, sem=CSEM))
    ld0.append(P.dma("gp", identb[:], g["identb"], sem=CSEM))
    ld0.append(P.dma("gp", ones128b[:], g["ones128"], sem=CSEM))
    ld0.append(P.dma("sy", gatet_sb[:], g["gatet"], sem=CSEM))
    for _eng in ("ve", "te", "se", "sy", "gp"):
        P.wait(_eng, CSEM, max(ld0))

    negm_own = P.sb("negm_own", [128, NT], F32)
    rstd_own = P.sb("rstd_own", [128, NT], F32)
    stats1_negm = P.sb("s1_negm", [128, NT], F32)
    stats1_rstd = P.sb("s1_rstd", [128, NT], F32)
    stats1_rms = P.sb("s1_rms", [128, NT], F32)
    statsy_rms = P.sb("sy_rms", [128, NT], F32)

    def emit_stats(src_dram, dst_negm, dst_rstd, dst_rms=None, eps=1e-5, rms_eps=1e-6,
                   pre_wait=()):
        sums = P.sb("st_sums", [128, NT], F32)
        sq = P.sb("st_sq", [128, NT], F32)
        xb = [P.sb(f"st_xb{i}", [128, E], F32) for i in range(3)]
        xsem = [P.dsem(f"stx{i}") for i in range(3)]
        scr = P.sb("st_scr", [128, E], F32)
        vdone = [0] * 3
        sdone = [0] * 3
        P.waitall("sy", pre_wait)
        for f in range(NT):
            i = f % 3
            P.waitall("sy", [("sv", vdone[i]), ("ss", sdone[i])])
            w = P.dma("sy", xb[i][:], src_dram[f * 128:(f + 1) * 128, :], sem=xsem[i])
            P.wait("se", xsem[i], w)
            sdone[i] = P.op("se", lambda e, x=xb[i], c=f: e.activation(
                scr[:], x[:], AF.Square, accum_out=sq[:, c:c + 1]))
            P.wait("ve", xsem[i], w)
            vdone[i] = P.op("ve", lambda e, x=xb[i], c=f: e.reduce_sum(
                sums[:, c:c + 1], x[:], AX.X))
        P.fence("ve")
        P.wait("ve", "ss", P.sems["ss"].n)
        mean = P.sb("st_mean", [128, NT], F32)
        var = P.sb("st_var", [128, NT], F32)
        msq = P.sb("st_msq", [128, NT], F32)
        P.op("ve", lambda e: e.tensor_scalar(mean[:], sums[:], 1.0 / E, None, ALU.mult))
        P.fence("ve")
        P.op("ve", lambda e: e.tensor_tensor(msq[:], mean[:], mean[:], ALU.mult))
        P.fence("ve")
        P.op("ve", lambda e: e.scalar_tensor_tensor(var[:], sq[:], 1.0 / E, msq[:],
                                                    ALU.mult, ALU.subtract))
        P.fence("ve")
        P.op("ve", lambda e: e.tensor_scalar(dst_negm[:], mean[:], -1.0, None, ALU.mult))
        if dst_rms is not None:
            vp = P.sb("st_vp", [128, NT], F32)
            vpa = P.sb("st_vpa", [128, NT], F32)
            vpb = P.sb("st_vpb", [128, NT], F32)
            P.op("ve", lambda e: e.tensor_scalar(vpa[:], var[:], eps, None, ALU.add))
            P.fence("ve")
            P.op("ve", lambda e: e.reciprocal(vpa[:] if False else vpb[:], vpa[:]))
            P.fence("ve")
            P.op("ve", lambda e: e.tensor_tensor(vpb[:], var[:], vpb[:], ALU.mult)) if False else None
            P.op("ve", lambda e: e.tensor_tensor(vpa[:], var[:], vpb[:], ALU.mult))
            P.fence("ve")
            P.op("ve", lambda e: e.tensor_scalar(vp[:], vpa[:], rms_eps, None, ALU.add))
            P.fence("ve")
            P.wait("se", "sv", P.sems["sv"].n)
            vp2 = P.sb("st_vp2", [128, NT], F32)
            sv = P.op("se", lambda e: e.activation(vp2[:], vp[:], AF.Sqrt))
            P.wait("ve", "ss", sv)
            P.op("ve", lambda e: e.reciprocal(dst_rms[:], vp2[:]))
        vare = P.sb("st_vare", [128, NT], F32)
        P.op("ve", lambda e: e.tensor_scalar(vare[:], var[:], eps, None, ALU.add))
        P.fence("ve")
        P.wait("se", "sv", P.sems["sv"].n)
        var2 = P.sb("st_var2", [128, NT], F32)
        sv = P.op("se", lambda e: e.activation(var2[:], vare[:], AF.Sqrt))
        P.wait("ve", "ss", sv)
        P.op("ve", lambda e: e.reciprocal(dst_rstd[:], var2[:]))
        P.fence("ve")

    def _l0():
        with ExitStack() as ph1:
            _sb = P.sb; P.sb = lambda n, s, d, _st=ph1: _st.enter_context(P.nc.sbuf_tensor(n, s, d))
            emit_stats(g["x0_tok"], negm_own, rstd_own)
            P.sb = _sb
        P.barrier("ph1")
        dump("negm_own", negm_own[:], (128, NT))
        dump("rstd_own", rstd_own[:], (128, NT))

        if upto < 2:
            P.build(); return True

        with ExitStack() as ph2:
            sbl = lambda n, s, d: ph2.enter_context(P.nc.sbuf_tensor(n, s, d))
            psl = lambda n, s: ph2.enter_context(P.nc.psum_tensor(n, s, F32))
            smG = sbl("smG", [128, 256], F32)
            sqG = sbl("sqG", [128, 256], F32)
            xst = [sbl(f"xst{i}", [128, E], F32) for i in range(3)]
            gsem = [P.dsem(f"g2x{i}") for i in range(3)]
            scr = sbl("g_scr", [128, E], F32)
            vdone = [0] * 3
            sdone = [0] * 3
            for f in range(256):
                i = f % 3
                P.waitall("sy", [("sv", vdone[i]), ("ss", sdone[i])])
                src = g["x0_full"][f::256, :]
                w = P.dma("sy", xst[i][:], src, sem=gsem[i])
                P.wait("se", gsem[i], w)
                sdone[i] = P.op("se", lambda e, x=xst[i], c=f: e.activation(
                    scr[:], x[:], AF.Square, accum_out=sqG[:, c:c + 1]))
                P.wait("ve", gsem[i], w)
                vdone[i] = P.op("ve", lambda e, x=xst[i], c=f: e.reduce_sum(
                    smG[:, c:c + 1], x[:], AX.X))
            P.fence("ve"); P.wait("ve", "ss", P.sems["ss"].n)
            meanG = sbl("meanG", [128, 256], F32)
            varG = sbl("varG", [128, 256], F32)
            aG = sbl("aG", [128, 256], F32)
            bG = sbl("bG", [128, 256], F32)
            msqG = sbl("msqG", [128, 256], F32)
            varGe = sbl("varGe", [128, 256], F32)
            P.op("ve", lambda e: e.tensor_scalar(meanG[:], smG[:], 1.0 / E, None, ALU.mult))
            P.fence("ve")
            P.op("ve", lambda e: e.tensor_tensor(msqG[:], meanG[:], meanG[:], ALU.mult))
            P.fence("ve")
            P.op("ve", lambda e: e.scalar_tensor_tensor(varG[:], sqG[:], 1.0 / E, msqG[:],
                                                        ALU.mult, ALU.subtract))
            P.fence("ve")
            P.op("ve", lambda e: e.tensor_scalar(varGe[:], varG[:], 1e-5, None, ALU.add))
            P.fence("ve")
            P.wait("se", "sv", P.sems["sv"].n)
            varG2 = sbl("varG2", [128, 256], F32)
            sv = P.op("se", lambda e: e.activation(varG2[:], varGe[:], AF.Sqrt))
            P.wait("ve", "ss", sv)
            P.op("ve", lambda e: e.reciprocal(aG[:], varG2[:]))
            P.fence("ve")
            P.op("ve", lambda e: e.tensor_tensor(bG[:], meanG[:], aG[:], ALU.mult))
            vb = P.fence("ve") or P.sems["sv"].n
            SDSEM = P.dsem("sd")
            P.wait("sy", "sv", P.sems["sv"].n)
            w1 = P.dma("sy", g["stats_d"][0:1, :].rearrange("o (p f) -> (o p) f", p=128), aG[:], sem=SDSEM)
            w2 = P.dma("sy", g["stats_d"][1:2, :].rearrange("o (p f) -> (o p) f", p=128), bG[:], sem=SDSEM)

            psG = psl("psG", [1, 8])
            for k in range(8):
                P.op("te", lambda e, k=k: e.matmul(psG[:], ones128[:], gatet_sb[:, 8 * k:8 * k + 8],
                                                   start=(k == 0), stop=(k == 7)), inc=(k == 7))
            P.fence("te")
            G_sb = sbl("G_sb", [1, 8], F32)
            tv = P.sems["st"].n
            P.wait("ve", "st", tv)
            vv = P.op("ve", lambda e: e.tensor_copy(G_sb[:], psG[:]))
            GBSEM = P.dsem("gb")
            P.wait("sy", "sv", vv)
            wv = P.dma("sy", g["scr8"], G_sb[:], sem=GBSEM)
            Gcol = sbl("Gcol", [8, 1], F32)
            P.wait("sy", GBSEM, wv)
            wv2 = P.dma("sy", Gcol[:], g["scr8"].rearrange("o (n x) -> (o n) x", x=1), sem=GBSEM)

            arow = sbl("arow", [1, 2048], F32)
            brow = sbl("brow", [1, 2048], F32)
            xg = [sbl(f"xg{i}", [128, 8 * 512], F32) for i in range(2)]
            psA = psl("psA", [8, 512]); psB = psl("psB", [8, 512])
            psraw = psl("psraw", [8, 512]); psZ = psl("psZ", [1, 512]); psL = psl("psL", [8, 512])
            A_sb = sbl("A_sb", [8, 512], F32); B_sb = sbl("B_sb", [8, 512], F32)
            l_sb = sbl("l_sb", [8, 512], F32); t2_sb = sbl("t2_sb", [8, 512], F32)
            l2_sb = sbl("l2_sb", [8, 512], F32)
            E_sb = sbl("E_sb", [8, 512], F32); lse_sb = sbl("lse_sb", [1, 512], F32)
            s_sb = sbl("s_sb", [8, 512], F32); pr_sb = sbl("pr_sb", [8, 512], F32)
            P.wait("sy", GBSEM, wv2)
            P.wait("sy", SDSEM, max(w1, w2))
            P.wait("te", GBSEM, wv2)
            ABSEM = P.dsem("ab")
            XGS = [P.dsem("xg0"), P.dsem("xg1")]
            PWSEM = P.dsem("pw")
            prev_pr_w = 0
            xg_done = [0, 0]
            for Q in range(16):
                P.wait("sy", "st", P.sems["st"].n)
                wa = P.dma("sy", arow[:], g["stats_d"][0:1, Q * 2048:(Q + 1) * 2048], sem=ABSEM)
                wb = P.dma("sy", brow[:], g["stats_d"][1:2, Q * 2048:(Q + 1) * 2048], sem=ABSEM)
                for tt in range(4):
                    i = tt % 2
                    off = Q * 2048 + tt * 512
                    P.wait("sy", "st", xg_done[i])
                    wx = 0
                    for k in range(8):
                        wx = P.dma("sy", xg[i][:, 512 * k:512 * (k + 1)],
                                   g["x0T_full"][128 * k:128 * (k + 1), off:off + 512], sem=XGS[i])
                    P.waitall("te", [(ABSEM, max(wa, wb))])
                    P.op("te", lambda e, tt=tt: e.matmul(psA[:], onesrow[0:1, 0:8],
                                                         arow[0:1, tt * 512:(tt + 1) * 512],
                                                         start=True, stop=True), inc=False)
                    P.op("te", lambda e, tt=tt: e.matmul(psB[:], onesrow[0:1, 0:8],
                                                         brow[0:1, tt * 512:(tt + 1) * 512],
                                                         start=True, stop=True), inc=False)
                    P.wait("te", XGS[i], wx)
                    P.wait("te", "sv", prev_pr_w)
                    for k in range(8):
                        P.op("te", lambda e, i=i, k=k: e.matmul(
                            psraw[:], gatet_sb[:, 8 * k:8 * k + 8], xg[i][:, 512 * k:512 * (k + 1)],
                            start=(k == 0), stop=(k == 7)), inc=(k == 7))
                    xg_done[i] = P.sems["st"].n
                    tm = P.sems["st"].n
                    P.wait("ve", "st", tm)
                    P.op("ve", lambda e: e.tensor_copy(A_sb[:], psA[:]), inc=False)
                    P.op("ve", lambda e: e.tensor_copy(B_sb[:], psB[:]))
                    P.fence("ve")
                    P.op("ve", lambda e: e.tensor_tensor(l_sb[:], psraw[:], A_sb[:], ALU.mult), inc=False)
                    P.op("ve", lambda e: e.tensor_scalar(t2_sb[:], B_sb[:], Gcol[:, 0:1], None, ALU.mult))
                    P.fence("ve")
                    P.op("ve", lambda e: e.tensor_tensor(l2_sb[:], l_sb[:], t2_sb[:], ALU.subtract))
                    vv = P.fence("ve") or P.sems["sv"].n
                    P.wait("se", "sv", P.sems["sv"].n)
                    sv1 = P.op("se", lambda e: e.activation(E_sb[:], l2_sb[:], AF.Exp))
                    P.wait("te", "ss", sv1)
                    tz = P.op("te", lambda e: e.matmul(psZ[:], ones128[0:8, 0:1], E_sb[:],
                                                       start=True, stop=True))
                    P.wait("se", "st", tz)
                    sv2 = P.op("se", lambda e: e.activation(lse_sb[:], psZ[:], AF.Ln))
                    P.wait("te", "ss", sv2)
                    tl = P.op("te", lambda e: e.matmul(psL[:], onesrow[0:1, 0:8], lse_sb[:],
                                                       start=True, stop=True))
                    P.wait("ve", "st", tl)
                    P.op("ve", lambda e: e.tensor_tensor(s_sb[:], l2_sb[:], psL[:], ALU.subtract))
                    vv = P.fence("ve") or P.sems["sv"].n
                    P.wait("se", "sv", P.sems["sv"].n)
                    sv3 = P.op("se", lambda e: e.activation(pr_sb[:], s_sb[:], AF.Exp))
                    P.wait("sy", "ss", sv3)
                    prw = P.dma("sy", g["probs_d"][:, off:off + 512], pr_sb[:], sem=PWSEM)
                    P.wait("ve", PWSEM, prw)
                    prev_pr_w = P.sems["sv"].n
        P.barrier("ph2")
        dump("probs", g["probs_d"], (8, B * S))
        dump("statsd", g["stats_d"], (2, B * S))

        if upto < 3:
            P.build(); return True


        dint = None
        with ExitStack() as phx:
            sbl = lambda n, s, d=F32: phx.enter_context(P.nc.sbuf_tensor(n, s, d))
            psl = lambda n, s: phx.enter_context(P.nc.psum_tensor(n, s, F32))
            pT = psl("pT2b", [128, 128])
            stT = sbl("stT2b", [128, 128])
            B2SEM = P.dsem("b2b")
            for row, tile in ((0, negm_own), (1, rstd_own)):
                P.wait("te", "sv", P.sems["sv"].n)
                tv = P.op("te", lambda e, t=tile: e.transpose(pT[0:NT, 0:128], t[:], ident[:]))
                P.waitall("ve", [("st", tv), (B2SEM, B2SEM.n)])
                vv = P.op("ve", lambda e: e.tensor_copy(stT[0:NT, :], pT[0:NT, 0:128]))
                P.wait("sy", "sv", vv)
                P.dma("sy", g["own_sd"][row:row + 1, :].rearrange("o (f p) -> (o f) p", p=128),
                      stT[0:NT, :], sem=B2SEM)
        P.barrier("ph2b")
        dump("ownsd", g["own_sd"], (2, S))

        moe_sb = ExitStack()
        RSSM = P.dsem("rssm")
        MZS = P.dsem("mzs")
        slots_v = 0
        mz_final = 0
        slots_i = None
        ttab = None
        if upto >= 7:
            sblm = lambda n, s, d=F32: moe_sb.enter_context(P.nc.sbuf_tensor(n, s, d))
            slots_i = sblm("p6_slots", [128, 8 * 32], I32)
            ttab = sblm("p6_ttab", [128, 32])
            ztm = sblm("p6_ztm", [128, 41])
            with ExitStack() as ph6a_st:
                sbl = lambda n, s, d=F32: ph6a_st.enter_context(P.nc.sbuf_tensor(n, s, d))
                psla = lambda n, s: ph6a_st.enter_context(P.nc.psum_tensor(n, s, F32))
                zmv = P.op("ve", lambda e: e.memset(ztm[:], 0.0))
                P.wait("se", "sv", zmv)
                mz_final = P.dma("se", g["meta_d"].rearrange("(p f) c -> p (f c)", p=128),
                                 ztm[:], sem=MZS)
                pb = sbl("p6_pb", [128, 2048])
                P6P = P.dsem("p6p")
                wpb = P.dma("sy", pb[:], g["probs_d"].rearrange("e (sub q) -> (e sub) q", sub=16), sem=P6P)
                lo = [sbl("p6_lo0", [8, 1]), sbl("p6_lo1", [8, 1])]
                hi = [sbl("p6_hi0", [8, 1]), sbl("p6_hi1", [8, 1])]
                mid = sbl("p6_mid", [8, 1])
                mid2 = sbl("p6_mid2", [8, 1])
                thr = sbl("p6_thr", [128, 1])
                cmp_ = sbl("p6_cmp", [128, 2048])
                part = sbl("p6_part", [128, 1])
                sel = sbl("p6_sel", [8, 1])
                tq = sbl("p6_tq", [8, 1])
                pbc = psla("p6_pbc", [128, 1])
                pcnt = psla("p6_pcnt", [8, 1])
                P.op("ve", lambda e: e.memset(lo[0][:], 0.0), inc=False)
                P.op("ve", lambda e: e.memset(hi[0][:], 1.0))
                P.fence("ve")
                P.wait("ve", P6P, wpb)
                cur = 0
                for it in range(34):
                    nxt = 1 - cur
                    P.op("ve", lambda e, c=cur: e.tensor_tensor(mid[:], lo[c][:], hi[c][:], ALU.add))
                    P.fence("ve")
                    vv = P.op("ve", lambda e: e.tensor_scalar(mid2[:], mid[:], 0.5, None, ALU.mult))
                    P.wait("te", "sv", vv)
                    tv = P.op("te", lambda e: e.matmul(pbc[:], repl8x128[:], mid2[:], start=True, stop=True))
                    P.wait("ve", "st", tv)
                    P.op("ve", lambda e: e.tensor_copy(thr[:], pbc[:]))
                    P.fence("ve")
                    P.op("ve", lambda e: e.tensor_scalar(cmp_[:], pb[:], thr[:, 0:1], None, ALU.is_gt))
                    P.fence("ve")
                    vv = P.op("ve", lambda e: e.reduce_sum(part[:], cmp_[:], AX.X))
                    P.wait("te", "sv", vv)
                    tv = P.op("te", lambda e: e.matmul(pcnt[:], ind128x8[:], part[:], start=True, stop=True))
                    P.wait("ve", "st", tv)
                    P.op("ve", lambda e: e.tensor_scalar(sel[:], pcnt[:], 4096.0, None, ALU.is_ge))
                    P.fence("ve")
                    P.op("ve", lambda e, c=cur: e.tensor_tensor(tq[:], mid2[:], lo[c][:], ALU.subtract))
                    P.fence("ve")
                    P.op("ve", lambda e: e.tensor_tensor(tq[:], tq[:], sel[:], ALU.mult))
                    P.fence("ve")
                    P.op("ve", lambda e, c=cur, n=nxt: e.tensor_tensor(lo[n][:], lo[c][:], tq[:], ALU.add))
                    P.fence("ve")
                    P.op("ve", lambda e, c=cur: e.tensor_tensor(tq[:], hi[c][:], mid2[:], ALU.subtract))
                    P.fence("ve")
                    P.op("ve", lambda e: e.tensor_tensor(tq[:], tq[:], sel[:], ALU.mult))
                    P.fence("ve")
                    P.op("ve", lambda e, n=nxt: e.tensor_tensor(hi[n][:], mid2[:], tq[:], ALU.add))
                    P.fence("ve")
                    cur = nxt
                vv = P.sems["sv"].n
                P.wait("sy", "sv", vv)
                P6B = P.dsem("p6b")
                wb1 = P.dma("sy", g["scr8"].rearrange("o (n x) -> (o n) x", n=8), lo[cur][:], sem=P6B)
                loT = sbl("p6_loT", [1, 8])
                P.wait("sy", P6B, wb1)
                wb2 = P.dma("sy", loT[:], g["scr8"], sem=P6B)
                P.wait("te", P6B, wb2)
                pbc8 = psla("p6_pbc8", [128, 8])
                tv = P.op("te", lambda e: e.matmul(pbc8[:], onesrow[:], loT[:], start=True, stop=True))
                lo_bc = sbl("p6_lobc", [128, 8])
                P.wait("ve", "st", tv)
                P.op("ve", lambda e: e.tensor_copy(lo_bc[:], pbc8[:]))
                P.fence("ve")
                dump("lo_bc", lo_bc[:], (128, 8), waits=[("sv", P.sems["sv"].n)])

                wq_all = sbl("p6_wq", [128, 8 * 32])
                mask = sbl("p6_mask", [128, 32])
                scn = sbl("p6_scn", [128, 32])
                rexc = sbl("p6_rexc", [128, 32])
                rowtot = sbl("p6_rt", [128, 1])
                pcol = psla("p6_pcol", [128, 1])
                slf = sbl("p6_slf", [128, 32])
                part6 = sbl("p6_part6", [128, 1])
                P6W = P.dsem("p6w")
                for e_ in range(8):
                    P.wait("sy", "sv", P.sems["sv"].n)
                    wqe = P.dma("sy", wq_all[:, 32 * e_:32 * (e_ + 1)],
                                g["probs_d"][e_:e_ + 1, 0:4096].rearrange("o (p f) -> (o p) f", p=128),
                                sem=P6W)
                    P.wait("ve", P6W, wqe)
                    P.op("ve", lambda e, e_=e_: e.tensor_scalar(
                        mask[:], wq_all[:, 32 * e_:32 * (e_ + 1)], lo_bc[:, e_:e_ + 1], None, ALU.is_gt))
                    P.fence("ve")
                    P.op("ve", lambda e: e.tensor_tensor_scan(scn[:], mask[:], mask[:], 0.0, ALU.add, ALU.bypass), inc=False)
                    P.op("ve", lambda e: e.reduce_sum(rowtot[:], mask[:], AX.X))
                    P.fence("ve")
                    P.op("ve", lambda e: e.tensor_tensor(rexc[:], scn[:], mask[:], ALU.subtract), inc=False)
                    vv = P.sems["sv"].n + 1
                    P.op("ve", lambda e: e.tensor_copy(part6[:], rowtot[:]))
                    P.wait("te", "sv", vv)
                    tv = P.op("te", lambda e: e.matmul(pcol[:], lt128[:], part6[:], start=True, stop=True))
                    P.wait("ve", "st", tv)
                    P.op("ve", lambda e: e.tensor_scalar(slf[:], rexc[:], pcol[:, 0:1], None, ALU.add))
                    P.fence("ve")
                    P.op("ve", lambda e, e_=e_: e.tensor_scalar(
                        slf[:], slf[:], 1.0, float(CAP * e_ - 8 * CAP), ALU.mult, ALU.add))
                    P.fence("ve")
                    P.op("ve", lambda e: e.tensor_tensor(slf[:], slf[:], mask[:], ALU.mult))
                    P.fence("ve")
                    P.op("ve", lambda e: e.tensor_scalar(slf[:], slf[:], float(8 * CAP),
                                                         float(8 * CAP + 127), ALU.add, ALU.min))
                    P.fence("ve")
                    P.op("ve", lambda e, e_=e_: e.tensor_copy(slots_i[:, 32 * e_:32 * (e_ + 1)], slf[:]))
                    P.fence("ve")
                dump("slots", slots_i[:], (128, 256), I32, waits=[("sv", P.sems["sv"].n)])
                for f in range(32):
                    P.op("ve", lambda e, f=f: e.tensor_scalar(
                        ttab[:, f:f + 1], iotaf[:], 32.0, float(f + 1), ALU.mult, ALU.add))
                slots_v = P.sems["sv"].n

        xn_ctx = P.nc.sbuf_tensor("xn_fm", [128, NC * S], BF16)
        xn_fm = xn_ctx.__enter__()
        with ExitStack() as ph3:
            sbl = lambda n, s, d=F32: ph3.enter_context(P.nc.sbuf_tensor(n, s, d))
            psl = lambda n, s: ph3.enter_context(P.nc.psum_tensor(n, s, F32))
            nrow = sbl("p3_nrow", [1, 512])
            rrow = sbl("p3_rrow", [1, 512])
            pA = psl("p3_pA", [128, 512])
            pB = psl("p3_pB", [128, 512])
            xc = [sbl(f"p3_xc{i}", [128, 512]) for i in range(2)]
            xcs = [P.dsem("p3x0"), P.dsem("p3x1")]
            RSEM = P.dsem("p3r")
            tmp = sbl("p3_tmp", [128, 512])
            xcd = [0, 0]
            for tt in range(8):
                P.wait("sy", "st", P.sems["st"].n)
                wn = P.dma("sy", nrow[:], g["own_sd"][0:1, tt * 512:(tt + 1) * 512], sem=RSEM)
                wr = P.dma("sy", rrow[:], g["own_sd"][1:2, tt * 512:(tt + 1) * 512], sem=RSEM)
                P.wait("te", RSEM, max(wn, wr))
                P.wait("te", "sv", P.sems["sv"].n)
                P.op("te", lambda e: e.matmul(pA[:], onesrow[:], nrow[:], start=True, stop=True), inc=False)
                tb = P.op("te", lambda e: e.matmul(pB[:], onesrow[:], rrow[:], start=True, stop=True))
                for c in range(NC):
                    i = c % 2
                    P.wait("sy", "sv", xcd[i])
                    wx = P.dma("sy", xc[i][:], g["x0T_own"][128 * c:128 * (c + 1), tt * 512:(tt + 1) * 512],
                               sem=xcs[i])
                    P.waitall("ve", [(xcs[i], wx), ("st", tb)])
                    P.op("ve", lambda e, i=i: e.tensor_tensor(tmp[:], xc[i][:], pA[:], ALU.add))
                    P.fence("ve")
                    xcd[i] = P.op("ve", lambda e, i=i, c=c, tt=tt: e.tensor_tensor(
                        xn_fm[:, c * S + tt * 512: c * S + tt * 512 + 512], tmp[:], pB[:], ALU.mult))
        P.barrier("ph3")
        dump("xnfm", xn_fm[:], (128, NC * S), BF16)

        if upto < 4:
            P.build(); return True

        with ExitStack() as ph4:
            sbl = lambda n, s, d=F32: ph4.enter_context(P.nc.sbuf_tensor(n, s, d))
            psl = lambda n, s: ph4.enter_context(P.nc.psum_tensor(n, s, F32))
            cosr = sbl("p4_cosr", [64, S])
            sinr = sbl("p4_sinr", [64, S])
            P4C = P.dsem("p4c")
            wc = P.dma("sy", cosr[:], g["cosr"], sem=P4C)
            ws_ = P.dma("sy", sinr[:], g["sinr"], sem=P4C)
            P.wait("ve", P4C, max(wc, ws_))
            pq = psl("p4_pq", [128, 2048])
            t1 = sbl("p4_t1", [64, 512]); t2 = sbl("p4_t2", [64, 512])
            t3 = sbl("p4_t3", [64, 512]); t4 = sbl("p4_t4", [64, 512])
            wbuf = [sbl(f"p4_w{i}", [128, 1024], BF16) for i in range(2)]
            wbs = [P.dsem("p4w0"), P.dsem("p4w1")]
            qst = [sbl(f"p4_qst{i}", [128, 512], BF16) for i in range(2)]
            qss = [P.dsem("p4q0"), P.dsem("p4q1")]
            qstd = [0, 0]
            wdone = [0, 0]

            def proj_qk(wt_name, dest_dram):
                for c in range(NC):
                    i = c % 2
                    P.wait("gp", "st", wdone[i])
                    ww = P.dma("gp", wbuf[i][:].rearrange("p (k q) -> p k q", k=8),
                               g[wt_name][:, 128 * c:128 * (c + 1)].rearrange("(k p) q -> p k q", p=128),
                               sem=wbs[i])
                    P.wait("te", wbs[i], ww)
                    for half in range(2):
                        P.wait("te", "sv", P.sems["sv"].n)
                        for k in range(8):
                            for nt in range(4):
                                P.op("te", lambda e, i=i, k=k, half=half, nt=nt: e.matmul(
                                    pq[:, 512 * nt:512 * (nt + 1)],
                                    wbuf[i][:, 128 * k:128 * (k + 1)],
                                    xn_fm[:, k * S + half * 2048 + 512 * nt:
                                          k * S + half * 2048 + 512 * nt + 512],
                                    start=(k == 0), stop=(k == 7)),
                                    inc=(k == 7 and nt == 3))
                        wdone[i] = P.sems["st"].n
                        tmm = P.sems["st"].n
                        P.wait("ve", "st", tmm)
                        for hh in range(4):
                            h = 4 * half + hh
                            csl = cosr[:, 512 * c:512 * (c + 1)]
                            ssl = sinr[:, 512 * c:512 * (c + 1)]
                            pe_ = pq[0:64, 512 * hh:512 * (hh + 1)]
                            po_ = pq[64:128, 512 * hh:512 * (hh + 1)]
                            P.op("ve", lambda e, pe_=pe_, csl=csl: e.tensor_tensor(t1[:], pe_, csl, ALU.mult), inc=False)
                            P.op("ve", lambda e, po_=po_, ssl=ssl: e.tensor_tensor(t2[:], po_, ssl, ALU.mult), inc=False)
                            P.op("ve", lambda e, po_=po_, csl=csl: e.tensor_tensor(t3[:], po_, csl, ALU.mult), inc=False)
                            P.op("ve", lambda e, pe_=pe_, ssl=ssl: e.tensor_tensor(t4[:], pe_, ssl, ALU.mult))
                            P.fence("ve")
                            si = (c * 8 + h) % 2
                            P.wait("ve", qss[si], qstd[si])
                            P.op("ve", lambda e, si=si: e.tensor_tensor(qst[si][0:64, :], t1[:], t2[:], ALU.subtract), inc=False)
                            vv = P.op("ve", lambda e, si=si: e.tensor_tensor(qst[si][64:128, :], t3[:], t4[:], ALU.add))
                            P.wait("sy", "sv", vv)
                            qstd[si] = P.dma("sy", dest_dram[:, h * S + c * 512: h * S + c * 512 + 512],
                                             qst[si][:], sem=qss[si])
            proj_qk("w1t", g["qv_d"])
            proj_qk("w2t", g["kv_d"])
            w3b = [sbl(f"p4_w3{i}", [128, 1024], BF16) for i in range(2)]
            w3s = [P.dsem("p4v0"), P.dsem("p4v1")]
            VW = P.dsem("p4vw")
            pv = psl("p4_pv", [128, 1024])
            vbf = sbl("p4_vbf", [128, 1024], BF16)
            w3d = [0, 0]
            for stile in range(NT):
                P.wait("te", "sv", P.sems["sv"].n)
                for k in range(8):
                    i = k % 2
                    P.wait("gp", "st", w3d[i])
                    ww = P.dma("gp", w3b[i][:], g["w3t"][128 * k:128 * (k + 1), :], sem=w3s[i])
                    P.wait("te", w3s[i], ww)
                    for nn in range(2):
                        P.op("te", lambda e, i=i, k=k, nn=nn, stile=stile: e.matmul(
                            pv[:, 512 * nn:512 * (nn + 1)],
                            xn_fm[:, k * S + stile * 128: k * S + stile * 128 + 128],
                            w3b[i][:, 512 * nn:512 * (nn + 1)],
                            start=(k == 0), stop=(k == 7)), inc=(nn == 1))
                    w3d[i] = P.sems["st"].n
                tv = P.sems["st"].n
                P.wait("ve", "st", tv)
                vv = P.op("ve", lambda e: e.tensor_copy(vbf[:], pv[:]))
                P.wait("sy", "sv", vv)
                wv = P.dma("sy", g["vl0_d"][stile * 128:(stile + 1) * 128, :], vbf[:], sem=VW)
                P.wait("ve", VW, wv)
        xn_ctx.__exit__(None, None, None)
        P.barrier("ph4")
        dump("q_v", g["qv_d"], (128, 8 * S), BF16)
        dump("k_v", g["kv_d"], (128, 8 * S), BF16)
        dump("vl0", g["vl0_d"], (S, E), BF16)

        rss_final = 0
        if upto >= 7:
            P.wait("gp", MZS, mz_final)
            P.wait("gp", "sv", slots_v)
            _breg = [None]
            def _mk_breg(e):
                _breg[0] = e.to_reg(8 * CAP - 1)
            P.raw("gp", _mk_breg)
            for e_ in range(8):
                for f in range(32):
                    rss_final = RSSM.inc(16)
                    def _sc(e, oo=IndirectOffsetOnAxis(slots_i[:, 32 * e_ + f:32 * e_ + f + 1], 0),
                            ii=ttab[:, f:f + 1], s=RSSM.h):
                        return e.indirect_dma_start(
                            out=g["meta_d"], out_offset=oo, in_=ii, in_offset=None,
                            bounds_check=_breg[0], oob_is_err=False).then_inc(s, 16)
                    P.raw("gp", _sc)

        if upto < 5:
            P.build(); return True

        with ExitStack() as ph5:
            sbl = lambda n, s, d=F32: ph5.enter_context(P.nc.sbuf_tensor(n, s, d))
            psl = lambda n, s, d=F32: ph5.enter_context(P.nc.psum_tensor(n, s, d))
            v_tok = sbl("p5_vtok", [128, NT * 128], BF16)
            q_h = sbl("p5_qh", [128, S], BF16)
            k_h = sbl("p5_kh", [128, S], BF16)
            pt = sbl("p5_pt", [128, NT * 512], BF16)
            pp = [psl(f"p5_pp{i}", [128, 1024]) for i in range(2)]
            pO = psl("p5_pO", [128, 512])
            pZ = psl("p5_pZ", [1, 512])
            pMB = psl("p5_pMB", [128, 512])
            pTb = psl("p5_pTb", [128, 128], BF16)
            tmax = sbl("p5_tmax", [128, 1])
            mrun = sbl("p5_mrun", [128, 1])
            negm = sbl("p5_negm", [1, 1])
            negm2 = sbl("p5_negm2", [1, 1])
            bias_nm = sbl("p5_bias", [128, 1])
            rz = sbl("p5_rz", [1, 512])
            rzbc = sbl("p5_rzbc", [128, 512])
            obf = sbl("p5_obf", [128, 512], BF16)
            o2st = sbl("p5_o2st", [128, 512], BF16)
            vload = 0

            P1N = int(os.environ.get("P1N", "16"))

            def st_tiles(h, gq, which, on_tile):
                for jp in range(P1N):
                    i = jp % 2
                    buf = pp[i]
                    if which == 1:
                        P.wait("te", "sv", P.sems["sv"].n)
                    else:
                        P.wait("te", "ss", P.sems["ss"].n)
                    for u in range(2):
                        j = 2 * jp + u
                        c2, sb2 = j // 4, j % 4
                        ksl = k_h[:, c2 * 512 + sb2 * 128: c2 * 512 + sb2 * 128 + 128]
                        qsl = q_h[:, gq * 512: gq * 512 + 512]
                        P.op("te", lambda e, b_=buf, u=u, a=ksl, qq=qsl: e.matmul(
                            b_[:, 512 * u:512 * (u + 1)], a, qq, start=True, stop=True), inc=True)
                    on_tile(jp, buf)

            H5 = int(os.environ.get("H5", "8"))
            G5 = int(os.environ.get("G5", "8"))
            ST5 = int(os.environ.get("ST5", "9"))
            P5L = P.dsem("p5l")
            for h in range(H5):
                P.wait("gp", "st", vload)
                P.wait("sy", "st", vload)
                wv = P.dma("gp", v_tok[:].rearrange("p (c sb d) -> p c sb d", c=8, sb=4),
                           g["vl0_d"][512 * h:512 * (h + 1), :]
                           .rearrange("(sb sl) (c d) -> sl c sb d", sl=128, c=8), sem=P5L)
                wq_ = P.dma("sy", q_h[:], g["qv_d"][:, h * S:(h + 1) * S], sem=P5L)
                wk_ = P.dma("sy", k_h[:], g["kv_d"][:, h * S:(h + 1) * S], sem=P5L)
                P.wait("te", P5L, max(wv, wq_, wk_))
                for gq in range(G5):
                    P.op("ve", lambda e: e.memset(mrun[:], -1e30), inc=True)
                    P.fence("ve")

                    def p1(jp, buf):
                        tm = P.sems["st"].n
                        P.wait("ve", "st", tm)
                        P.op("ve", lambda e, b_=buf: e.reduce_max(tmax[:], b_[:], AX.X))
                        P.fence("ve")
                        P.op("ve", lambda e: e.tensor_max(mrun[:], mrun[:], tmax[:]))
                        P.fence("ve")
                    st_tiles(h, gq, 1, p1)
                    if ST5 < 2:
                        continue
                    vv = P.sems["sv"].n
                    P.wait("te", "sv", vv)
                    tv = P.op("te", lambda e: e.transpose(pMB[0:1, 0:128], mrun[:], ident[:]))
                    P.wait("ve", "st", tv)
                    P.op("ve", lambda e: e.reduce_max(negm[:], pMB[0:1, 0:128], AX.X, negate=True))
                    P.fence("ve")
                    vv = P.op("ve", lambda e: e.tensor_scalar(negm2[:], negm[:], SQ8, None, ALU.mult))
                    P.wait("te", "sv", vv)
                    tv = P.op("te", lambda e: e.matmul(pMB[:, 0:1], onesrow[:], negm2[:], start=True, stop=True))
                    P.wait("ve", "st", tv)
                    P.op("ve", lambda e: e.tensor_copy(bias_nm[:], pMB[:, 0:1]))
                    P.fence("ve")
                    P.wait("se", "sv", P.sems["sv"].n)

                    if ST5 < 3:
                        continue
                    EXP5 = int(os.environ.get("EXP5", "1"))

                    def p2(jp, buf):
                        tm = P.sems["st"].n
                        P.wait("se", "st", tm)
                        for u in range(2):
                            j = 2 * jp + u
                            if EXP5 == 1:
                                P.op("se", lambda e, b_=buf, u=u, j=j: e.activation(
                                    pt[:, 512 * j:512 * (j + 1)], b_[:, 512 * u:512 * (u + 1)],
                                    AF.Exp, bias=bias_nm[:, 0:1], scale=SQ8))
                            elif EXP5 == 2:
                                P.op("se", lambda e, b_=buf, u=u, j=j: e.activation(
                                    pt[:, 512 * j:512 * (j + 1)], b_[:, 512 * u:512 * (u + 1)],
                                    AF.Exp, bias=-20.0, scale=SQ8))
                            elif EXP5 == 3:
                                P.op("se", lambda e, b_=buf, u=u, j=j: e.activation(
                                    pt[:, 512 * j:512 * (j + 1)], b_[:, 512 * u:512 * (u + 1)],
                                    AF.Identity, bias=bias_nm[:, 0:1], scale=SQ8))
                    st_tiles(h, gq, 2, p2)
                    sdone = P.sems["ss"].n
                    if ST5 < 4:
                        continue
                    P.wait("te", "ss", sdone)
                    P.wait("te", "sv", P.sems["sv"].n)
                    for j in range(NT):
                        P.op("te", lambda e, j=j: e.matmul(
                            pO[:], v_tok[:, j * 128:(j + 1) * 128],
                            pt[:, 512 * j:512 * (j + 1)], start=(j == 0), stop=(j == NT - 1)),
                            inc=False)
                    for j in range(NT):
                        P.op("te", lambda e, j=j: e.matmul(
                            pZ[:], ones128b[:], pt[:, 512 * j:512 * (j + 1)],
                            start=(j == 0), stop=(j == NT - 1)), inc=(j == NT - 1))
                    tav = P.sems["st"].n
                    if ST5 < 5:
                        continue
                    P.wait("ve", "st", tav)
                    P.op("ve", lambda e: e.reciprocal(rz[:], pZ[:]))
                    vv = P.fence("ve") or P.sems["sv"].n
                    P.wait("te", "sv", P.sems["sv"].n)
                    tv = P.op("te", lambda e: e.matmul(pMB[:], onesrow[:], rz[:], start=True, stop=True))
                    P.wait("ve", "st", tv)
                    P.op("ve", lambda e: e.tensor_copy(rzbc[:], pMB[:]))
                    P.fence("ve")
                    P.op("ve", lambda e: e.tensor_tensor(obf[:], pO[:], rzbc[:], ALU.mult))
                    vv = P.fence("ve") or P.sems["sv"].n
                    if ST5 < 6:
                        continue
                    for t in range(4):
                        P.wait("te", "sv", P.sems["sv"].n)
                        tv = P.op("te", lambda e, t=t: e.transpose(
                            pTb[:], obf[:, 128 * t:128 * (t + 1)], identb[:]))
                        P.wait("ve", "st", tv)
                        P.op("ve", lambda e, t=t: e.tensor_copy(o2st[:, 128 * t:128 * (t + 1)], pTb[:]))
                    vv = P.fence("ve") or P.sems["sv"].n
                    if ST5 < 7:
                        continue
                    P.wait("sy", "sv", P.sems["sv"].n)
                    wo_ = P.dma("sy",
                                g["o2_d"].rearrange("(t sl gg) (hh d) -> sl gg hh t d",
                                                    t=4, sl=128, gg=8, hh=8)[:, gq, h, :, :],
                                o2st[:].rearrange("p (t d) -> p t d", t=4), sem=P.dsem("p5o"))
                    P.wait("ve", P.dsem("p5o"), wo_)
                vload = P.sems["st"].n
        P.barrier("ph5")
        dump("o2", g["o2_d"], (S, E), BF16)

        if upto < 6:
            P.build(); return True

        with ExitStack() as ph7:
            sbl = lambda n, s, d=F32: ph7.enter_context(P.nc.sbuf_tensor(n, s, d))
            psl = lambda n, s, d=F32: ph7.enter_context(P.nc.psum_tensor(n, s, d))
            wotb = sbl("p7_wot", [128, 8 * 1024], BF16)
            P7W = P.dsem("p7w"); P7O = P.dsem("p7o")
            P7X = [P.dsem("p7x0"), P.dsem("p7x1")]
            P7B = [P.dsem("p7b0"), P.dsem("p7b1")]
            wwo = P.dma("gp", wotb[:].rearrange("p (k q) -> p k q", k=8),
                        g["wot"].rearrange("(k p) q -> p k q", p=128), sem=P7W)
            o2t = [sbl(f"p7_o2{i}", [128, 1024], BF16) for i in range(8)]
            x0t = [sbl(f"p7_x0{i}", [128, 1024]) for i in range(2)]
            pA0 = psl("p7_pA0", [128, 1024])
            xnt = sbl("p7_xn", [128, 1024])
            x1sb = [sbl(f"p7_x1{i}", [128, 1024]) for i in range(2)]
            P.wait("te", P7W, wwo)
            x0d = [0, 0]
            x1d = [0, 0]
            te_prev_cblk = 0
            for cblk in range(4):
                P.wait("sy", "st", te_prev_cblk)
                o2d = [P.dma("sy", o2t[jc][:],
                             g["o2_d"][1024 * cblk + 128 * jc: 1024 * cblk + 128 * (jc + 1), :], sem=P7O)
                       for jc in range(8)]
                wo2 = max(o2d)
                for ut in range(8):
                    tile = 8 * cblk + ut
                    i = tile % 2
                    P.wait("sy", "sv", x0d[i])
                    wx = P.dma("sy", x0t[i][:], g["x0_tok"][tile * 128:(tile + 1) * 128, :], sem=P7B[i])
                    P.waitall("te", [(P7O, wo2), ("sv", P.sems["sv"].n)])
                    for jc in range(8):
                        for nn in range(2):
                            P.op("te", lambda e, jc=jc, nn=nn, ut=ut: e.matmul(
                                pA0[:, 512 * nn:512 * (nn + 1)],
                                wotb[:, 1024 * jc + 128 * ut: 1024 * jc + 128 * ut + 128],
                                o2t[jc][:, 512 * nn:512 * (nn + 1)],
                                start=(jc == 0), stop=(jc == 7)), inc=(jc == 7 and nn == 1))
                    tmm = P.sems["st"].n
                    P.waitall("ve", [("st", tmm), (P7B[i], wx)])
                    P.op("ve", lambda e, i=i, tile=tile: e.tensor_scalar(
                        xnt[:], x0t[i][:], negm_own[:, tile:tile + 1], rstd_own[:, tile:tile + 1],
                        ALU.add, ALU.mult))
                    P.fence("ve")
                    P.op("ve", lambda e, i=i: e.tensor_tensor(xnt[:], xnt[:], x0t[i][:], ALU.add))
                    P.fence("ve")
                    x0d[i] = P.sems["sv"].n
                    P.wait("ve", P7X[i], x1d[i])
                    vv = P.op("ve", lambda e, i=i: e.tensor_tensor(x1sb[i][:], xnt[:], pA0[:], ALU.add))
                    P.wait("sy", "sv", vv)
                    x1d[i] = P.dma("sy", g["x1_d"][tile * 128:(tile + 1) * 128, :], x1sb[i][:], sem=P7X[i])
                te_prev_cblk = P.sems["st"].n
        P.barrier("ph7")
        dump("x1base", g["x1_d"], (S, E))

        if upto < 7:
            P.build(); return True

        with ExitStack() as ph6:
            sbl = lambda n, s, d=F32: ph6.enter_context(P.nc.sbuf_tensor(n, s, d))
            psl = lambda n, s, d=F32: ph6.enter_context(P.nc.psum_tensor(n, s, d))
            for _eng in P.ENGS:
                P.wait(_eng, RSSM, rss_final)
                P.wait(_eng, MZS, mz_final)
            moe_sb.close()
            lnwbc = sbl("p6_lnwbc", [128, 1024])
            lnbbc = sbl("p6_lnbbc", [128, 1024])
            pbig = psl("p6_pbig", [128, 1024])
            for dst, row in ((lnwbc, lnw_sb), (lnbbc, lnb_sb)):
                for nn in range(2):
                    tv = P.op("te", lambda e, row=row, nn=nn: e.matmul(
                        pbig[:, 512 * nn:512 * (nn + 1)], onesrow[:], row[0:1, 512 * nn:512 * (nn + 1)],
                        start=True, stop=True))
                P.wait("ve", "st", tv)
                P.op("ve", lambda e, dst=dst: e.tensor_copy(dst[:], pbig[:]))
                P.fence("ve")
                P.wait("te", "sv", P.sems["sv"].n)
            wsb = sbl("p6_wsb", [128, 1024])
            mtile = sbl("p6_mt", [128, 1])
            tfl = sbl("p6_tfl", [128, 1])
            msk = sbl("p6_msk", [128, 1])
            wraw = sbl("p6_wraw", [128, 1])
            y2b = sbl("p6_y2b", [128, 1024], BF16)
            yfm = sbl("p6_yfm", [128, 8 * CAP], BF16)
            hfm = sbl("p6_hfm", [128, 32 * CAP], BF16)
            offm = sbl("p6_offm", [128, 8 * CAP])
            wcol = sbl("p6_wcol", [128, 5])
            ticol = sbl("p6_ti", [128, 5], I32)
            tie = sbl("p6_tie", [128, 5], I32)
            lnsum = sbl("p6_lnsum", [128, 1])
            lnsq = sbl("p6_lnsq", [128, 1])
            lnscr = sbl("p6_lnscr", [128, 1024])
            lnm = sbl("p6_lnm", [128, 1])
            lnv = sbl("p6_lnv", [128, 1])
            lnv2 = sbl("p6_lnv2", [128, 1])
            lnr = sbl("p6_lnr", [128, 1])
            w1s = [sbl(f"p6_w1s{i}", [128, 1024], BF16) for i in range(2)]
            w2s = [sbl(f"p6_w2s{i}", [128, 32 * 128], BF16) for i in range(2)]
            b1c = sbl("p6_b1c", [128, 32])
            b2c = sbl("p6_b2c", [128, 8])
            otok = sbl("p6_otok", [128, 1024])
            pmm = psl("p6_pmm", [128, CAP])
            ptr = psl("p6_ptr", [128, 128], BF16)
            ptrf = psl("p6_ptrf", [128, 128])
            w1d = [0, 0]
            w2d = [0, 0]
            scatter_prev = 0
            P6BB = P.dsem("p6bb")
            WS6 = P.dsem("p6ws")
            W1S = [P.dsem("p6w1a"), P.dsem("p6w1b")]
            W2S = [P.dsem("p6w2a"), P.dsem("p6w2b")]
            XSC = P.dsem("p6xsc")
            GW6 = P.dsem("p6gw")
            mt_free = 0
            wsb_v = 0
            wsb_s = 0
            tic_g = [0] * 5
            for e_ in range(8):
                wb1_ = P.dma("sy", b1c[:], g["b1s"][e_], sem=P6BB)
                wb2_ = P.dma("sy", b2c[:], g["b2s"][e_], sem=P6BB)
                for t5 in range(5):
                    P.wait("sy", "sv", mt_free)
                    wm_ = P.dma("sy", mtile[:],
                                g["meta_d"][CAP * e_ + 128 * t5: CAP * e_ + 128 * (t5 + 1), :], sem=WS6)
                    P.waitall("ve", [(WS6, wm_), (XSC, scatter_prev), (GW6, tic_g[t5])])
                    P.op("ve", lambda e: e.tensor_scalar(tfl[:], mtile[:], -1.0, 0.0, ALU.add, ALU.max), inc=False)
                    P.op("ve", lambda e: e.tensor_scalar(msk[:], mtile[:], 0.5, None, ALU.is_gt))
                    P.fence("ve")
                    P.op("ve", lambda e, t5=t5: e.tensor_copy(ticol[:, t5:t5 + 1], tfl[:]))
                    P.fence("ve")
                    P.op("ve", lambda e, e_=e_: e.tensor_scalar(tfl[:], tfl[:], 1.0, float(B * S * e_),
                                                                ALU.mult, ALU.add))
                    P.fence("ve")
                    P.op("ve", lambda e, t5=t5: e.tensor_copy(tie[:, t5:t5 + 1], tfl[:]))
                    P.fence("ve")
                    mt_free = P.sems["sv"].n
                    P.waitall("gp", [("sv", mt_free), ("sv", wsb_v), ("ss", wsb_s)])
                    P.idma(wsb[:, 0:1024], None, g["x0_tok"],
                           IndirectOffsetOnAxis(ticol[:, t5:t5 + 1], 0), sem=GW6)
                    gw_ = P.idma(wraw[:], None,
                                 g["probs_d"].rearrange("e (t x) -> (e t) x", x=1),
                                 IndirectOffsetOnAxis(tie[:, t5:t5 + 1], 0), sem=GW6)
                    tic_g[t5] = gw_
                    P.wait("ve", GW6, gw_)
                    P.op("ve", lambda e, t5=t5: e.tensor_tensor(wcol[:, t5:t5 + 1], wraw[:], msk[:], ALU.mult), inc=False)
                    P.op("ve", lambda e: e.reduce_sum(lnsum[:], wsb[:, 0:1024], AX.X))
                    P.fence("ve")
                    P.wait("se", "sv", P.sems["sv"].n)
                    sv_ = P.op("se", lambda e: e.activation(lnscr[:], wsb[:, 0:1024], AF.Square,
                                                            accum_out=lnsq[:]))
                    P.wait("ve", "ss", sv_)
                    P.op("ve", lambda e: e.tensor_scalar(lnm[:], lnsum[:], 1.0 / 1024, None, ALU.mult))
                    P.fence("ve")
                    P.op("ve", lambda e: e.tensor_tensor(lnv[:], lnm[:], lnm[:], ALU.mult))
                    P.fence("ve")
                    P.op("ve", lambda e: e.scalar_tensor_tensor(lnv[:], lnsq[:], 1.0 / 1024, lnv[:],
                                                                ALU.mult, ALU.subtract))
                    P.fence("ve")
                    P.op("ve", lambda e: e.tensor_scalar(lnv2[:], lnv[:], 1e-5, None, ALU.add))
                    P.fence("ve")
                    P.wait("se", "sv", P.sems["sv"].n)
                    sv_ = P.op("se", lambda e: e.activation(lnv2[:], lnv2[:], AF.Sqrt))
                    P.wait("ve", "ss", sv_)
                    P.op("ve", lambda e: e.reciprocal(lnr[:], lnv2[:]))
                    P.fence("ve")
                    P.op("ve", lambda e: e.tensor_scalar(lnm[:], lnm[:], -1.0, None, ALU.mult))
                    P.fence("ve")
                    P.op("ve", lambda e: e.tensor_scalar(lnscr[:], wsb[:, 0:1024], lnm[:, 0:1],
                                                         lnr[:, 0:1], ALU.add, ALU.mult))
                    P.fence("ve")
                    P.op("ve", lambda e: e.tensor_tensor(lnscr[:], lnscr[:], lnwbc[:], ALU.mult))
                    P.fence("ve")
                    P.op("ve", lambda e: e.tensor_tensor(y2b[:], lnscr[:], lnbbc[:], ALU.add))
                    P.fence("ve")
                    vv = P.sems["sv"].n
                    for k in range(8):
                        P.wait("te", "sv", vv)
                        tv = P.op("te", lambda e, k=k: e.transpose(ptr[:], y2b[:, 128 * k:128 * (k + 1)], identb[:]))
                        P.wait("ve", "st", tv)
                        vv = P.op("ve", lambda e, k=k, t5=t5: e.tensor_copy(
                            yfm[:, CAP * k + 128 * t5: CAP * k + 128 * (t5 + 1)], ptr[:]))
                        P.wait("te", "sv", vv)
                    wsb_v = P.sems["sv"].n
                    wsb_s = P.sems["ss"].n
                for m in range(32):
                    i = m % 2
                    P.wait("gp", "st", w1d[i])
                    ww = P.dma("gp", w1s[i][:].rearrange("p (k q) -> p k q", k=8),
                               g["fc1sT"][e_, :, 128 * m:128 * (m + 1)].rearrange("(k p) q -> p k q", p=128),
                               sem=W1S[i])
                    P.waitall("te", [(W1S[i], ww), ("ss", P.sems["ss"].n)])
                    for k in range(8):
                        for nn in range(2):
                            n0, n1 = (0, 512) if nn == 0 else (512, CAP)
                            P.op("te", lambda e, i=i, k=k, n0=n0, n1=n1: e.matmul(
                                pmm[:, n0:n1], w1s[i][:, 128 * k:128 * (k + 1)],
                                yfm[:, CAP * k + n0: CAP * k + n1],
                                start=(k == 0), stop=(k == 7)), inc=(nn == 1))
                    w1d[i] = P.sems["st"].n
                    P.waitall("se", [("st", w1d[i]), (P6BB, wb1_)])
                    P.op("se", lambda e, m=m: e.activation(
                        hfm[:, CAP * m: CAP * (m + 1)], pmm[:], AF.Gelu_apprx_tanh,
                        bias=b1c[:, m:m + 1], scale=1.0))
                    P.fence("se")
                sprev = P.sems["ss"].n
                for oc in range(8):
                    i = oc % 2
                    P.wait("gp", "st", w2d[i])
                    ww = P.dma("gp", w2s[i][:].rearrange("p (k q) -> p k q", k=32),
                               g["fc2sT"][e_, :, 128 * oc:128 * (oc + 1)].rearrange("(k p) q -> p k q", p=128),
                               sem=W2S[i])
                    P.waitall("te", [(W2S[i], ww), ("ss", sprev)])
                    for k in range(32):
                        for nn in range(2):
                            n0, n1 = (0, 512) if nn == 0 else (512, CAP)
                            P.op("te", lambda e, i=i, k=k, n0=n0, n1=n1: e.matmul(
                                pmm[:, n0:n1], w2s[i][:, 128 * k:128 * (k + 1)],
                                hfm[:, CAP * k + n0: CAP * k + n1],
                                start=(k == 0), stop=(k == 31)), inc=(nn == 1))
                    w2d[i] = P.sems["st"].n
                    P.waitall("se", [("st", w2d[i]), (P6BB, wb2_)])
                    sprev = P.op("se", lambda e, oc=oc: e.activation(
                        offm[:, CAP * oc: CAP * (oc + 1)], pmm[:], AF.Identity,
                        bias=b2c[:, oc:oc + 1], scale=1.0))
                    P.fence("se")
                for t5 in range(5):
                    vv = P.sems["sv"].n
                    for oc in range(8):
                        P.waitall("te", [("sv", vv), ("ss", P.sems["ss"].n)])
                        tv = P.op("te", lambda e, oc=oc, t5=t5: e.transpose(
                            ptrf[:], offm[:, CAP * oc + 128 * t5: CAP * oc + 128 * (t5 + 1)], ident[:]))
                        P.wait("ve", "st", tv)
                        vv = P.op("ve", lambda e, oc=oc, t5=t5: e.tensor_scalar(
                            otok[:, 128 * oc:128 * (oc + 1)], ptrf[:], wcol[:, t5:t5 + 1], None, ALU.mult))
                    P.fence("ve")
                    vv = P.sems["sv"].n
                    P.waitall("gp", [("sv", vv), (XSC, scatter_prev)])
                    scatter_prev = P.idma(g["x1_d"], IndirectOffsetOnAxis(ticol[:, t5:t5 + 1], 0),
                                          otok[:], None, compute_op=ALU.add, sem=XSC)
                    P.wait("ve", XSC, scatter_prev)
        P.barrier("ph6")
        dump("x1full", g["x1_d"], (S, E))
        return False

    if start <= 7:
        _pre = set(P._dsems)
        if _l0():
            return dbg_outs, st
        P.release_dsems(set(P._dsems) - _pre)

    if upto < 8:
        P.build(); return dbg_outs, st

    _pre8 = set(P._dsems)
    with ExitStack() as ph8:
        _sb = P.sb
        P.sb = lambda n, s, d, _st=ph8: _st.enter_context(P.nc.sbuf_tensor("x1" + n, s, d))
        emit_stats(g["x1_d"], stats1_negm, stats1_rstd, dst_rms=stats1_rms)
        P.sb = _sb
    with ExitStack() as ph8b:
        psl = lambda n, s: ph8b.enter_context(P.nc.psum_tensor(n, s, F32))
        sbl = lambda n, s, d=F32: ph8b.enter_context(P.nc.sbuf_tensor(n, s, d))
        pT = psl("p8_pT", [128, 128])
        stT = sbl("p8_stT", [128, 128])
        B8 = P.dsem("b8")
        for row, tile in ((0, stats1_negm), (1, stats1_rstd), (2, stats1_rms)):
            P.wait("te", "sv", P.sems["sv"].n)
            tv = P.op("te", lambda e, t=tile: e.transpose(pT[0:NT, 0:128], t[:], ident[:]))
            P.waitall("ve", [("st", tv), (B8, B8.n)])
            vv = P.op("ve", lambda e: e.tensor_copy(stT[0:NT, :], pT[0:NT, 0:128]))
            P.wait("sy", "sv", vv)
            P.dma("sy", g["sd1_d"][row:row + 1, :].rearrange("o (f p) -> (o f) p", p=128),
                  stT[0:NT, :], sem=B8)
    P.barrier("ph8")
    P.release_dsems(set(P._dsems) - _pre8)
    dump("sd1", g["sd1_d"], (3, S))
    dump("x1v8", g["x1_d"], (S, E))

    if upto < 9:
        P.build(); return dbg_outs, st

    k1sb_ctx = P.nc.sbuf_tensor("k1sb", [128, 2 * S], BF16)
    k1sb = k1sb_ctx.__enter__()
    v1sb_ctx = P.nc.sbuf_tensor("v1sb", [128, NT * 256], BF16)
    v1sb = v1sb_ctx.__enter__()
    xn1_ctx = P.nc.sbuf_tensor("xn1_fm", [128, NC * S], BF16)
    xn1_fm = xn1_ctx.__enter__()
    def _ph9_fn():
        with ExitStack() as ph9:
            sbl = lambda n, s, d=F32: ph9.enter_context(P.nc.sbuf_tensor(n, s, d))
            psl = lambda n, s, d=F32: ph9.enter_context(P.nc.psum_tensor(n, s, d))
            x1t = [sbl(f"p9_x1{i}", [128, 1024]) for i in range(2)]
            x1s = [P.dsem("p9x0"), P.dsem("p9x1")]
            xnb = sbl("p9_xnb", [128, 1024], BF16)
            pt9 = psl("p9_pt", [128, 128], BF16)
            x1d = [0, 0]
            for tile in range(NT):
                i = tile % 2
                P.wait("sy", "sv", x1d[i])
                wx = P.dma("sy", x1t[i][:], g["x1_d"][tile * 128:(tile + 1) * 128, :], sem=x1s[i])
                P.waitall("ve", [(x1s[i], wx), ("st", P.sems["st"].n)])
                x1d[i] = P.op("ve", lambda e, i=i, tile=tile: e.tensor_scalar(
                    xnb[:], x1t[i][:], stats1_negm[:, tile:tile + 1], stats1_rstd[:, tile:tile + 1],
                    ALU.add, ALU.mult))
                P.fence("ve")
                vv = P.sems["sv"].n
                for k in range(NC):
                    P.wait("te", "sv", vv)
                    tv = P.op("te", lambda e, k=k: e.transpose(pt9[:], xnb[:, 128 * k:128 * (k + 1)], identb[:]))
                    P.wait("ve", "st", tv)
                    vv = P.op("ve", lambda e, k=k, tile=tile: e.tensor_copy(
                        xn1_fm[:, k * S + tile * 128: k * S + tile * 128 + 128], pt9[:]))
                P.fence("ve")
    _pre9 = set(P._dsems)
    _ph9_fn()
    P.barrier("ph9")
    P.release_dsems(set(P._dsems) - _pre9)
    dump("xn1fm", xn1_fm[:], (128, NC * S), BF16)
    dump("x1v9", g["x1_d"], (S, E))

    if upto < 10:
        P.build(); return dbg_outs, st

    P10 = int(os.environ.get("P10", "9"))

    def _ph10a_fn():
        with ExitStack() as ph10:
            sbl = lambda n, s, d=F32: ph10.enter_context(P.nc.sbuf_tensor(n, s, d))
            psl = lambda n, s, d=F32: ph10.enter_context(P.nc.psum_tensor(n, s, d))

            def proj_fm(wt_key, src_fm_sb, dst_d, tag):
                wb = [sbl(f"{tag}_w{i}", [128, 1024], BF16) for i in range(2)]
                wsem = [P.dsem(f"{tag}w{i}") for i in range(2)]
                osem = [P.dsem(f"{tag}o{i}") for i in range(2)]
                ob = [sbl(f"{tag}_ob{i}", [128, 2048], BF16) for i in range(2)]
                pq = psl(f"{tag}_pq", [128, 2048])
                wdone = [0, 0]
                odone = [0, 0]
                for c in range(NC):
                    i = c % 2
                    P.wait("gp", "st", wdone[i])
                    ww = P.dma("gp", wb[i][:].rearrange("p (k q) -> p k q", k=8),
                               g[wt_key][:, 128 * c:128 * (c + 1)].rearrange("(k p) q -> p k q", p=128),
                               sem=wsem[i])
                    P.wait("te", wsem[i], ww)
                    for half in range(2):
                        hi = half
                        P.wait("te", "sv", P.sems["sv"].n)
                        for k in range(8):
                            for nt in range(4):
                                P.op("te", lambda e, i=i, k=k, half=half, nt=nt: e.matmul(
                                    pq[:, 512 * nt:512 * (nt + 1)], wb[i][:, 128 * k:128 * (k + 1)],
                                    src_fm_sb[:, k * S + half * 2048 + 512 * nt:
                                              k * S + half * 2048 + 512 * nt + 512],
                                    start=(k == 0), stop=(k == 7)), inc=(k == 7 and nt == 3))
                        wdone[i] = P.sems["st"].n
                        P.waitall("ve", [("st", P.sems["st"].n), (osem[hi], odone[hi])])
                        odone[hi] = P.op("ve", lambda e, hi=hi: e.tensor_copy(ob[hi][:], pq[:]))
                        P.wait("sy", "sv", P.sems["sv"].n)
                        odone[hi] = P.dma("sy", dst_d[128 * c:128 * (c + 1),
                                                      half * 2048:(half + 1) * 2048],
                                          ob[hi][:], sem=osem[hi])
            proj_fm("l1_attn_f", xn1_fm, g["xf_d"], "pF")
            vw = [sbl(f"p10_vw{i}", [128, 1024], BF16) for i in range(2)] if P10 >= 2 else None
            vws = [P.dsem("p10v0"), P.dsem("p10v1")]
            VV = P.dsem("p10vv")
            pv = psl("p10_pv", [128, 1024]) if P10 >= 2 else None
            vbf = sbl("p10_vbf", [128, 1024], BF16) if P10 >= 2 else None
            vd = [0, 0]
            for stile in range(NT if P10 >= 2 else 0):
                P.wait("te", "sv", P.sems["sv"].n)
                for k in range(8):
                    i = k % 2
                    P.wait("gp", "st", vd[i])
                    ww = P.dma("gp", vw[i][:], g["l1_attn_v"][128 * k:128 * (k + 1), :], sem=vws[i])
                    P.wait("te", vws[i], ww)
                    for nn in range(2):
                        P.op("te", lambda e, i=i, k=k, nn=nn, stile=stile: e.matmul(
                            pv[:, 512 * nn:512 * (nn + 1)],
                            xn1_fm[:, k * S + stile * 128: k * S + stile * 128 + 128],
                            vw[i][:, 512 * nn:512 * (nn + 1)],
                            start=(k == 0), stop=(k == 7)), inc=(nn == 1))
                    vd[i] = P.sems["st"].n
                tv = P.sems["st"].n
                P.wait("ve", "st", tv)
                vv = P.op("ve", lambda e: e.tensor_copy(vbf[:], pv[:]))
                P.wait("sy", "sv", vv)
                wv = P.dma("sy", g["vx_d"][stile * 128:(stile + 1) * 128, :], vbf[:], sem=VV)
                P.wait("ve", VV, wv)
    _pre10a = set(P._dsems)
    _ph10a_fn()
    P.barrier("ph10a")
    P.release_dsems(set(P._dsems) - _pre10a)
    if P10 < 2:
        P.build(); return dbg_outs, st
    if P10 < 3:
        P.build(); return dbg_outs, st
    xf_ctx = P.nc.sbuf_tensor("xf_fm", [128, NC * S], BF16)
    xf_fm = xf_ctx.__enter__()
    def _ph10b_fn():
        with ExitStack() as ph10b:
            sbl = lambda n, s, d=F32: ph10b.enter_context(P.nc.sbuf_tensor(n, s, d))
            psl = lambda n, s, d=F32: ph10b.enter_context(P.nc.psum_tensor(n, s, d))
            XFL = P.dsem("xfl")
            wxf = P.dma("sy", xf_fm[:].rearrange("p (c t) -> p c t", c=8),
                        g["xf_d"].rearrange("(c p) t -> p c t", p=128), sem=XFL)
            for _e in ("te", "ve"):
                P.wait(_e, XFL, wxf)

            def proj_fm2(wt_key, src_fm_sb, dst_d, tag):
                wb = [sbl(f"{tag}_w{i}", [128, 1024], BF16) for i in range(2)]
                wsem = [P.dsem(f"{tag}w{i}") for i in range(2)]
                osem = [P.dsem(f"{tag}o{i}") for i in range(2)]
                ob = [sbl(f"{tag}_ob{i}", [128, 2048], BF16) for i in range(2)]
                pq = psl(f"{tag}_pq", [128, 2048])
                wdone = [0, 0]
                odone = [0, 0]
                for c in range(NC):
                    i = c % 2
                    P.wait("gp", "st", wdone[i])
                    ww = P.dma("gp", wb[i][:].rearrange("p (k q) -> p k q", k=8),
                               g[wt_key][:, 128 * c:128 * (c + 1)].rearrange("(k p) q -> p k q", p=128),
                               sem=wsem[i])
                    P.wait("te", wsem[i], ww)
                    for half in range(2):
                        hi = half
                        P.wait("te", "sv", P.sems["sv"].n)
                        for k in range(8):
                            for nt in range(4):
                                P.op("te", lambda e, i=i, k=k, half=half, nt=nt: e.matmul(
                                    pq[:, 512 * nt:512 * (nt + 1)], wb[i][:, 128 * k:128 * (k + 1)],
                                    src_fm_sb[:, k * S + half * 2048 + 512 * nt:
                                              k * S + half * 2048 + 512 * nt + 512],
                                    start=(k == 0), stop=(k == 7)), inc=(k == 7 and nt == 3))
                        wdone[i] = P.sems["st"].n
                        P.waitall("ve", [("st", P.sems["st"].n), (osem[hi], odone[hi])])
                        odone[hi] = P.op("ve", lambda e, hi=hi: e.tensor_copy(ob[hi][:], pq[:]))
                        P.wait("sy", "sv", P.sems["sv"].n)
                        odone[hi] = P.dma("sy", dst_d[128 * c:128 * (c + 1),
                                                      half * 2048:(half + 1) * 2048],
                                          ob[hi][:], sem=osem[hi])
            proj_fm2("l1_attn_q", xf_fm, g["qx_d"], "pQ")
            proj_fm2("l1_attn_k", xf_fm, g["kx_d"], "pK")
    _pre10b = set(P._dsems)
    _ph10b_fn()
    xf_ctx.__exit__(None, None, None)
    P.barrier("ph10b")
    P.release_dsems(set(P._dsems) - _pre10b)
    if P10 < 4:
        P.build(); return dbg_outs, st

    def _ph10c_fn():
        with ExitStack() as ph10c:
            sbl = lambda n, s, d=F32: ph10c.enter_context(P.nc.sbuf_tensor(n, s, d))
            psl = lambda n, s, d=F32: ph10c.enter_context(P.nc.psum_tensor(n, s, d))
            qg_sb = sbl("pA_qg", [128, 8 * 512], BF16)
            kt = [sbl(f"pA_kt{i}", [128, 1024], BF16) for i in range(2)]
            vt = [sbl(f"pA_vt{i}", [128, 1024], BF16) for i in range(2)]
            pt = sbl("pA_pt", [128, NT * 512], BF16)
            QGS = P.dsem("pAq")
            kts = [P.dsem("pAk0"), P.dsem("pAk1")]
            vts = [P.dsem("pAv0"), P.dsem("pAv1")]
            A1W = P.dsem("pAa")
            pp = [psl(f"pA_pp{i}", [128, 512]) for i in range(2)]
            pO4 = psl("pA_pO4", [128, 2048])
            pZ = psl("pA_pZ", [1, 512])
            pMB = psl("pA_pMB", [128, 512])
            pTf = pMB
            tmax = sbl("pA_tmax", [128, 1])
            mrun = sbl("pA_mrun", [128, 1])
            negm = sbl("pA_negm", [1, 1])
            negm2 = sbl("pA_negm2", [1, 1])
            bias_nm = sbl("pA_bias", [128, 1])
            rz = sbl("pA_rz", [1, 512])
            rzbc = sbl("pA_rzbc", [128, 512])
            onrm = sbl("pA_onrm", [128, 512])
            a1st = sbl("pA_a1st", [128, 4 * 1024])
            ktd = [0, 0]
            vtd = [0, 0]
            a1w = 0
            for gq in range(8):
                wq_ = P.dma("sy", qg_sb[:].rearrange("p (c t) -> p c t", c=8),
                            g["qx_d"][:, gq * 512:(gq + 1) * 512].rearrange("(c p) t -> p c t", p=128),
                            sem=QGS)
                P.wait("te", QGS, wq_)
                P.op("ve", lambda e: e.memset(mrun[:], -1e30))
                P.fence("ve")
                for which in (1, 2):
                    for j in range(NT):
                        i = j % 2
                        P.wait("gp", "st", ktd[i])
                        wk_ = P.dma("gp", kt[i][:].rearrange("p (k q) -> p k q", k=8),
                                    g["kx_d"][:, 128 * j:128 * (j + 1)].rearrange("(k p) q -> p k q", p=128),
                                    sem=kts[i])
                        P.wait("te", kts[i], wk_)
                        if which == 1:
                            P.wait("te", "sv", P.sems["sv"].n)
                        else:
                            P.wait("te", "ss", P.sems["ss"].n)
                        buf = pp[j % 2]
                        for k in range(8):
                            P.op("te", lambda e, i=i, k=k, b_=buf: e.matmul(
                                b_[:], kt[i][:, 128 * k:128 * (k + 1)],
                                qg_sb[:, k * 512:(k + 1) * 512],
                                start=(k == 0), stop=(k == 7)), inc=(k == 7))
                        ktd[i] = P.sems["st"].n
                        tm = P.sems["st"].n
                        if which == 1:
                            P.wait("ve", "st", tm)
                            P.op("ve", lambda e, b_=buf: e.reduce_max(tmax[:], b_[:], AX.X))
                            P.fence("ve")
                            P.op("ve", lambda e: e.tensor_max(mrun[:], mrun[:], tmax[:]))
                            P.fence("ve")
                        else:
                            P.wait("se", "st", tm)
                            P.op("se", lambda e, b_=buf, j=j: e.activation(
                                pt[:, 512 * j:512 * (j + 1)], b_[:],
                                AF.Exp, bias=bias_nm[:, 0:1], scale=S32))
                    if which == 1:
                        vv = P.sems["sv"].n
                        P.wait("te", "sv", vv)
                        tv = P.op("te", lambda e: e.transpose(pMB[0:1, 0:128], mrun[:], ident[:]))
                        P.wait("ve", "st", tv)
                        P.op("ve", lambda e: e.reduce_max(negm[:], pMB[0:1, 0:128], AX.X, negate=True))
                        P.fence("ve")
                        vv = P.op("ve", lambda e: e.tensor_scalar(negm2[:], negm[:], S32, None, ALU.mult))
                        P.wait("te", "sv", vv)
                        tv = P.op("te", lambda e: e.matmul(pMB[:, 0:1], onesrow[:], negm2[:],
                                                           start=True, stop=True))
                        P.wait("ve", "st", tv)
                        P.op("ve", lambda e: e.tensor_copy(bias_nm[:], pMB[:, 0:1]))
                        P.fence("ve")
                        P.wait("se", "sv", P.sems["sv"].n)
                sdone = P.sems["ss"].n
                P.waitall("te", [("ss", sdone), ("sv", P.sems["sv"].n)])
                for j in range(NT):
                    P.op("te", lambda e, j=j: e.matmul(pZ[:], ones128b[:], pt[:, 512 * j:512 * (j + 1)],
                                                       start=(j == 0), stop=(j == NT - 1)),
                         inc=(j == NT - 1))
                tz = P.sems["st"].n
                P.wait("ve", "st", tz)
                P.op("ve", lambda e: e.reciprocal(rz[:], pZ[:]))
                vv = P.fence("ve") or P.sems["sv"].n
                P.wait("te", "sv", P.sems["sv"].n)
                tv = P.op("te", lambda e: e.matmul(pMB[:], onesrow[:], rz[:], start=True, stop=True))
                P.wait("ve", "st", tv)
                P.op("ve", lambda e: e.tensor_copy(rzbc[:], pMB[:]))
                P.fence("ve")
                for dh in range(2):
                    P.wait("te", "sv", P.sems["sv"].n)
                    for j in range(NT):
                        i = j % 2
                        P.wait("gp", "st", vtd[i])
                        wv_ = P.dma("gp", vt[i][:, 0:512],
                                    g["vx_d"][128 * j:128 * (j + 1), 512 * dh:512 * (dh + 1)], sem=vts[i])
                        P.wait("te", vts[i], wv_)
                        for dq in range(4):
                            P.op("te", lambda e, i=i, j=j, dq=dq: e.matmul(
                                pO4[:, 512 * dq:512 * (dq + 1)], vt[i][:, 128 * dq:128 * (dq + 1)],
                                pt[:, 512 * j:512 * (j + 1)],
                                start=(j == 0), stop=(j == NT - 1)), inc=(dq == 3))
                        vtd[i] = P.sems["st"].n
                    tav = P.sems["st"].n
                    for dq in range(4):
                        dt = 4 * dh + dq
                        P.wait("ve", "st", tav)
                        P.op("ve", lambda e, dq=dq: e.tensor_tensor(
                            onrm[:], pO4[:, 512 * dq:512 * (dq + 1)], rzbc[:], ALU.mult))
                        P.fence("ve")
                        vv = P.sems["sv"].n
                        for t in range(4):
                            P.wait("te", "sv", vv)
                            tv = P.op("te", lambda e, t=t: e.transpose(
                                pTf[:, 0:128], onrm[:, 128 * t:128 * (t + 1)], ident[:]))
                            P.wait("ve", "st", tv)
                            vv = P.op("ve", lambda e, t=t, dt=dt: e.tensor_copy(
                                a1st[:, 1024 * t + 128 * dt: 1024 * t + 128 * (dt + 1)], pTf[:, 0:128]))
                        P.fence("ve")
                P.waitall("sy", [("sv", P.sems["sv"].n), (A1W, a1w)])
                a1w = P.dma("sy", g["a1_d"][gq * 512:(gq + 1) * 512, :]
                            .rearrange("(t sl) d -> sl t d", sl=128),
                            a1st[:].rearrange("p (t d) -> p t d", t=4), sem=A1W)
                P.wait("ve", A1W, a1w)
    _pre10c = set(P._dsems)
    _ph10c_fn()
    P.barrier("ph10")
    P.release_dsems(set(P._dsems) - _pre10c)
    dump("a1", g["a1_d"], (S, E))

    if upto < 11:
        P.build(); return dbg_outs, st

    def _ph11_fn():
        with ExitStack() as ph11:
            sbl = lambda n, s, d=F32: ph11.enter_context(P.nc.sbuf_tensor(n, s, d))
            psl = lambda n, s, d=F32: ph11.enter_context(P.nc.psum_tensor(n, s, d))
            cos2t = sbl("p11_cos2", [64, S])
            sin2t = sbl("p11_sin2", [64, S])
            C11 = P.dsem("p11c")
            wc = P.dma("sy", cos2t[:], g["cos2"], sem=C11)
            ws2 = P.dma("sy", sin2t[:], g["sin2"], sem=C11)
            P.wait("ve", C11, max(wc, ws2))
            wqb = [sbl(f"p11_wq{i}", [128, 1024], BF16) for i in range(2)]
            wqs = [P.dsem("p11w0"), P.dsem("p11w1")]
            QO = P.dsem("p11qo")
            pq = psl("p11_pq", [128, 512])
            prm0 = psl("p11_prm0", [1, 512])
            prm1 = psl("p11_prm1", [1, 512])
            pbc = psl("p11_pbc", [64, 512])
            sqs = sbl("p11_sqs", [128, 512])
            rr0 = sbl("p11_rr0", [1, 512]); rr1 = sbl("p11_rr1", [1, 512])
            rr0b = sbl("p11_rr0b", [1, 512]); rr1b = sbl("p11_rr1b", [1, 512])
            sbc = sbl("p11_sbc", [64, 512])
            t1 = sbl("p11_t1", [64, 512]); t2 = sbl("p11_t2", [64, 512])
            t3 = sbl("p11_t3", [64, 512]); t4 = sbl("p11_t4", [64, 512])
            u1 = sbl("p11_u1", [64, 512]); u2 = sbl("p11_u2", [64, 512])
            q1bA = sbl("p11_q1bA", [64, 512], BF16)
            q1bB = sbl("p11_q1bB", [64, 512], BF16)
            wdone = [0, 0]
            q1w = 0

            def qk_proj(wt_key, nchunks, dest, gain_sb, tag):
                nonlocal q1w
                for c in range(nchunks):
                    i = c % 2
                    P.wait("gp", "st", wdone[i])
                    ww = P.dma("gp", wqb[i][:].rearrange("p (k q) -> p k q", k=8),
                               g[wt_key][:, 128 * c:128 * (c + 1)].rearrange("(k p) q -> p k q", p=128),
                               sem=wqs[i])
                    P.wait("te", wqs[i], ww)
                    for sub in range(8):
                        t0 = sub * 512
                        P.waitall("te", [("sv", P.sems["sv"].n), ("ss", P.sems["ss"].n)])
                        for k in range(8):
                            P.op("te", lambda e, i=i, k=k, t0=t0: e.matmul(
                                pq[:], wqb[i][:, 128 * k:128 * (k + 1)],
                                xn1_fm[:, k * S + t0: k * S + t0 + 512],
                                start=(k == 0), stop=(k == 7)), inc=(k == 7))
                        wdone[i] = P.sems["st"].n
                        tmm = P.sems["st"].n
                        P.wait("se", "st", tmm)
                        sv_ = P.op("se", lambda e: e.activation(sqs[:], pq[:], AF.Square))
                        P.wait("te", "ss", sv_)
                        P.op("te", lambda e: e.matmul(prm0[:], hsel_sb[:, 0:1], sqs[:],
                                                      start=True, stop=True), inc=False)
                        tv = P.op("te", lambda e: e.matmul(prm1[:], hsel_sb[:, 1:2], sqs[:],
                                                           start=True, stop=True))
                        P.wait("ve", "st", tv)
                        for hh, rrx, rrxb, prmx in ((0, rr0, rr0b, prm0), (1, rr1, rr1b, prm1)):
                            h = 2 * c + hh
                            P.op("ve", lambda e, rrx=rrx, prmx=prmx: e.tensor_scalar(
                                rrx[:], prmx[:], 1.0 / 64, 1e-6, ALU.mult, ALU.add))
                            P.fence("ve")
                            P.wait("se", "sv", P.sems["sv"].n)
                            sv_ = P.op("se", lambda e, rrx=rrx, rrxb=rrxb: e.activation(rrxb[:], rrx[:], AF.Sqrt))
                            P.wait("ve", "ss", sv_)
                            P.op("ve", lambda e, rrx=rrx, rrxb=rrxb: e.reciprocal(rrx[:], rrxb[:]))
                            P.fence("ve")
                            P.op("ve", lambda e, rrx=rrx, rrxb=rrxb, h=h, gain_sb=gain_sb: e.tensor_single_scalar(
                                rrxb[:], rrx[:], gain_sb[0:1, h:h + 1], ALU.mult))
                            P.fence("ve")
                            P.op("ve", lambda e, rrx=rrx, rrxb=rrxb: e.tensor_copy(rrx[:], rrxb[:]))
                            P.fence("ve")
                        vv = P.sems["sv"].n
                        P.wait("te", "sv", vv)
                        P.op("te", lambda e: e.matmul(pbc[0:32, :], onesrow[0:1, 0:32], rr0[:],
                                                      start=True, stop=True), inc=False)
                        tv = P.op("te", lambda e: e.matmul(pbc[32:64, :], onesrow[0:1, 0:32], rr1[:],
                                                           start=True, stop=True))
                        P.wait("ve", "st", tv)
                        P.op("ve", lambda e: e.tensor_copy(sbc[:], pbc[:]))
                        P.fence("ve")
                        cs = cos2t[:, t0:t0 + 512]
                        sn = sin2t[:, t0:t0 + 512]
                        P.op("ve", lambda e, cs=cs: e.tensor_tensor(t1[:], pq[0:64, :], cs, ALU.mult), inc=False)
                        P.op("ve", lambda e, sn=sn: e.tensor_tensor(t2[:], pq[64:128, :], sn, ALU.mult), inc=False)
                        P.op("ve", lambda e, sn=sn: e.tensor_tensor(t3[:], pq[0:64, :], sn, ALU.mult), inc=False)
                        P.op("ve", lambda e, cs=cs: e.tensor_tensor(t4[:], pq[64:128, :], cs, ALU.mult))
                        P.fence("ve")
                        P.op("ve", lambda e: e.tensor_tensor(u1[:], t1[:], t2[:], ALU.add), inc=False)
                        P.op("ve", lambda e: e.tensor_tensor(u2[:], t4[:], t3[:], ALU.subtract))
                        P.fence("ve")
                        P.op("ve", lambda e: e.tensor_tensor(q1bA[0:32, :], u1[0:32, :], sbc[0:32, :], ALU.mult), inc=False)
                        P.op("ve", lambda e: e.tensor_tensor(q1bA[32:64, :], u2[0:32, :], sbc[0:32, :], ALU.mult), inc=False)
                        P.op("ve", lambda e: e.tensor_tensor(q1bB[0:32, :], u1[32:64, :], sbc[32:64, :], ALU.mult), inc=False)
                        P.op("ve", lambda e: e.tensor_tensor(q1bB[32:64, :], u2[32:64, :], sbc[32:64, :], ALU.mult))
                        P.fence("ve")
                        vv = P.sems["sv"].n
                        dest(c, sub, q1bA, q1bB, vv)
            def q_dest(c, sub, qA, qB, vv):
                nonlocal q1w
                P.wait("sy", "sv", vv)
                P.dma("sy", g["q1_d"][128 * c:128 * c + 64, sub * 512:(sub + 1) * 512],
                      qA[:], sem=QO)
                q1w = P.dma("sy", g["q1_d"][128 * c + 64:128 * c + 128, sub * 512:(sub + 1) * 512],
                            qB[:], sem=QO)
                P.wait("ve", QO, q1w)
            qk_proj("wqt", 8, q_dest, qgain_sb, "q")
            def k_dest(c, sub, qA, qB, vv):
                P.op("ve", lambda e, c=c, sub=sub, qA=qA: e.tensor_copy(
                    k1sb[0:64, c * S + sub * 512: c * S + sub * 512 + 512], qA[:]), inc=False)
                P.op("ve", lambda e, c=c, sub=sub, qB=qB: e.tensor_copy(
                    k1sb[64:128, c * S + sub * 512: c * S + sub * 512 + 512], qB[:]))
                P.fence("ve")
            qk_proj("wkt", 2, k_dest, kgain_sb, "k")
            vwb = [sbl(f"p11_vw{i}", [128, 256], BF16) for i in range(2)]
            vws = [P.dsem("p11v0"), P.dsem("p11v1")]
            pv1 = psl("p11_pv1", [128, 256])
            v1d = [0, 0]
            for stile in range(NT):
                P.wait("te", "sv", P.sems["sv"].n)
                for k in range(8):
                    i = k % 2
                    P.wait("gp", "st", v1d[i])
                    ww = P.dma("gp", vwb[i][:], g["wvt"][128 * k:128 * (k + 1), :], sem=vws[i])
                    P.wait("te", vws[i], ww)
                    P.op("te", lambda e, i=i, k=k, stile=stile: e.matmul(
                        pv1[:], xn1_fm[:, k * S + stile * 128: k * S + stile * 128 + 128],
                        vwb[i][:], start=(k == 0), stop=(k == 7)), inc=True)
                    v1d[i] = P.sems["st"].n
                tv = P.sems["st"].n
                P.wait("ve", "st", tv)
                P.op("ve", lambda e, stile=stile: e.tensor_scalar(
                    v1sb[:, stile * 256:(stile + 1) * 256], pv1[:],
                    stats1_rms[:, stile:stile + 1], None, ALU.mult))
                P.fence("ve")
    _pre11 = set(P._dsems)
    _ph11_fn()
    xn1_ctx.__exit__(None, None, None)
    P.barrier("ph11")
    P.release_dsems(set(P._dsems) - _pre11)
    dump("q1", g["q1_d"], (E, S), BF16)
    dump("k1", k1sb[:], (128, 2 * S), BF16)
    dump("v1", v1sb[:], (128, NT * 256), BF16)

    if upto < 12:
        P.build(); return dbg_outs, st

    Hfm_ctx = P.nc.sbuf_tensor("Hfm", [128, NC * S], BF16)
    Hfm = Hfm_ctx.__enter__()
    def _ph12_fn():
        with ExitStack() as ph12:
            sbl = lambda n, s, d=F32: ph12.enter_context(P.nc.sbuf_tensor(n, s, d))
            psl = lambda n, s, d=F32: ph12.enter_context(P.nc.psum_tensor(n, s, d))
            trimask = sbl("p12_trim", [128, 4 * 512], F32)
            binmask = sbl("p12_binm", [128, 4 * 512], BF16)
            M12 = P.dsem("p12m")
            wm1 = P.dma("sy", trimask[:], g["trimask4"], sem=M12)
            wm2 = P.dma("gp", binmask[:], g["binmask4"], sem=M12)
            for _e in ("ve", "se"):
                P.wait(_e, M12, max(wm1, wm2))
            qh = sbl("p12_qh", [128, 512], BF16)
            QH = P.dsem("p12q")
            pt12 = sbl("p12_pt", [128, NT * 512], BF16)
            escr = sbl("p12_escr", [128, 512], BF16)
            pp = [psl(f"p12_pp{i}", [128, 512]) for i in range(2)]
            pO = psl("p12_pO", [64, 512])
            pZ = psl("p12_pZ", [1, 512])
            pMB = psl("p12_pMB", [128, 512])
            tmax = sbl("p12_tmax", [128, 1])
            mrun = sbl("p12_mrun", [128, 1])
            negm = sbl("p12_negm", [1, 1])
            negm2 = sbl("p12_negm2", [1, 1])
            bias_nm = sbl("p12_bias", [128, 1])
            rz = sbl("p12_rz", [1, 512])
            rzbc = sbl("p12_rzbc", [64, 512])
            qhw = 0
            ppv = [0, 0]
            pps = [0, 0]
            hfm_v = 0
            rz_v = 0
            for h in range(16):
                kv = h // 4
                cc, kb = kv // 2, (kv % 2) * 64
                for gq in range(8):
                    nj = 4 * gq + 4
                    P.wait("sy", "st", qhw)
                    wq_ = P.dma("sy", qh[kb:kb + 64, :],
                                g["q1_d"][64 * h:64 * (h + 1), gq * 512:(gq + 1) * 512], sem=QH)
                    P.wait("te", QH, wq_)
                    P.op("ve", lambda e: e.memset(mrun[:], -1e30))
                    P.fence("ve")
                    for which in (1, 2):
                        for j in range(nj):
                            i = j % 2
                            buf = pp[i]
                            P.waitall("te", [("sv", ppv[i]), ("ss", pps[i])])
                            P.op("te", lambda e, b_=buf, j=j, kb=kb, cc=cc: e.matmul(
                                b_[:], k1sb[kb:kb + 64, cc * S + 128 * j: cc * S + 128 * (j + 1)],
                                qh[kb:kb + 64, :], start=True, stop=True), inc=True)
                            tm = P.sems["st"].n
                            if which == 1:
                                P.wait("ve", "st", tm)
                                P.op("ve", lambda e, b_=buf: e.reduce_max(tmax[:], b_[:], AX.X))
                                P.fence("ve")
                                ppv[i] = P.sems["sv"].n
                                P.op("ve", lambda e: e.tensor_max(mrun[:], mrun[:], tmax[:]))
                                P.fence("ve")
                            else:
                                dk = j - 4 * gq
                                vstt = 0
                                if dk >= 0:
                                    P.wait("ve", "st", tm)
                                    vstt = P.op("ve", lambda e, b_=buf, dk=dk: e.scalar_tensor_tensor(
                                        b_[:], trimask[:, 512 * dk:512 * (dk + 1)], 1.0, b_[:],
                                        ALU.mult, ALU.add))
                                    ppv[i] = vstt
                                    P.wait("se", "sv", vstt)
                                P.wait("se", "st", tm)
                                pps[i] = P.op("se", lambda e, b_=buf, j=j: e.activation(
                                    pt12[:, 512 * j:512 * (j + 1)], b_[:],
                                    AF.Exp, bias=bias_nm[:, 0:1], scale=S8))
                        if which == 1:
                            vv = P.sems["sv"].n
                            P.wait("te", "sv", vv)
                            tv = P.op("te", lambda e: e.transpose(pMB[0:1, 0:128], mrun[:], ident[:]))
                            P.wait("ve", "st", tv)
                            P.op("ve", lambda e: e.reduce_max(negm[:], pMB[0:1, 0:128], AX.X, negate=True))
                            P.fence("ve")
                            vv = P.op("ve", lambda e: e.tensor_scalar(negm2[:], negm[:], S8, None, ALU.mult))
                            P.wait("te", "sv", vv)
                            tv = P.op("te", lambda e: e.matmul(pMB[:, 0:1], onesrow[:], negm2[:],
                                                               start=True, stop=True))
                            P.wait("ve", "st", tv)
                            P.op("ve", lambda e: e.tensor_copy(bias_nm[:], pMB[:, 0:1]))
                            P.fence("ve")
                            P.wait("se", "sv", P.sems["sv"].n)
                    sdone = P.sems["ss"].n
                    P.waitall("te", [("ss", sdone), ("sv", hfm_v)])
                    for j in range(nj):
                        P.op("te", lambda e, j=j, kv=kv, nj=nj: e.matmul(
                            pO[:], v1sb[:, j * 256 + kv * 64: j * 256 + kv * 64 + 64],
                            pt12[:, 512 * j:512 * (j + 1)],
                            start=(j == 0), stop=(j == nj - 1)), inc=False)
                    P.wait("te", "sv", rz_v)
                    for j in range(nj):
                        P.op("te", lambda e, j=j, nj=nj: e.matmul(
                            pZ[:], ones128b[:], pt12[:, 512 * j:512 * (j + 1)],
                            start=(j == 0), stop=(j == nj - 1)), inc=(j == nj - 1))
                    tz = P.sems["st"].n
                    P.wait("ve", "st", tz)
                    rz_v = P.op("ve", lambda e: e.reciprocal(rz[:], pZ[:]))
                    P.fence("ve")
                    vv = P.sems["sv"].n
                    P.wait("te", "sv", vv)
                    tv = P.op("te", lambda e: e.matmul(pMB[0:64, :], onesrow[0:1, 0:64], rz[:],
                                                       start=True, stop=True))
                    P.wait("ve", "st", tv)
                    hc, hb = h // 2, (h % 2) * 64
                    P.op("ve", lambda e: e.tensor_copy(rzbc[:], pMB[0:64, :]))
                    P.fence("ve")
                    hfm_v = P.op("ve", lambda e, hc=hc, hb=hb, gq=gq: e.tensor_tensor(
                        Hfm[hb:hb + 64, hc * S + 512 * gq: hc * S + 512 * (gq + 1)],
                        pO[:], rzbc[:], ALU.mult))
                    P.fence("ve")
                    qhw = P.sems["st"].n
    _pre12 = set(P._dsems)
    _ph12_fn()
    P.barrier("ph12")
    P.release_dsems(set(P._dsems) - _pre12)
    dump("hfm", Hfm[:], (128, NC * S), BF16)

    if upto < 13:
        P.build(); return dbg_outs, st

    def _ph13_fn():
        with ExitStack() as ph13:
            sbl = lambda n, s, d=F32: ph13.enter_context(P.nc.sbuf_tensor(n, s, d))
            psl = lambda n, s, d=F32: ph13.enter_context(P.nc.psum_tensor(n, s, d))
            wob = [sbl(f"p13_wo{i}", [128, 1024], BF16) for i in range(2)]
            wos = [P.dsem("p13w0"), P.dsem("p13w1")]
            X13 = [P.dsem("p13x0"), P.dsem("p13x1")]
            Y13 = [P.dsem("p13y0"), P.dsem("p13y1")]
            x1t = [sbl(f"p13_x1{i}", [128, 1024]) for i in range(2)]
            ysb = [sbl(f"p13_y{i}", [128, 1024]) for i in range(2)]
            patt = psl("p13_patt", [128, 1024])
            ysum = sbl("p13_ysum", [128, NT])
            ysq = sbl("p13_ysq", [128, NT])
            yscr = sbl("p13_yscr", [128, 1024])
            wdone = [0, 0]
            xd = [0, 0]
            yd = [0, 0]
            for tile in range(NT):
                i = tile % 2
                P.wait("sy", "sv", xd[i])
                wx = P.dma("sy", x1t[i][:], g["x1_d"][tile * 128:(tile + 1) * 128, :], sem=X13[i])
                P.wait("te", "sv", P.sems["sv"].n)
                for ic in range(8):
                    wi = ic % 2
                    P.wait("gp", "st", wdone[wi])
                    ww = P.dma("gp", wob[wi][:], g["wot1"][128 * ic:128 * (ic + 1), :], sem=wos[wi])
                    P.wait("te", wos[wi], ww)
                    for nn in range(2):
                        P.op("te", lambda e, wi=wi, ic=ic, nn=nn, tile=tile: e.matmul(
                            patt[:, 512 * nn:512 * (nn + 1)],
                            Hfm[:, ic * S + tile * 128: ic * S + tile * 128 + 128],
                            wob[wi][:, 512 * nn:512 * (nn + 1)],
                            start=(ic == 0), stop=(ic == 7)), inc=(nn == 1))
                    wdone[wi] = P.sems["st"].n
                tmm = P.sems["st"].n
                P.waitall("ve", [("st", tmm), (X13[i], wx), (Y13[i], yd[i])])
                P.op("ve", lambda e, i=i, tile=tile: e.tensor_scalar(
                    yscr[:], x1t[i][:], stats1_negm[:, tile:tile + 1], stats1_rstd[:, tile:tile + 1],
                    ALU.add, ALU.mult))
                P.fence("ve")
                P.op("ve", lambda e, i=i: e.tensor_tensor(ysb[i][:], yscr[:], patt[:], ALU.add))
                P.fence("ve")
                P.op("ve", lambda e, i=i, tile=tile: e.reduce_sum(ysum[:, tile:tile + 1], ysb[i][:], AX.X))
                P.fence("ve")
                vv = P.sems["sv"].n
                P.wait("se", "sv", vv)
                sv_ = P.op("se", lambda e, i=i, tile=tile: e.activation(
                    yscr[:], ysb[i][:], AF.Square, accum_out=ysq[:, tile:tile + 1]))
                xd[i] = P.sems["sv"].n
                P.wait("sy", "sv", vv)
                yd[i] = P.dma("sy", g["y_d"][tile * 128:(tile + 1) * 128, :], ysb[i][:], sem=Y13[i])
            P.fence("ve"); P.wait("ve", "ss", P.sems["ss"].n)
            yv = sbl("p13_yv", [128, NT]); yv2 = sbl("p13_yv2", [128, NT])
            P.op("ve", lambda e: e.tensor_scalar(yv[:], ysq[:], 1.0 / 1024, 1e-6, ALU.mult, ALU.add))
            P.fence("ve")
            P.wait("se", "sv", P.sems["sv"].n)
            sv_ = P.op("se", lambda e: e.activation(yv2[:], yv[:], AF.Sqrt))
            P.wait("ve", "ss", sv_)
            P.op("ve", lambda e: e.reciprocal(statsy_rms[:], yv2[:]))
            P.fence("ve")
    _pre13 = set(P._dsems)
    _ph13_fn()
    Hfm_ctx.__exit__(None, None, None)
    v1sb_ctx.__exit__(None, None, None)
    k1sb_ctx.__exit__(None, None, None)
    P.barrier("ph13")
    P.release_dsems(set(P._dsems) - _pre13)
    dump("ydump", g["y_d"], (S, E))

    if upto < 14:
        P.build(); return dbg_outs, st

    cbc = P.sb("cbc14", [128, 4], F32)
    hfm_ctx = P.nc.sbuf_tensor("hfm2", [128, NC * S], BF16)
    h_fm = hfm_ctx.__enter__()
    def _ph14_fn():
        with ExitStack() as ph14:
            sbl = lambda n, s, d=F32: ph14.enter_context(P.nc.sbuf_tensor(n, s, d))
            psl = lambda n, s, d=F32: ph14.enter_context(P.nc.psum_tensor(n, s, d))
            selc = {}
            m1t = sbl("p14_m1", [1, 1]); m2t = sbl("p14_m2", [1, 1])
            msk = sbl("p14_msk", [1, 8]); mi = sbl("p14_mi", [1, 8])
            l2t = sbl("p14_l2", [1, 8])
            i1t = sbl("p14_i1", [1, 1]); i2t = sbl("p14_i2", [1, 1])
            dt_ = sbl("p14_dt", [1, 1]); c1t = sbl("p14_c1", [1, 1]); c2t = sbl("p14_c2", [1, 1])
            sel_sb = sbl("p14_sel", [1, 8])
            for bi, logit in ((0, fclog_sb), (1, pjlog_sb)):
                P.op("ve", lambda e, logit=logit: e.reduce_max(m1t[:], logit[:], AX.X))
                P.fence("ve")
                P.op("ve", lambda e, logit=logit: e.tensor_single_scalar(msk[:], logit[:], m1t[0:1, 0:1], ALU.is_ge))
                P.fence("ve")
                P.op("ve", lambda e: e.tensor_scalar(mi[:], iota8row[:], -100.0, None, ALU.add))
                P.fence("ve")
                P.op("ve", lambda e: e.tensor_tensor(mi[:], mi[:], msk[:], ALU.mult))
                P.fence("ve")
                P.op("ve", lambda e: e.tensor_scalar(mi[:], mi[:], 100.0, None, ALU.add))
                P.fence("ve")
                P.op("ve", lambda e: e.tensor_reduce(i1t[:], mi[:], AX.X, ALU.min))
                P.fence("ve")
                P.op("ve", lambda e: e.tensor_single_scalar(msk[:], iota8row[:], i1t[0:1, 0:1], ALU.is_equal))
                P.fence("ve")
                P.op("ve", lambda e: e.tensor_scalar(msk[:], msk[:], -1e30, None, ALU.mult))
                P.fence("ve")
                P.op("ve", lambda e, logit=logit: e.tensor_tensor(l2t[:], logit[:], msk[:], ALU.add))
                P.fence("ve")
                P.op("ve", lambda e: e.reduce_max(m2t[:], l2t[:], AX.X))
                P.fence("ve")
                P.op("ve", lambda e: e.tensor_single_scalar(msk[:], l2t[:], m2t[0:1, 0:1], ALU.is_ge))
                P.fence("ve")
                P.op("ve", lambda e: e.tensor_scalar(mi[:], iota8row[:], -100.0, None, ALU.add))
                P.fence("ve")
                P.op("ve", lambda e: e.tensor_tensor(mi[:], mi[:], msk[:], ALU.mult))
                P.fence("ve")
                P.op("ve", lambda e: e.tensor_scalar(mi[:], mi[:], 100.0, None, ALU.add))
                P.fence("ve")
                P.op("ve", lambda e: e.tensor_reduce(i2t[:], mi[:], AX.X, ALU.min))
                P.fence("ve")
                P.op("ve", lambda e: e.tensor_single_scalar(dt_[:], m2t[:], m1t[0:1, 0:1], ALU.subtract))
                P.fence("ve")
                P.wait("se", "sv", P.sems["sv"].n)
                sv_ = P.op("se", lambda e: e.activation(dt_[:], dt_[:], AF.Exp))
                P.wait("ve", "ss", sv_)
                P.op("ve", lambda e: e.tensor_scalar(c1t[:], dt_[:], 1.0, None, ALU.add))
                P.fence("ve")
                P.op("ve", lambda e: e.reciprocal(c1t[:], c1t[:]) if False else e.reciprocal(c2t[:], c1t[:]))
                P.fence("ve")
                P.op("ve", lambda e: e.tensor_copy(c1t[:], c2t[:]))
                P.fence("ve")
                P.op("ve", lambda e: e.tensor_tensor(c2t[:], dt_[:], c1t[:], ALU.mult))
                P.fence("ve")
                for col, t in ((0, i1t), (1, i2t), (2, c1t), (3, c2t)):
                    P.op("ve", lambda e, bi=bi, col=col, t=t: e.tensor_copy(
                        sel_sb[0:1, 4 * bi + col:4 * bi + col + 1], t[:]))
                P.fence("ve")
            dump("selsb", sel_sb[:], (1, 8))
            pbc1 = psl("p14_pbc", [128, 1])
            imbc = sbl("p14_imbc", [128, 1])
            idxf = sbl("p14_idxf", [128, 1])
            idxi = sbl("p14_idxi", [128, 1], I32)
            gtmp = sbl("p14_gt", [128, 4096])
            GW = P.dsem("p14g")
            GTO = P.dsem("p14go")
            gw_v = 0
            gto_v = 0

            def gather_rows(src_flat, nrows, bank_rows, dst_d, bi, mem):
                nonlocal gw_v, gto_v
                width = int(src_flat.shape[-1])
                vv = P.sems["sv"].n
                P.wait("te", "sv", vv)
                tv = P.op("te", lambda e, bi=bi, mem=mem: e.matmul(
                    pbc1[:], onesrow[:], sel_sb[0:1, 4 * bi + mem:4 * bi + mem + 1],
                    start=True, stop=True))
                P.wait("ve", "st", tv)
                P.op("ve", lambda e: e.tensor_copy(imbc[:], pbc1[:]))
                P.fence("ve")
                for t in range(nrows // 128):
                    P.op("ve", lambda e, t=t, bank_rows=bank_rows: e.tensor_scalar(
                        idxf[:], imbc[:], float(bank_rows), float(128 * t), ALU.mult, ALU.add))
                    P.fence("ve")
                    P.op("ve", lambda e: e.tensor_tensor(idxf[:], idxf[:], iotaf[:], ALU.add))
                    P.fence("ve")
                    P.op("ve", lambda e: e.tensor_copy(idxi[:], idxf[:]))
                    P.fence("ve")
                    vv = P.sems["sv"].n
                    P.waitall("gp", [("sv", vv), (GW, gw_v), (GTO, gto_v)])
                    gw_v = P.idma(gtmp[:, 0:width], None,
                                  src_flat, IndirectOffsetOnAxis(idxi[:], 0), sem=GW)
                    P.wait("sy", GW, gw_v)
                    gto_v = P.dma("sy", dst_d[128 * t:128 * (t + 1), :], gtmp[:, 0:width], sem=GTO)
                    P.wait("ve", GW, gw_v)
            gather_rows(g["fcs1"].rearrange("a b c -> (a b) c"), 1024, 1024, g["s1sel"].rearrange("m a c -> (m a) c")[0:1024, :], 0, 0)
            gather_rows(g["fcs1"].rearrange("a b c -> (a b) c"), 1024, 1024, g["s1sel"].rearrange("m a c -> (m a) c")[1024:2048, :], 0, 1)
            gather_rows(g["fcs2"].rearrange("a b c -> (a b) c"), 4096, 4096, g["s2sel"].rearrange("m a c -> (m a) c")[0:4096, :], 0, 0)
            gather_rows(g["fcs2"].rearrange("a b c -> (a b) c"), 4096, 4096, g["s2sel"].rearrange("m a c -> (m a) c")[4096:8192, :], 0, 1)
            gather_rows(g["fcmixT"].rearrange("a b c -> (a b) c"), 1024, 1024, g["mixsel"].rearrange("m a c -> (m a) c")[0:1024, :], 0, 0)
            gather_rows(g["fcmixT"].rearrange("a b c -> (a b) c"), 1024, 1024, g["mixsel"].rearrange("m a c -> (m a) c")[1024:2048, :], 0, 1)
            gather_rows(g["pjs1"].rearrange("a b c -> (a b) c"), 4096, 4096, g["pj1sel"].rearrange("m a c -> (m a) c")[0:4096, :], 1, 0)
            gather_rows(g["pjs1"].rearrange("a b c -> (a b) c"), 4096, 4096, g["pj1sel"].rearrange("m a c -> (m a) c")[4096:8192, :], 1, 1)
            gather_rows(g["pjs2"].rearrange("a b c -> (a b) c"), 1024, 1024, g["pj2sel"].rearrange("m a c -> (m a) c")[0:1024, :], 1, 0)
            gather_rows(g["pjs2"].rearrange("a b c -> (a b) c"), 1024, 1024, g["pj2sel"].rearrange("m a c -> (m a) c")[1024:2048, :], 1, 1)
            gather_rows(g["pjmixT"].rearrange("a b c -> (a b) c"), 4096, 4096, g["pjmixsel"].rearrange("m a c -> (m a) c")[0:4096, :], 1, 0)
            gather_rows(g["pjmixT"].rearrange("a b c -> (a b) c"), 4096, 4096, g["pjmixsel"].rearrange("m a c -> (m a) c")[4096:8192, :], 1, 1)
            for col in range(4):
                vv = P.sems["sv"].n
                P.wait("te", "sv", vv)
                tv = P.op("te", lambda e, col=col: e.matmul(
                    pbc1[:], onesrow[:], sel_sb[0:1, 2 + 4 * (col // 2) + col % 2: 3 + 4 * (col // 2) + col % 2],
                    start=True, stop=True))
                P.wait("ve", "st", tv)
                P.op("ve", lambda e, col=col: e.tensor_copy(cbc[:, col:col + 1], pbc1[:]))
                P.fence("ve")
            yt14 = [sbl(f"p14_y{i}", [128, 1024]) for i in range(2)]
            xt14 = [sbl(f"p14_x{i}", [128, 1024]) for i in range(2)]
            at14 = [sbl(f"p14_a{i}", [128, 1024]) for i in range(2)]
            bt14 = [sbl(f"p14_b{i}", [128, 1024]) for i in range(2)]
            ct14 = [sbl(f"p14_ct{i}", [128, 1024]) for i in range(2)]
            ys14 = [P.dsem("p14y0"), P.dsem("p14y1")]
            xs14 = [P.dsem("p14x0"), P.dsem("p14x1")]
            as14 = [P.dsem("p14a0"), P.dsem("p14a1")]
            OS14 = [P.dsem("p14o0"), P.dsem("p14o1")]
            hnb = sbl("p14_hnb", [128, 1024], BF16)
            pt14 = psl("p14_pt", [128, 128], BF16)
            ydn = [0, 0]
            xdn = [0, 0]
            adn = [0, 0]
            odn = [0, 0]
            for tile in range(NT):
                i = tile % 2
                P.wait("sy", "sv", ydn[i])
                wy = P.dma("sy", yt14[i][:], g["y_d"][tile * 128:(tile + 1) * 128, :], sem=ys14[i])
                P.wait("sy", "sv", xdn[i])
                wx = P.dma("sy", xt14[i][:], g["x1_d"][tile * 128:(tile + 1) * 128, :], sem=xs14[i])
                P.wait("sy", "sv", adn[i])
                wa = P.dma("sy", at14[i][:], g["a1_d"][tile * 128:(tile + 1) * 128, :], sem=as14[i])
                P.waitall("ve", [(ys14[i], wy), ("st", P.sems["st"].n)])
                P.op("ve", lambda e, i=i, tile=tile: e.tensor_scalar(
                    hnb[:], yt14[i][:], statsy_rms[:, tile:tile + 1], None, ALU.mult))
                P.fence("ve")
                vv = P.sems["sv"].n
                for k in range(NC):
                    P.wait("te", "sv", vv)
                    tv = P.op("te", lambda e, k=k: e.transpose(pt14[:], hnb[:, 128 * k:128 * (k + 1)], identb[:]))
                    P.wait("ve", "st", tv)
                    vv = P.op("ve", lambda e, k=k, tile=tile: e.tensor_copy(
                        h_fm[:, k * S + tile * 128: k * S + tile * 128 + 128], pt14[:]))
                P.fence("ve")
                P.waitall("ve", [(xs14[i], wx), (as14[i], wa), (OS14[i], odn[i])])
                P.op("ve", lambda e, i=i: e.tensor_tensor(bt14[i][:], yt14[i][:], xt14[i][:], ALU.add))
                P.fence("ve")
                P.op("ve", lambda e, i=i: e.tensor_tensor(ct14[i][:], bt14[i][:], at14[i][:], ALU.add))
                P.fence("ve")
                vv2 = P.sems["sv"].n
                ydn[i] = vv2
                xdn[i] = vv2
                adn[i] = vv2
                P.wait("sy", "sv", vv2)
                odn[i] = P.dma("sy", g["out"][tile * 128:(tile + 1) * 128, :], ct14[i][:], sem=OS14[i])
    _pre14 = set(P._dsems)
    _ph14_fn()
    P.barrier("ph14")
    P.release_dsems(set(P._dsems) - _pre14)

    if upto < 15:
        P.build(); return dbg_outs, st

    def _ph15_fn():
        with ExitStack() as ph15:
            sbl = lambda n, s, d=F32: ph15.enter_context(P.nc.sbuf_tensor(n, s, d))
            psl = lambda n, s, d=F32: ph15.enter_context(P.nc.psum_tensor(n, s, d))
            s1b = sbl("p15_s1", [128, 2 * 8 * 64], BF16)
            s2b = sbl("p15_s2", [128, 2 * 32 * 64], BF16)
            WL15 = P.dsem("p15w")
            w1_ = P.dma("gp", s1b[:].rearrange("p (m k q) -> p (m k) q", m=2, k=8),
                        g["s1sel"].rearrange("m (k p) c -> p (m k) c", p=128), sem=WL15)
            w2_ = P.dma("gp", s2b[:].rearrange("p (m k q) -> p (m k) q", m=2, k=32),
                        g["s2sel"].rearrange("m (k p) c -> p (m k) c", p=128), sem=WL15)
            P.wait("te", WL15, max(w1_, w2_))
            mxslab = sbl("p15_mx", [128, 8 * 4096], BF16)
            MXS = P.dsem("p15mx")
            hb15 = sbl("p15_hb", [128, 8 * 512], BF16)
            mfm = sbl("p15_mfm", [128, 512], BF16)
            fch = [sbl(f"p15_f{i}", [128, 512]) for i in range(2)]
            fch2 = [sbl(f"p15_g{i}", [128, 512]) for i in range(2)]
            ftl = [sbl(f"p15_ft{i}", [128, 512]) for i in range(2)]
            h2b = [sbl(f"p15_h2{i}", [128, 512], BF16) for i in range(2)]
            FW = [P.dsem("p15fw0"), P.dsem("p15fw1")]
            FL = [P.dsem("p15fl0"), P.dsem("p15fl1")]
            HW = [P.dsem("p15hw0"), P.dsem("p15hw1")]
            pb1a = psl("p15_p1a", [64, 512])
            pb1b = psl("p15_p1b", [64, 512])
            pm15 = [psl(f"p15_pm{i}", [128, 512]) for i in range(2)]
            pb2a = [psl(f"p15_p2a{i}", [64, 512]) for i in range(2)]
            pb2b = [psl(f"p15_p2b{i}", [64, 512]) for i in range(2)]
            mfm2 = [mfm, sbl("p15_mfm1", [128, 512], BF16)]
            fwv = [0, 0]
            flv = [0, 0]
            hwv = [0, 0]
            mfv = [0, 0]
            b2v = [0, 0]
            for m in range(2):
                P.barrier(f"p15m{m}")
                wmx = 0
                for k in range(8):
                    wmx = P.dma("gp", mxslab[:, k * 4096:(k + 1) * 4096],
                                g["mixsel"][m, 128 * k:128 * (k + 1), :], sem=MXS)
                P.wait("te", MXS, wmx)
                for toff in range(8):
                    t0 = toff * 512
                    for k in range(8):
                        P.op("te", lambda e, m=m, k=k, t0=t0: e.matmul(
                            pb1a[:], s1b[0:64, (m * 8 + k) * 64:(m * 8 + k) * 64 + 64],
                            h_fm[0:64, k * S + t0: k * S + t0 + 512],
                            start=True, stop=True), inc=False)
                        tv = P.op("te", lambda e, m=m, k=k, t0=t0: e.matmul(
                            pb1b[:], s1b[64:128, (m * 8 + k) * 64:(m * 8 + k) * 64 + 64],
                            h_fm[64:128, k * S + t0: k * S + t0 + 512],
                            start=True, stop=True))
                        P.wait("ve", "st", tv)
                        P.op("ve", lambda e, k=k: e.tensor_copy(
                            hb15[0:64, k * 512:(k + 1) * 512], pb1a[:]), inc=False)
                        vcp = P.op("ve", lambda e, k=k: e.tensor_copy(
                            hb15[64:128, k * 512:(k + 1) * 512], pb1b[:]))
                        P.wait("te", "sv", vcp)
                    for oc in range(32):
                        i = oc % 2
                        flw = 0
                        if m == 1:
                            P.wait("sy", "sv", flv[i])
                            flw = P.dma("sy", ftl[i][:], g["fc_d"][oc, :, t0:t0 + 512], sem=FL[i])
                        P.wait("te", "sv", mfv[i])
                        for k in range(8):
                            P.op("te", lambda e, i=i, k=k, oc=oc: e.matmul(
                                pm15[i][:], mxslab[:, k * 4096 + oc * 128: k * 4096 + oc * 128 + 128],
                                hb15[:, k * 512:(k + 1) * 512],
                                start=(k == 0), stop=(k == 7)), inc=(k == 7))
                        tmm = P.sems["st"].n
                        P.wait("ve", "st", tmm)
                        mfv[i] = P.op("ve", lambda e, i=i: e.tensor_copy(mfm2[i][:], pm15[i][:]))
                        P.fence("ve")
                        P.waitall("te", [("sv", mfv[i]), ("sv", b2v[i])])
                        P.op("te", lambda e, i=i, m=m, oc=oc: e.matmul(
                            pb2a[i][:], s2b[0:64, (m * 32 + oc) * 64:(m * 32 + oc) * 64 + 64],
                            mfm2[i][0:64, :], start=True, stop=True), inc=False)
                        tv2 = P.op("te", lambda e, i=i, m=m, oc=oc: e.matmul(
                            pb2b[i][:], s2b[64:128, (m * 32 + oc) * 64:(m * 32 + oc) * 64 + 64],
                            mfm2[i][64:128, :], start=True, stop=True))
                        P.wait("ve", "st", tv2)
                        if m == 0:
                            P.wait("ve", FW[i], fwv[i])
                            P.op("ve", lambda e, i=i: e.tensor_scalar(
                                fch[i][0:64, :], pb2a[i][:], cbc[0:64, 0:1], None, ALU.mult), inc=False)
                            vsc = P.op("ve", lambda e, i=i: e.tensor_scalar(
                                fch[i][64:128, :], pb2b[i][:], cbc[0:64, 0:1], None, ALU.mult))
                            b2v[i] = vsc
                            P.wait("sy", "sv", vsc)
                            fwv[i] = P.dma("sy", g["fc_d"][oc, :, t0:t0 + 512], fch[i][:], sem=FW[i])
                        else:
                            P.waitall("ve", [(FL[i], flw), (HW[i], hwv[i])])
                            P.op("ve", lambda e, i=i: e.tensor_scalar(
                                fch[i][0:64, :], pb2a[i][:], cbc[0:64, 1:2], None, ALU.mult), inc=False)
                            b2v[i] = P.op("ve", lambda e, i=i: e.tensor_scalar(
                                fch[i][64:128, :], pb2b[i][:], cbc[0:64, 1:2], None, ALU.mult))
                            P.fence("ve")
                            P.op("ve", lambda e, i=i: e.tensor_tensor(
                                fch2[i][:], fch[i][:], ftl[i][:], ALU.add))
                            P.fence("ve")
                            flv[i] = P.sems["sv"].n
                            P.op("ve", lambda e, i=i: e.tensor_scalar(
                                fch[i][:], fch2[i][:], 0.0, None, ALU.max))
                            P.fence("ve")
                            vsq = P.op("ve", lambda e, i=i: e.tensor_tensor(
                                h2b[i][:], fch[i][:], fch[i][:], ALU.mult))
                            P.wait("sy", "sv", vsq)
                            hwv[i] = P.dma("sy", g["h2_d"][oc * 128:(oc + 1) * 128, t0:t0 + 512],
                                           h2b[i][:], sem=HW[i])
    _pre15 = set(P._dsems)
    _ph15_fn()
    hfm_ctx.__exit__(None, None, None)
    P.barrier("ph15")
    P.release_dsems(set(P._dsems) - _pre15)
    dump("h2dump", g["h2_d"], (4 * E, S), BF16)

    if upto < 16:
        P.build(); return dbg_outs, st

    def _ph16_fn():
        with ExitStack() as ph16:
            sbl = lambda n, s, d=F32: ph16.enter_context(P.nc.sbuf_tensor(n, s, d))
            psl = lambda n, s, d=F32: ph16.enter_context(P.nc.psum_tensor(n, s, d))
            s1b6 = sbl("p16_s1", [128, 2 * 32 * 64], BF16)
            s2b6 = sbl("p16_s2", [128, 2 * 8 * 64], BF16)
            WL16 = P.dsem("p16w")
            w1_ = P.dma("gp", s1b6[:].rearrange("p (m k q) -> p (m k) q", m=2, k=32),
                        g["pj1sel"].rearrange("m (k p) c -> p (m k) c", p=128), sem=WL16)
            w2_ = P.dma("gp", s2b6[:].rearrange("p (m k q) -> p (m k) q", m=2, k=8),
                        g["pj2sel"].rearrange("m (k p) c -> p (m k) c", p=128), sem=WL16)
            P.wait("te", WL16, max(w1_, w2_))
            mxs6 = sbl("p16_mx", [128, 32 * 1024], BF16)
            MX6 = P.dsem("p16mx")
            htl = sbl("p16_ht", [128, 32 * 512], BF16)
            HT6 = P.dsem("p16ht")
            hb16 = sbl("p16_hb", [128, 32 * 512], BF16)
            mfm6 = sbl("p16_mfm", [128, 512], BF16)
            pjh = [sbl(f"p16_f{i}", [128, 512]) for i in range(2)]
            pjf = [sbl(f"p16_g{i}", [128, 512]) for i in range(2)]
            ptl = [sbl(f"p16_pt{i}", [128, 512]) for i in range(2)]
            PW6 = [P.dsem("p16pw0"), P.dsem("p16pw1")]
            PL6 = [P.dsem("p16pl0"), P.dsem("p16pl1")]
            obuf = [sbl(f"p16_o{i}", [128, 128]) for i in range(2)]
            osum = [sbl(f"p16_os{i}", [128, 128]) for i in range(2)]
            OL6 = [P.dsem("p16ol0"), P.dsem("p16ol1")]
            OW6 = [P.dsem("p16ow0"), P.dsem("p16ow1")]
            pb1a6 = psl("p16_p1a", [64, 512])
            pb1b6 = psl("p16_p1b", [64, 512])
            _pm16 = psl("p16_pm0", [128, 512])
            pm16 = [_pm16, _pm16]
            pb2a6 = [psl(f"p16_p2a{i}", [64, 512]) for i in range(2)]
            pb2b6 = [psl(f"p16_p2b{i}", [64, 512]) for i in range(2)]
            mfm62 = [mfm6, sbl("p16_mfm1", [128, 512], BF16)]
            ptr16 = psl("p16_ptr", [128, 128])
            pwv = [0, 0]
            plv = [0, 0]
            olv = [0, 0]
            owv = [0, 0]
            mfv6 = [0, 0]
            b2v6 = [0, 0]
            for m in range(2):
                P.barrier(f"p16m{m}")
                wmx = 0
                for k in range(32):
                    wmx = P.dma("gp", mxs6[:, k * 1024:(k + 1) * 1024],
                                g["pjmixsel"][m, 128 * k:128 * (k + 1), :], sem=MX6)
                P.wait("te", MX6, wmx)
                for toff in range(8):
                    t0 = toff * 512
                    P.wait("sy", "st", P.sems["st"].n)
                    wht = 0
                    for k in range(32):
                        wht = P.dma("sy", htl[:, k * 512:(k + 1) * 512],
                                    g["h2_d"][128 * k:128 * (k + 1), t0:t0 + 512], sem=HT6)
                    P.wait("te", HT6, wht)
                    for k in range(32):
                        P.op("te", lambda e, m=m, k=k: e.matmul(
                            pb1a6[:], s1b6[0:64, (m * 32 + k) * 64:(m * 32 + k) * 64 + 64],
                            htl[0:64, k * 512:(k + 1) * 512],
                            start=True, stop=True), inc=False)
                        tv = P.op("te", lambda e, m=m, k=k: e.matmul(
                            pb1b6[:], s1b6[64:128, (m * 32 + k) * 64:(m * 32 + k) * 64 + 64],
                            htl[64:128, k * 512:(k + 1) * 512],
                            start=True, stop=True))
                        P.wait("ve", "st", tv)
                        P.op("ve", lambda e, k=k: e.tensor_copy(
                            hb16[0:64, k * 512:(k + 1) * 512], pb1a6[:]), inc=False)
                        vcp = P.op("ve", lambda e, k=k: e.tensor_copy(
                            hb16[64:128, k * 512:(k + 1) * 512], pb1b6[:]))
                        P.wait("te", "sv", vcp)
                    for oc in range(8):
                        i = oc % 2
                        plw = 0
                        if m == 1:
                            P.wait("sy", "sv", plv[i])
                            plw = P.dma("sy", ptl[i][:], g["pj0_d"][oc, :, t0:t0 + 512], sem=PL6[i])
                        P.wait("te", "sv", max(mfv6))
                        for k in range(32):
                            P.op("te", lambda e, i=i, k=k, oc=oc: e.matmul(
                                pm16[i][:], mxs6[:, k * 1024 + oc * 128: k * 1024 + oc * 128 + 128],
                                hb16[:, k * 512:(k + 1) * 512],
                                start=(k == 0), stop=(k == 31)), inc=(k == 31))
                        tmm = P.sems["st"].n
                        P.wait("ve", "st", tmm)
                        mfv6[i] = P.op("ve", lambda e, i=i: e.tensor_copy(mfm62[i][:], pm16[i][:]))
                        P.fence("ve")
                        P.waitall("te", [("sv", mfv6[i]), ("sv", b2v6[i])])
                        P.op("te", lambda e, i=i, m=m, oc=oc: e.matmul(
                            pb2a6[i][:], s2b6[0:64, (m * 8 + oc) * 64:(m * 8 + oc) * 64 + 64],
                            mfm62[i][0:64, :], start=True, stop=True), inc=False)
                        tv2 = P.op("te", lambda e, i=i, m=m, oc=oc: e.matmul(
                            pb2b6[i][:], s2b6[64:128, (m * 8 + oc) * 64:(m * 8 + oc) * 64 + 64],
                            mfm62[i][64:128, :], start=True, stop=True))
                        P.wait("ve", "st", tv2)
                        if m == 0:
                            P.wait("ve", PW6[i], pwv[i])
                            P.op("ve", lambda e, i=i: e.tensor_scalar(
                                pjh[i][0:64, :], pb2a6[i][:], cbc[0:64, 2:3], None, ALU.mult), inc=False)
                            vsc = P.op("ve", lambda e, i=i: e.tensor_scalar(
                                pjh[i][64:128, :], pb2b6[i][:], cbc[0:64, 2:3], None, ALU.mult))
                            b2v6[i] = vsc
                            P.wait("sy", "sv", vsc)
                            pwv[i] = P.dma("sy", g["pj0_d"][oc, :, t0:t0 + 512], pjh[i][:], sem=PW6[i])
                        else:
                            P.wait("ve", PL6[i], plw)
                            P.op("ve", lambda e, i=i: e.tensor_scalar(
                                pjh[i][0:64, :], pb2a6[i][:], cbc[0:64, 3:4], None, ALU.mult), inc=False)
                            b2v6[i] = P.op("ve", lambda e, i=i: e.tensor_scalar(
                                pjh[i][64:128, :], pb2b6[i][:], cbc[0:64, 3:4], None, ALU.mult))
                            P.fence("ve")
                            P.op("ve", lambda e, i=i: e.tensor_tensor(
                                pjf[i][:], pjh[i][:], ptl[i][:], ALU.add))
                            P.fence("ve")
                            plv[i] = P.sems["sv"].n
                            for q4 in range(4):
                                j = q4 % 2
                                vvw = P.sems["sv"].n
                                P.wait("te", "sv", vvw)
                                tv3 = P.op("te", lambda e, i=i, q4=q4: e.transpose(
                                    ptr16[:], pjf[i][:, q4 * 128:(q4 + 1) * 128], ident[:]))
                                P.wait("sy", "sv", olv[j])
                                olw = P.dma("sy", obuf[j][:],
                                            g["out"][t0 + q4 * 128: t0 + (q4 + 1) * 128,
                                                     oc * 128:(oc + 1) * 128], sem=OL6[j])
                                P.waitall("ve", [("st", tv3), (OL6[j], olw), (OW6[j], owv[j])])
                                vos = P.op("ve", lambda e, j=j: e.tensor_tensor(
                                    osum[j][:], obuf[j][:], ptr16[:], ALU.add))
                                olv[j] = vos
                                P.wait("sy", "sv", vos)
                                owv[j] = P.dma("sy", g["out"][t0 + q4 * 128: t0 + (q4 + 1) * 128,
                                                              oc * 128:(oc + 1) * 128],
                                               osum[j][:], sem=OW6[j])
                                P.wait("te", "sv", vos)
    _ph16_fn()
    P.barrier("ph16")
    P.build()
    return dbg_outs, st




def kernel(**inputs):
    in_maps = [host_prep(inputs, b) for b in range(B)]
    nc = bass.Bass("TRN2", target_bir_lowering=False)
    build_core(nc, 0)
    res = run_bass_kernel_spmd(nc, in_maps, core_ids=list(range(B)))
    out = np.stack([res.results[i]["out"] for i in range(B)], axis=0)
    return out.astype(np.float32)
